# version 14
# baseline (speedup 1.0000x reference)
import math
import numpy as np
import ml_dtypes
import concourse.bass as bass
import concourse.mybir as mybir
from concourse.bass_utils import run_bass_kernel_spmd
from contextlib import ExitStack

F32 = mybir.dt.float32
BF16 = mybir.dt.bfloat16
I32 = mybir.dt.int32
I8 = mybir.dt.int8
AF = mybir.ActivationFunctionType
ALU = mybir.AluOpType
AX = mybir.AxisListType

B, S, D = 4, 4096, 2048
H, E = 16, 128
HID = 8192
NCORES = 8
OWN = 2048
NT = S // 128
NOT = OWN // 128
NCH = D // 128
NHCH = HID // 128
NQB = 4
SCALE = 1.0 / math.sqrt(E)
EXP_BIAS = -12.0

PHASES = 6
DBG_SCRATCH = False


class Emitter:
    ENGS = ("t", "v", "s", "g", "y")

    def __init__(self):
        self.ops = {k: [] for k in self.ENGS}
        self.cnt = {}
        self.sems = {}

    def new(self, name):
        self.cnt[name] = 0

    def wait(self, eng, sem, val):
        if val and val > 0:
            self.ops[eng].append(lambda e, s=sem, v=val: e.wait_ge(self.sems[s], v))

    def op(self, eng, fn, inc=None, by=1):
        tgt = None
        if inc is not None:
            self.cnt[inc] += by
            tgt = self.cnt[inc]
            self.ops[eng].append(lambda e, f=fn, s=inc, b=by: f(e).then_inc(self.sems[s], b))
        else:
            self.ops[eng].append(lambda e, f=fn: f(e))
        return tgt

    def dma(self, eng, out, in_, sem):
        if sem not in self.cnt:
            self.cnt[sem] = 0
        self.cnt[sem] += 16
        tgt = self.cnt[sem]
        self.ops[eng].append(lambda e, o=out, i=in_, s=sem:
                             e.dma_start(out=o, in_=i).then_inc(self.sems[s], 16))
        return tgt

    def seq(self, eng, fn):
        t = self.op(eng, fn, inc=eng)
        self.wait(eng, eng, t)
        return t

    def barrier(self):
        snap = dict(self.cnt)
        for eng in self.ENGS:
            for s, v in snap.items():
                self.wait(eng, s, v)


def build_program():
    nc = bass.Bass("TRN2", target_bir_lowering=False)

    x_tm = nc.dram_tensor("x_tm", [S, D], F32, kind="ExternalInput").ap()
    xT_own = nc.dram_tensor("xT_own", [D, OWN], F32, kind="ExternalInput").ap()
    wqT = nc.dram_tensor("wqT", [H, D, E], F32, kind="ExternalInput").ap()
    wkT = nc.dram_tensor("wkT", [H, D, E], F32, kind="ExternalInput").ap()
    wvT = nc.dram_tensor("wvT", [H, D, E], F32, kind="ExternalInput").ap()
    woT = nc.dram_tensor("woT", [H, E, D], F32, kind="ExternalInput").ap()
    gwT = nc.dram_tensor("gwT", [D, HID], F32, kind="ExternalInput").ap()
    vwT = nc.dram_tensor("vwT", [D, HID], F32, kind="ExternalInput").ap()
    owT = nc.dram_tensor("owT", [HID, D], F32, kind="ExternalInput").ap()
    gb_in = nc.dram_tensor("gb_in", [1, HID], F32, kind="ExternalInput").ap()
    vb_in = nc.dram_tensor("vb_in", [1, HID], F32, kind="ExternalInput").ap()
    obc_in = nc.dram_tensor("obc_in", [128, NCH], F32, kind="ExternalInput").ap()
    iden_bf_in = nc.dram_tensor("iden_bf_in", [128, 128], BF16, kind="ExternalInput").ap()
    iden_f_in = nc.dram_tensor("iden_f_in", [128, 128], F32, kind="ExternalInput").ap()
    masks_in = nc.dram_tensor("masks_in", [8, 128, 512], BF16, kind="ExternalInput").ap()

    out_fm = nc.dram_tensor("out_fm", [D, OWN], F32, kind="ExternalOutput").ap()

    sk = "ExternalOutput" if DBG_SCRATCH else "Internal"
    xn_sc = nc.dram_tensor("xn_sc", [D, S], BF16, kind=sk).ap()
    xq_sc = nc.dram_tensor("xq_sc", [D, OWN], BF16, kind=sk).ap()
    z_sc = nc.dram_tensor("z_sc", [H * E, OWN], BF16, kind=sk).ap()
    ao_sc = nc.dram_tensor("ao_sc", [D, OWN], F32, kind=sk).ap()
    gw_t8 = nc.dram_tensor("gw_t8", [D, HID], I8, kind=sk).ap()
    vw_t8 = nc.dram_tensor("vw_t8", [D, HID], I8, kind=sk).ap()
    ow_t8 = nc.dram_tensor("ow_t8", [HID, D], I8, kind=sk).ap()
    gv_sc = nc.dram_tensor("gv_sc", [OWN, HID], F32, kind=sk).ap()
    hq_sc = nc.dram_tensor("hq_sc", [4, NHCH, 128, 512], I8, kind=sk).ap()
    row_sc = nc.dram_tensor("row_sc", [1, OWN], F32, kind="Internal").ap()
    xn_tm_dbg = nc.dram_tensor("xn_tm_dbg", [S, D], BF16, kind=sk).ap() if DBG_SCRATCH else None
    st_dbg = nc.dram_tensor("st_dbg", [NT, 128, 8], F32, kind=sk).ap() if DBG_SCRATCH else None

    em = Emitter()
    for s in ("t", "v", "s", "g"):
        em.new(s)

    es = ExitStack()
    e = es.enter_context

    def sb(name, cols, dtype, parts=128):
        return e(nc.sbuf_tensor(name, [parts, cols], dtype))

    idn_b = sb("idn_b", 128, BF16)
    idn_f = sb("idn_f", 128, F32)
    masks = sb("masks", 8 * 512, BF16)
    ones_b = sb("ones_b", 1, BF16)
    onesrow_f = sb("onesrow_f", 128, F32, parts=1)
    csc = sb("csc", 16, F32)
    ones_f = csc[:, 0:1]
    c_zero = csc[:, 1:2]
    c_one = csc[:, 2:3]
    c_mone = csc[:, 3:4]
    c_eps = csc[:, 4:5]
    c_geps = csc[:, 5:6]
    c_invD = csc[:, 6:7]
    c_inv127 = csc[:, 7:8]
    c_expb = csc[:, 8:9]
    rg_w = {"gw": csc[:, 9:10], "vw": csc[:, 10:11], "ow": csc[:, 11:12]}
    g127_w = {"gw": csc[:, 12:13], "vw": csc[:, 13:14], "ow": csc[:, 14:15]}
    obc_sb = sb("obc_sb", NCH, F32)
    gsc = sb("gsc", 6 * NOT + 2, F32)
    gx_all = gsc[:, 0 * NOT:1 * NOT]
    gxr127 = gsc[:, 1 * NOT:2 * NOT]
    gh_run = gsc[:, 2 * NOT:3 * NOT]
    sc_g = gsc[:, 3 * NOT:4 * NOT]
    sc_v = gsc[:, 4 * NOT:5 * NOT]
    sc_h127 = gsc[:, 5 * NOT:6 * NOT]
    f0tmp_a = gsc[:, 6 * NOT:6 * NOT + 1]
    f0tmp_b = gsc[:, 6 * NOT + 1:6 * NOT + 2]
    ghrow_t = sb("ghrow_t", 128, F32, parts=NOT)
    shrow = sb("shrow", OWN, F32, parts=1)
    wsum = sb("wsum", 384, F32)
    gtmp = sb("gtmp", 8, F32, parts=1)
    rca = sb("rca", 16, F32)
    rcb = sb("rcb", 16, F32)
    rc16 = sb("rc16", NOT, F32)

    ARENA = 192320
    AR = e(nc.sbuf_tensor("AR", [128, ARENA], mybir.dt.uint8))
    AR_ADDR = nc.lookup_mloc(AR).addr
    _cvn = [0]

    class Carver:

        def __init__(self, base, limit):
            self.off, self.limit = base, limit

        def take(self, cols, dtype, parts=128):
            nb = cols * mybir.dt.size(dtype)
            off = (self.off + 63) // 64 * 64
            self.off = off + nb
            assert self.off <= self.limit, (off, nb, self.limit)
            _cvn[0] += 1
            h = nc.alloc_sbuf_tensor_at(f"cv{_cvn[0]}", [parts, cols], dtype,
                                        offset=AR_ADDR + off)
            return h.ap()

    F0_SZ = 22528
    NF0 = 3
    cf0 = Carver(0, F0_SZ)
    FTW = 1024
    f0_in = [cf0.take(FTW, F32) for i in range(NF0)]
    f0_t1 = [cf0.take(FTW, F32) for i in range(1)]
    f0_ab = cf0.take(FTW, BF16)
    f0_i8 = [cf0.take(FTW, I8) for i in range(NF0)]

    c1 = Carver(F0_SZ, ARENA)
    p1_x = [c1.take(D, F32) for i in range(2)]
    p1_sq = [c1.take(D, F32) for i in range(2)]
    p1_nf = [c1.take(D, F32) for i in range(2)]
    p1_i32 = c1.take(D, I32)
    p1_qb = c1.take(D, F32)
    p1_st = [c1.take(NCH * 128, BF16) for i in range(2)]
    p1_stq = c1.take(NCH * 128, BF16)
    p1_m = [c1.take(1, F32) for i in range(2)]
    p1_m2 = [c1.take(1, F32) for i in range(2)]
    p1_v = [c1.take(1, F32) for i in range(2)]
    p1_sd = [c1.take(1, F32) for i in range(2)]
    p1_rs = [c1.take(1, F32) for i in range(2)]
    p1_g0 = c1.take(1, F32)

    c2 = Carver(F0_SZ, ARENA)
    p2_xn = [c2.take(NCH * 256, BF16) for i in range(2)]
    p2_K = [c2.take(S, BF16) for h in range(4)]
    p2_Q = [c2.take(OWN, BF16) for h in range(4)]
    p2_V = c2.take(NT * 512, BF16)
    p2_wk = [c2.take(NCH * 128, BF16) for h in range(4)]
    p2_wq = [c2.take(NCH * 128, BF16) for h in range(4)]
    p2_wv = c2.take(NCH * 512, BF16)
    p2_P = [c2.take(1024, BF16) for i in range(2)]
    p2_z = [c2.take(512, BF16) for i in range(2)]
    p2_rd = [c2.take(512, F32, parts=1) for i in range(2)]
    p2_dn = [c2.take(512, F32, parts=1) for i in range(2)]
    p2_rb = [c2.take(512, F32) for i in range(2)]

    c3 = Carver(0, ARENA)
    f1_wo = c3.take(H * 1024, BF16)
    f1_z = c3.take(H * 512, BF16)
    f1_ao = [c3.take(512, F32) for i in range(2)]

    c4 = Carver(0, ARENA)
    f2_xq = c4.take(NCH * OWN, BF16)
    f2_gw = [c4.take(NCH * 512, BF16) for i in range(2)]
    f2_vw = [c4.take(NCH * 512, BF16) for i in range(2)]
    f2_gbb = [c4.take(512, F32) for i in range(2)]
    f2_vbb = [c4.take(512, F32) for i in range(2)]
    f2_g1 = [c4.take(512, F32) for i in range(2)]
    f2_g2 = [c4.take(512, F32) for i in range(2)]
    f2_gs = [c4.take(512, F32) for i in range(2)]
    f2_v1 = [c4.take(512, F32) for i in range(2)]
    f2_v2 = [c4.take(512, F32) for i in range(2)]
    f2_gvb = [c4.take(512, F32) for i in range(2)]
    f2_am = c4.take(1, F32)
    f2_gbr = [c4.take(512, F32, parts=1) for i in range(2)]
    f2_vbr = [c4.take(512, F32, parts=1) for i in range(2)]

    c5 = Carver(0, ARENA)
    f3_gv = [c5.take(HID, F32) for i in range(1)]
    f3_i32 = c5.take(HID, I32)
    f3_hb = c5.take(HID, F32)
    f3_stg = c5.take(NHCH * 512, I8)

    c6 = Carver(0, ARENA)
    f3_hq = c6.take(NHCH * 512, BF16)
    f3_ow = [c6.take(8 * 1024, BF16) for i in range(2)]
    f3_x = [c6.take(512, F32) for i in range(2)]
    f3_ao = [c6.take(512, F32) for i in range(2)]
    f3_t = [c6.take(512, F32) for i in range(2)]
    f3_o = [c6.take(512, F32) for i in range(2)]
    f3_sbc = c6.take(512, F32)

    ps = e(nc.psum_tensor("ps", [128, 4096], F32))

    def bank(i):
        return ps[:, 512 * i:512 * (i + 1)]

    def bankpair(i):
        return ps[:, 512 * i:512 * (i + 2)]

    def bank_bf(i):
        return ps[:, 512 * i:512 * (i + 1)].bitcast(BF16)

    em.alloc_sems = lambda: None

    O, W, DMA = em.op, em.wait, em.dma
    SQ = em.seq

    DMA("y", idn_b[:], iden_bf_in[:], "lset")
    DMA("y", idn_f[:], iden_f_in[:], "lset")
    DMA("y", masks[:].rearrange("k (m q) -> k m q", m=8),
        masks_in[:].rearrange("m k q -> k m q"), "lset")
    t_setup = DMA("y", obc_sb[:], obc_in[:], "lset")
    for tile, val in ((ones_b, 1.0), (ones_f, 1.0), (onesrow_f, 1.0), (c_zero, 0.0),
                      (c_one, 1.0), (c_mone, -1.0), (c_eps, 1e-5), (c_geps, 1e-5),
                      (c_invD, 1.0 / D), (c_inv127, 1.0 / 127.0), (c_expb, EXP_BIAS)):
        O("v", lambda v, tl=tile, vv=val: v.memset(tl[:], vv))
    O("v", lambda v: v.memset(gh_run[:], 0.0))
    t_msets = SQ("v", lambda v: v.memset(wsum[:], 0.0))
    for _e in ("t", "s", "g", "y"):
        W(_e, "v", t_msets)
    W("t", "lset", t_setup)
    W("s", "lset", t_setup)

    f0_tiles = []
    for m, ap_ in (("gw", gwT), ("vw", vwT), ("ow", owT)):
        nr = NCH if m != "ow" else NHCH
        ncols = HID if m != "ow" else D
        for ct in range(ncols // FTW):
            for rt in range(nr):
                f0_tiles.append((m, ap_, rt, ct, FTW))
    assert len(f0_tiles) == 384
    f0 = {"i": 0, "guard": [("s", 0)] * NF0}

    def emit_f0(n):
        for _ in range(n):
            i = f0["i"]
            if i >= len(f0_tiles):
                return
            f0["i"] += 1
            m, ap_, rt, ct, tw = f0_tiles[i]
            buf = f0_in[i % NF0]
            eng = "s" if i % 2 == 0 else "v"
            ge, gc = f0["guard"][i % NF0]
            W("y", ge, gc)
            td = DMA("y", buf[:, 0:tw], ap_[rt * 128:(rt + 1) * 128, ct * tw:(ct + 1) * tw],
                     f"lf{i % NF0}")
            W(eng, f"lf{i % NF0}", td)
            if eng == "s":
                W("s", "s", em.cnt["s"])
                f0["guard"][i % NF0] = ("s", O("s", lambda s_, b=buf, tw_=tw, col=i:
                                               s_.activation(
                    out=f0_ab[:, 0:tw_], in_=b[:, 0:tw_], func=AF.Abs, bias=c_zero[:],
                    scale=1.0, accum_out=wsum[:, col:col + 1]), inc="s"))
            else:
                f0["guard"][i % NF0] = ("v", O("v", lambda v, b=buf, tw_=tw, col=i:
                                               v.tensor_reduce(
                    out=wsum[:, col:col + 1], in_=b[:, 0:tw_], axis=AX.X, op=ALU.add,
                    apply_absolute_value=True), inc="v"))

    def finish_f0():
        emit_f0(len(f0_tiles))
        W("v", "s", em.cnt["s"])
        W("v", "v", em.cnt["v"])
        spans = {"gw": [], "vw": [], "ow": []}
        for i, (m, *_r) in enumerate(f0_tiles):
            spans[m].append(i)
        for mi, m in enumerate(("gw", "vw", "ow")):
            lo, hi = min(spans[m]), max(spans[m]) + 1
            em.seq("v", lambda v, lo=lo, hi=hi: v.tensor_reduce(
                out=f0tmp_a[:], in_=wsum[:, lo:hi], axis=AX.X, op=ALU.add))
            tv = O("v", lambda v: v.tensor_copy(out=f0tmp_b[:], in_=f0tmp_a[:]), inc="v")
            W("t", "v", tv)
            tt = O("t", lambda t, b=mi: t.matmul(out=bank(7)[0:1, b:b + 1], lhsT=ones_f[:],
                                                 rhs=f0tmp_b[:], start=True, stop=True,
                                                 skip_group_check=True), inc="t")
            W("s", "t", tt)
            ts1 = O("s", lambda s_, b=mi: s_.activation(
                out=gtmp[0:1, 2 * b:2 * b + 1], in_=bank(7)[0:1, b:b + 1], func=AF.Identity,
                bias=c_zero[0:1, :], scale=1.0 / (D * HID)), inc="s")
            W("v", "s", ts1)
            em.seq("v", lambda v, b=mi: v.tensor_scalar(
                out=rca[0:1, 0:1], in0=gtmp[0:1, 2 * b:2 * b + 1],
                scalar1=c_geps[0:1, :], scalar2=None, op0=ALU.max))
            em.seq("v", lambda v: v.reciprocal(out=rcb[0:1, 0:1], in_=rca[0:1, 0:1]))
            em.seq("v", lambda v, b=mi: v.tensor_copy(
                out=gtmp[0:1, 2 * b + 1:2 * b + 2], in_=rcb[0:1, 0:1]))
            tv2 = O("v", lambda v, b=mi: v.tensor_scalar(
                out=gtmp[0:1, 2 * b:2 * b + 1], in0=rca[0:1, 0:1],
                scalar1=c_inv127[0:1, :], scalar2=None, op0=ALU.mult), inc="v")
            W("t", "v", tv2)
            tt2 = O("t", lambda t, b=mi: t.matmul(
                out=bank(7)[:, 8 + 2 * b:8 + 2 * b + 2], lhsT=onesrow_f[:],
                rhs=gtmp[0:1, 2 * b:2 * b + 2], start=True, stop=True,
                skip_group_check=True), inc="t")
            W("s", "t", tt2)
            O("s", lambda s_, m_=m, b=mi: s_.copy(out=g127_w[m_][:],
                                                  in_=bank(7)[:, 8 + 2 * b:8 + 2 * b + 1]))
            O("s", lambda s_, m_=m, b=mi: s_.copy(out=rg_w[m_][:],
                                                  in_=bank(7)[:, 8 + 2 * b + 1:8 + 2 * b + 2]),
              inc="s")
        return em.cnt["s"]

    f0b_tiles = []
    for m, ap_, o8 in (("gw", gwT, gw_t8), ("vw", vwT, vw_t8), ("ow", owT, ow_t8)):
        nr = NCH if m != "ow" else NHCH
        ncols = HID if m != "ow" else D
        for ct in range(ncols // FTW):
            for rt in range(nr):
                f0b_tiles.append((m, ap_, o8, rt, ct, FTW))
    assert len(f0b_tiles) == 384
    f0b = {"i": 0, "gv": [0] * NF0, "gd": [0] * NF0, "gamma": 0, "f0sg": [0] * NF0,
           "t1g": [0, 0]}

    def emit_f0b(n):
        for _ in range(n):
            i = f0b["i"]
            if i >= len(f0b_tiles):
                return
            f0b["i"] += 1
            m, ap_, o8, rt, ct, tw = f0b_tiles[i]
            buf, ob = f0_in[i % NF0], f0_i8[i % NF0]
            if i < NF0:
                ge, gc = f0["guard"][i % NF0]
                W("y", ge, gc)
            W("y", "v", f0b["gv"][i % NF0])
            td = DMA("y", buf[:, 0:tw], ap_[rt * 128:(rt + 1) * 128, ct * tw:(ct + 1) * tw],
                     f"lf{i % NF0}")
            W("v", f"lf{i % NF0}", td)
            W("v", "s", f0b["gamma"])
            tg = em.seq("v", lambda v, b=buf, tw_=tw, m_=m: v.tensor_scalar(
                out=f0_t1[0][:, 0:tw_], in0=b[:, 0:tw_], scalar1=rg_w[m_][:],
                scalar2=c_one[:], op0=ALU.mult, op1=ALU.min))
            f0b["gv"][i % NF0] = tg
            W("v", f"sf{i % NF0}", f0b["gd"][i % NF0])
            tv = em.seq("v", lambda v, ob_=ob, tw_=tw: v.tensor_scalar(
                out=ob_[:, 0:tw_], in0=f0_t1[0][:, 0:tw_], scalar1=c_mone[:],
                scalar2=None, op0=ALU.max))
            f0b["f0sg"][i % NF0] = tv
            W("y", "v", tv)
            f0b["gd"][i % NF0] = DMA(
                "y", o8[rt * 128:(rt + 1) * 128, ct * tw:(ct + 1) * tw], ob[:, 0:tw],
                f"sf{i % NF0}")

    g_xbuf = [0, 0]
    g_nb = [0, 0]
    g_qb = 0
    g_bank = [0, 0]
    g_stdma = [0, 0]
    g_stqdma = 0
    g_sv = [0, 0]
    own_idx = 0
    t_p1_ld = 0
    for t in range(NT):
        bi = t % 2
        xbuf, nbuf, stg = p1_x[bi], p1_nf[bi], p1_st[bi]
        is_own = (t % 2 == 0)
        W("y", "v", g_xbuf[bi])
        td = DMA("y", xbuf[:], x_tm[t * 128:(t + 1) * 128, :], f"lx{bi}")
        W("s", f"lx{bi}", td)
        W("s", "v", g_sv[bi])
        ts_sq = O("s", lambda s_, xb=xbuf, b=bi: s_.activation(
            out=p1_sq[b][:], in_=xb[:], func=AF.Square, bias=c_zero[:], scale=1.0), inc="s")
        W("v", f"lx{bi}", td)
        W("v", "s", ts_sq)
        O("v", lambda v, xb=xbuf, b=bi: v.tensor_reduce(out=p1_m[b][:], in_=xb[:],
                                                        axis=AX.X, op=ALU.add))
        O("v", lambda v, b=bi: v.tensor_reduce(out=p1_m2[b][:], in_=p1_sq[b][:],
                                               axis=AX.X, op=ALU.add))
        O("v", lambda v, b=bi: v.tensor_scalar(out=p1_m[b][:], in0=p1_m[b][:],
                                               scalar1=c_invD[:], scalar2=None, op0=ALU.mult))
        SQ("v", lambda v, b=bi: v.tensor_scalar(out=p1_m2[b][:], in0=p1_m2[b][:],
                                                scalar1=c_invD[:], scalar2=None, op0=ALU.mult))
        tv_var = O("v", lambda v, b=bi: v.scalar_tensor_tensor(out=p1_v[b][:], in0=p1_m[b][:],
                                                               scalar=p1_m[b][:], in1=p1_m2[b][:],
                                                               op0=ALU.mult, op1=ALU.subtract),
                   inc="v")
        W("s", "v", tv_var)
        ts_sd = O("s", lambda s_, b=bi: s_.activation(out=p1_sd[b][:], in_=p1_v[b][:],
                                                      func=AF.Sqrt,
                                                      bias=c_eps[:], scale=-1.0), inc="s")
        W("v", "s", ts_sd)
        g_sv[bi] = SQ("v", lambda v, b=bi: v.reciprocal(out=p1_rs[b][:], in_=p1_sd[b][:]))
        W("v", "t", g_nb[bi])
        tv_nb = O("v", lambda v, xb=xbuf, nb=nbuf, b=bi: v.tensor_scalar(
            out=nb[:], in0=xb[:], scalar1=p1_m[b][:], scalar2=p1_rs[b][:],
            op0=ALU.subtract, op1=ALU.mult), inc="v")
        tv_done = tv_nb
        if is_own:
            oi = own_idx
            O("v", lambda v, nb=nbuf: v.tensor_reduce(out=p1_g0[:], in_=nb[:], axis=AX.X,
                                                      op=ALU.max, apply_absolute_value=True))
            O("v", lambda v, o=oi: v.tensor_scalar(out=gx_all[:, o:o + 1], in0=p1_g0[:],
                                                   scalar1=c_geps[:], scalar2=None, op0=ALU.max))
            O("v", lambda v: v.tensor_scalar(out=p1_g0[:], in0=p1_g0[:],
                                             scalar1=c_geps[:], scalar2=None, op0=ALU.max))
            SQ("v", lambda v: v.tensor_scalar(out=p1_g0[:], in0=p1_g0[:],
                                              scalar1=c_inv127[:], scalar2=None,
                                              op0=ALU.mult))
            SQ("v", lambda v: v.reciprocal(out=rca[:, 0:1], in_=p1_g0[:]))
            O("v", lambda v, o=oi: v.tensor_copy(out=gxr127[:, o:o + 1], in_=rca[:, 0:1]))
            SQ("v", lambda v, nb=nbuf: v.tensor_scalar(out=p1_i32[:], in0=nb[:],
                                                       scalar1=rca[:, 0:1],
                                                       scalar2=None, op0=ALU.mult))
            W("v", "t", g_qb)
            tv_done = O("v", lambda v: v.tensor_copy(out=p1_qb[:], in_=p1_i32[:]), inc="v")
        g_xbuf[bi] = tv_done
        if DBG_SCRATCH:
            W("y", "v", tv_done)
            DMA("y", xn_tm_dbg[t * 128:(t + 1) * 128, :], nbuf[:], "dbg")
            for di, (dst, src_) in enumerate([(st_dbg[t, :, 0:1], p1_m), (st_dbg[t, :, 1:2], p1_m2),
                                              (st_dbg[t, :, 2:3], p1_v), (st_dbg[t, :, 3:4], p1_sd),
                                              (st_dbg[t, :, 4:5], p1_rs)]):
                if "dbg" not in em.cnt:
                    em.cnt["dbg"] = 0
                em.cnt["dbg"] += 16
                em.ops["y"].append(lambda e, o=dst, i=src_, s="dbg": e.dma_start(
                    out=o, in_=i[:], allow_slow_non_contiguous=True).then_inc(em.sems[s], 16))

        srcs = [(nbuf, 0, stg, 0), (nbuf, 8, stg, 1)]
        if is_own:
            srcs += [(p1_qb, 0, p1_stq, 0), (p1_qb, 8, p1_stq, 1)]
        W("t", "v", tv_done)
        for ci, (srcbuf, c0, dstg, half) in enumerate(srcs):
            bnk = ci % 2
            W("t", "v", g_bank[bnk])
            tt = None
            for k in range(8):
                tt = O("t", lambda t_, sb_=srcbuf, c=c0, k_=k, b=bnk: t_.transpose(
                    out=bankpair(2 * b)[:, k_ * 128:(k_ + 1) * 128],
                    in_=sb_[:, (c + k_) * 128:(c + k_ + 1) * 128],
                    identity=idn_f[:]), inc="t" if k == 7 else None)
            if ci == 1:
                g_nb[bi] = tt
            if ci == 3:
                g_qb = tt
            W("v", "t", tt)
            W("v", "sstq" if dstg is p1_stq else f"sst{bi}",
              g_stqdma if dstg is p1_stq else g_stdma[bi])
            tv_ev = O("v", lambda v, b=bnk, h=half, d=dstg: v.tensor_copy(
                out=d[:, h * 1024:(h + 1) * 1024], in_=bankpair(2 * b)[:]), inc="v")
            g_bank[bnk] = tv_ev
            if ci == 1:
                W("y", "v", tv_ev)
                g_stdma[bi] = DMA("y", xn_sc[:, t * 128:(t + 1) * 128].rearrange(
                    "(c dd) s -> dd c s", dd=128),
                    stg[:].rearrange("dd (c s) -> dd c s", s=128), f"sst{bi}")
                t_p1_ld = g_stdma[bi]
            if ci == 3:
                W("y", "v", tv_ev)
                g_stqdma = DMA("y", xq_sc[:, own_idx * 128:(own_idx + 1) * 128].rearrange(
                    "(c dd) s -> dd c s", dd=128),
                    p1_stq[:].rearrange("dd (c s) -> dd c s", s=128), "sstq")
                t_p1_ld = g_stqdma
        if is_own:
            own_idx += 1
        emit_f0(12)
    assert own_idx == NOT

    L = dict(locals())
    if PHASES >= 2:
        emit_p2(nc, em, L)
    if PHASES >= 3:
        emit_f1z(nc, em, L)
    if PHASES >= 4:
        emit_f2(nc, em, L)
    if PHASES >= 5:
        emit_f3a(nc, em, L)
    if PHASES >= 6:
        emit_f3b(nc, em, L)

    finish_all(nc, em, es)
    return nc


def emit_p2(nc, em, L):
    O, W, DMA = em.op, em.wait, em.dma
    g = L
    bank, bankpair, bank_bf = g["bank"], g["bankpair"], g["bank_bf"]
    p2_xn, p2_K, p2_Q, p2_V = g["p2_xn"], g["p2_K"], g["p2_Q"], g["p2_V"]
    p2_wk, p2_wq, p2_wv = g["p2_wk"], g["p2_wq"], g["p2_wv"]
    p2_P, p2_z, p2_rd, p2_rb = g["p2_P"], g["p2_z"], g["p2_rd"], g["p2_rb"]
    p2_dn = g["p2_dn"]
    masks, ones_b, onesrow_f = g["masks"], g["ones_b"], g["onesrow_f"]
    c_expb = g["c_expb"]
    wkT, wqT, wvT = g["wkT"], g["wqT"], g["wvT"]
    xn_sc, z_sc = g["xn_sc"], g["z_sc"]
    emit_f0, finish_f0, emit_f0b = g["emit_f0"], g["finish_f0"], g["emit_f0b"]
    f0b = g["f0b"]
    t_p1_ld = g["t_p1_ld"]
    t_setup = g["t_setup"]

    st = {
        "xn": [0, 0],
        "wkq": [0, 0, 0, 0],
        "wv": 0,
        "kqv_t": 0,
        "pb67": [0, 0, 0, 0],
        "P_s": [0, 0],
        "P_t": [0, 0],
        "sc_s": [0, 0],
        "z_t": [0, 0],
        "d_s": 0,
        "rb_s": [0, 0],
        "rb_v": [0, 0],
        "z_ld": [0, 0],
        "bc_s": 0,
    }
    W("v", "lset", t_setup)
    em.barrier()
    f0b["gamma"] = finish_f0()

    for grp in range(4):
        W("s", "t", st["kqv_t"])
        for hl in range(4):
            W("g", "t", st["wkq"][hl])
            DMA("g", p2_wk[hl][:].rearrange("dd (c e) -> dd c e", e=128),
                wkT[4 * grp + hl, :, :].rearrange("(c dd) e -> dd c e", dd=128), "lw")
        for hl in range(4):
            DMA("g", p2_wq[hl][:].rearrange("dd (c e) -> dd c e", e=128),
                wqT[4 * grp + hl, :, :].rearrange("(c dd) e -> dd c e", dd=128), "lw")

        W("g", "t", st["wv"])
        for hl in range(4):
            tw_v = DMA("g",
                       p2_wv[:].rearrange("dd (c h e) -> dd c h e", h=4, e=128)[:, :, hl, :],
                       wvT[4 * grp + hl, :, :].rearrange("(c dd) e -> dd c e", dd=128), "lw")

        W("t", "v", st["z_t"][0])
        W("t", "v", st["z_t"][1])
        W("t", "s", st["bc_s"])
        W("t", "s", st["d_s"])
        ecnt = 0
        NSLAB = S // 256
        for sbk in range(NSLAB):
            bi = sbk % 2
            W("y", "t", st["xn"][bi])
            txn = DMA("y", p2_xn[bi][:].rearrange("dd (c s) -> dd c s", s=256),
                xn_sc[:, sbk * 256:(sbk + 1) * 256].rearrange(
                "(c dd) s -> dd c s", dd=128), f"lxn{bi}")
            W("t", f"lxn{bi}", txn)
            if sbk == 0:
                W("t", "lw", tw_v)
            for hl in range(4):
                bnk = 4 + (ecnt % 4)
                W("t", "s", st["pb67"][ecnt % 4])
                for c in range(NCH):
                    ttk = O("t", lambda t_, hl_=hl, c_=c, b=bi, bk=bnk: t_.matmul(
                        out=bank(bk)[:, 0:256],
                        lhsT=p2_wk[hl_][:, c_ * 128:(c_ + 1) * 128],
                        rhs=p2_xn[b][:, c_ * 256:(c_ + 1) * 256],
                        start=(c_ == 0), stop=(c_ == NCH - 1), skip_group_check=True),
                        inc="t" if c == NCH - 1 else None)
                W("s", "t", ttk)
                st["pb67"][ecnt % 4] = O("s", lambda s_, hl_=hl, sb_=sbk, bk=bnk: s_.copy(
                    out=p2_K[hl_][:, sb_ * 256:(sb_ + 1) * 256], in_=bank(bk)[:, 0:256]),
                    inc="s")
                ecnt += 1
                bnk = 4 + (ecnt % 4)
                W("t", "s", st["pb67"][ecnt % 4])
                for c in range(NCH):
                    ttq = O("t", lambda t_, hl_=hl, c_=c, b=bi, bk=bnk: t_.matmul(
                        out=bank(bk)[:, 0:128],
                        lhsT=p2_wq[hl_][:, c_ * 128:(c_ + 1) * 128],
                        rhs=p2_xn[b][:, c_ * 256:c_ * 256 + 128],
                        start=(c_ == 0), stop=(c_ == NCH - 1), skip_group_check=True),
                        inc="t" if c == NCH - 1 else None)
                W("s", "t", ttq)
                st["pb67"][ecnt % 4] = O("s", lambda s_, hl_=hl, sb_=sbk, bk=bnk: s_.copy(
                    out=p2_Q[hl_][:, sb_ * 128:(sb_ + 1) * 128], in_=bank(bk)[:, 0:128]),
                    inc="s")
                ecnt += 1
                if sbk == NSLAB - 1:
                    st["wkq"][hl] = em.cnt["t"]
            for st4 in range(2):
                kc = sbk * 2 + st4
                bnk = 4 + (ecnt % 4)
                W("t", "s", st["pb67"][ecnt % 4])
                for c in range(NCH):
                    ttv = O("t", lambda t_, s4=st4, c_=c, b=bi, bk=bnk: t_.matmul(
                        out=bank(bk),
                        lhsT=p2_xn[b][:, c_ * 256 + s4 * 128: c_ * 256 + (s4 + 1) * 128],
                        rhs=p2_wv[:, c_ * 512:(c_ + 1) * 512],
                        start=(c_ == 0), stop=(c_ == NCH - 1), skip_group_check=True),
                        inc="t" if c == NCH - 1 else None)
                W("s", "t", ttv)
                st["pb67"][ecnt % 4] = O("s", lambda s_, kc_=kc, bk=bnk: s_.copy(
                    out=p2_V[:, kc_ * 512:(kc_ + 1) * 512], in_=bank(bk)), inc="s")
                ecnt += 1
            st["xn"][bi] = em.cnt["t"]
        st["wv"] = em.cnt["t"]
        t_proj_s = em.cnt["s"]
        W("t", "s", t_proj_s)

        for _i in range(4):
            W("t", "s", st["pb67"][_i])
        pending = [None]

        def flush_pending():
            if pending[0] is not None:
                pending[0]()
                pending[0] = None

        for hl in range(4):
            h = 4 * grp + hl
            for j in range(NQB):
                nch_j = 8 * (j + 1)
                npair = nch_j // 2
                ui = h * NQB + j
                zb = ui % 2
                zbank = 4 if zb == 0 else 7
                W("t", "v", st["z_t"][zb])
                W("t", "s", st["d_s"])

                def sc_pair(mp):
                    pr = mp % 2
                    W("t", "s", st["sc_s"][pr])
                    t2 = None
                    for mm, half in ((2 * mp, 0), (2 * mp + 1, 1)):
                        t2 = O("t", lambda t_, hl_=hl, j_=j, m_=mm, pr_=pr, hf=half: t_.matmul(
                            out=bank(2 * pr_ + hf),
                            lhsT=p2_K[hl_][:, m_ * 128:(m_ + 1) * 128],
                            rhs=p2_Q[hl_][:, j_ * 512:(j_ + 1) * 512],
                            start=True, stop=True, skip_group_check=True),
                            inc="t" if half == 1 else None)
                    return t2

                tsc = [None] * npair
                tsc[0] = sc_pair(0)
                flush_pending()
                tt_pv = 0
                for mp in range(npair):
                    pr = mp % 2
                    W("s", "t", tsc[mp])
                    W("s", "t", st["P_t"][pr])
                    ts_e = O("s", lambda s_, pr_=pr: s_.activation(
                        out=p2_P[pr_][:], in_=bankpair(2 * pr_), func=AF.Exp,
                        bias=c_expb[:], scale=SCALE), inc="s")
                    st["P_s"][pr] = ts_e
                    st["sc_s"][pr] = ts_e
                    if mp + 1 < npair:
                        tsc[mp + 1] = sc_pair(mp + 1)
                    tvm = [None, None]
                    for mm, half in ((2 * mp, 0), (2 * mp + 1, 1)):
                        mprime = mm - 8 * j
                        if mprime >= 0:
                            wgrp = min(mprime // 2 + 1, 4)
                            W("v", "s", ts_e)
                            tvm[half] = O("v", lambda v, pr_=pr, hf=half, mp_=mprime, w=wgrp:
                                          v.tensor_tensor(
                                out=p2_P[pr_][:, hf * 512: hf * 512 + w * 128],
                                in0=p2_P[pr_][:, hf * 512: hf * 512 + w * 128],
                                in1=masks[:, mp_ * 512: mp_ * 512 + w * 128],
                                op=ALU.mult), inc="v")
                    for mm, half in ((2 * mp, 0), (2 * mp + 1, 1)):
                        if tvm[half] is not None:
                            W("t", "v", tvm[half])
                        else:
                            W("t", "s", ts_e)
                        O("t", lambda t_, hl_=hl, m_=mm, pr_=pr, hf=half,
                          s0=(mm == 0), s1=(mm == nch_j - 1), zk=zbank: t_.matmul(
                            out=bank(zk),
                            lhsT=p2_V[:, m_ * 512 + hl_ * 128: m_ * 512 + (hl_ + 1) * 128],
                            rhs=p2_P[pr_][:, hf * 512:(hf + 1) * 512],
                            start=s0, stop=s1, skip_group_check=True))
                        tt_pv = O("t", lambda t_, m_=mm, pr_=pr, hf=half,
                                  s0=(mm == 0), s1=(mm == nch_j - 1): t_.matmul(
                            out=bank(5)[0:1, :], lhsT=ones_b[:],
                            rhs=p2_P[pr_][:, hf * 512:(hf + 1) * 512],
                            start=s0, stop=s1, skip_group_check=True), inc="t")
                    st["P_t"][pr] = tt_pv
                    emit_f0b(1)
                W("s", "t", tt_pv)
                ts_dn = O("s", lambda s_, u=ui % 2: s_.copy(out=p2_dn[u][:],
                                                            in_=bank(5)[0:1, :]), inc="s")
                st["d_s"] = ts_dn

                def mk_norm(h=h, j=j, zb=zb, zbank=zbank, ui=ui, ts_dn=ts_dn):
                    def emit_norm():
                        u2 = ui % 2
                        W("v", "s", ts_dn)
                        tv_rd = O("v", lambda v: v.reciprocal(out=p2_rd[u2][:],
                                                              in_=p2_dn[u2][:]), inc="v")
                        W("t", "v", tv_rd)
                        W("t", "s", st["bc_s"])
                        tt_bc = O("t", lambda t_: t_.matmul(out=bank(6), lhsT=onesrow_f[:],
                                                            rhs=p2_rd[u2][:], start=True,
                                                            stop=True,
                                                            skip_group_check=True), inc="t")
                        W("s", "t", tt_bc)
                        W("s", "v", st["rb_v"][u2])
                        ts_rb = O("s", lambda s_: s_.copy(out=p2_rb[u2][:], in_=bank(6)),
                                  inc="s")
                        st["bc_s"] = ts_rb
                        W("v", "s", ts_rb)
                        W("v", f"sz{zb}", st["z_ld"][zb])
                        tv_z = O("v", lambda v: v.tensor_tensor(
                            out=p2_z[zb][:], in0=bank(zbank), in1=p2_rb[u2][:],
                            op=ALU.mult), inc="v")
                        st["rb_v"][u2] = tv_z
                        st["z_t"][zb] = tv_z
                        W("y", "v", tv_z)
                        st["z_ld"][zb] = DMA("y", z_sc[h * 128:(h + 1) * 128,
                                                       j * 512:(j + 1) * 512],
                                             p2_z[zb][:], f"sz{zb}")
                    return emit_norm
                pending[0] = mk_norm()
        flush_pending()
        st["kqv_t"] = em.cnt["t"]
    emit_f0b(384)
    em.barrier()


def emit_f1z(nc, em, L):
    O, W, DMA = em.op, em.wait, em.dma
    bank = L["bank"]
    f1_wo, f1_z, f1_ao = L["f1_wo"], L["f1_z"], L["f1_ao"]
    woT, z_sc, ao_sc, obc_sb = L["woT"], L["z_sc"], L["ao_sc"], L["obc_sb"]

    g_wo = 0
    g_z = 0
    g_ao_s = [0, 0]
    g_ao_ld = [0, 0]
    for dh in range(2):
        W("g", "t", g_wo)
        two = DMA("g", f1_wo[:].rearrange("ee (h d) -> ee h d", h=H),
            woT[:, :, dh * 1024:(dh + 1) * 1024].rearrange(
            "h ee d -> ee h d"), "lwo")
        for sbk in range(4):
            W("y", "t", g_z)
            tz = DMA("y", f1_z[:].rearrange("ee (h s) -> ee h s", h=H),
                z_sc[:, sbk * 512:(sbk + 1) * 512].rearrange(
                "(h ee) s -> ee h s", ee=128), "lz")
            for dt in range(8):
                bnk = dt % 2
                W("t", "s", g_ao_s[bnk])
                if dt == 0:
                    W("t", "lwo", two)
                    W("t", "lz", tz)
                tt = None
                for h in range(H):
                    tt = O("t", lambda t_, h_=h, dt_=dt, b=bnk: t_.matmul(
                        out=bank(b),
                        lhsT=f1_wo[:, h_ * 1024 + dt_ * 128: h_ * 1024 + (dt_ + 1) * 128],
                        rhs=f1_z[:, h_ * 512:(h_ + 1) * 512],
                        start=(h_ == 0), stop=(h_ == H - 1), skip_group_check=True),
                        inc="t" if h == H - 1 else None)
                W("s", "t", tt)
                W("s", f"sao{bnk}", g_ao_ld[bnk])
                dchunk = dh * 8 + dt
                g_ao_s[bnk] = O("s", lambda s_, b=bnk, dc=dchunk: s_.activation(
                    out=f1_ao[b][:], in_=bank(b), func=AF.Identity,
                    bias=obc_sb[:, dc:dc + 1], scale=1.0), inc="s")
                W("y", "s", g_ao_s[bnk])
                g_ao_ld[bnk] = DMA("y", ao_sc[dchunk * 128:(dchunk + 1) * 128,
                                              sbk * 512:(sbk + 1) * 512], f1_ao[bnk][:],
                                   f"sao{bnk}")
            g_z = em.cnt["t"]
        g_wo = em.cnt["t"]
    em.barrier()


def emit_f2(nc, em, L):
    O, W, DMA = em.op, em.wait, em.dma
    bank = L["bank"]
    f2_xq, f2_gw, f2_vw = L["f2_xq"], L["f2_gw"], L["f2_vw"]
    f2_gbb, f2_vbb = L["f2_gbb"], L["f2_vbb"]
    f2_g1, f2_g2, f2_gs, f2_v1, f2_v2 = L["f2_g1"], L["f2_g2"], L["f2_gs"], L["f2_v1"], L["f2_v2"]
    f2_gvb, f2_am = L["f2_gvb"], L["f2_am"]
    f2_gbr, f2_vbr = L["f2_gbr"], L["f2_vbr"]
    onesrow_f = L["onesrow_f"]
    sc_g, sc_v, gx_all, gh_run = L["sc_g"], L["sc_v"], L["gx_all"], L["gh_run"]
    g127_w, c_zero = L["g127_w"], L["c_zero"]
    c_one, c_mone = L["c_one"], L["c_mone"]
    gw_t8, vw_t8, xq_sc, gv_sc = L["gw_t8"], L["vw_t8"], L["xq_sc"], L["gv_sc"]

    gb_in, vb_in = L["gb_in"], L["vb_in"]
    em.seq("v", lambda v: v.tensor_scalar(out=sc_g[:], in0=gx_all[:], scalar1=g127_w["gw"][:],
                                           scalar2=None, op0=ALU.mult))
    t_scv = O("v", lambda v: v.tensor_scalar(out=sc_v[:], in0=gx_all[:], scalar1=g127_w["vw"][:],
                                             scalar2=None, op0=ALU.mult), inc="v")
    W("s", "v", t_scv)

    txq = DMA("g", f2_xq[:].rearrange("dd (c s) -> dd c s", s=OWN),
              xq_sc[:, :].rearrange("(c dd) s -> dd c s", dd=128), "lxq")

    NHT = HID // 512
    g_w = [0, 0]
    g_psum = [0, 0, 0, 0]
    g_b4 = 0
    g_brow_t = [0, 0]
    g_bias_g = [0, 0]
    g_g1 = [0, 0]
    g_g2 = [0, 0]
    g_gs = [0, 0]
    g_gv_ld = [0, 0]
    for ht in range(NHT):
        bi = ht % 2
        W("g", "t", g_w[bi])
        tw1 = DMA("g", f2_gw[bi][:].rearrange("dd (c hh) -> dd c hh", hh=512),
            gw_t8[:, ht * 512:(ht + 1) * 512].rearrange(
            "(c dd) hh -> dd c hh", dd=128), f"ltw{bi}")
        tw2 = DMA("g", f2_vw[bi][:].rearrange("dd (c hh) -> dd c hh", hh=512),
            vw_t8[:, ht * 512:(ht + 1) * 512].rearrange(
            "(c dd) hh -> dd c hh", dd=128), f"ltw{bi}")
        W("v", f"ltw{bi}", tw2)
        O("v", lambda v, b=bi: v.tensor_scalar(out=f2_gw[b][:], in0=f2_gw[b][:],
                                               scalar1=c_one[:], scalar2=c_mone[:],
                                               op0=ALU.min, op1=ALU.max))
        tv_clip = O("v", lambda v, b=bi: v.tensor_scalar(out=f2_vw[b][:], in0=f2_vw[b][:],
                                                         scalar1=c_one[:], scalar2=c_mone[:],
                                                         op0=ALU.min, op1=ALU.max), inc="v")
        W("y", "t", g_brow_t[bi])
        DMA("y", f2_gbr[bi][:], gb_in[0:1, ht * 512:(ht + 1) * 512], f"lbr{bi}")
        t_brow = DMA("y", f2_vbr[bi][:], vb_in[0:1, ht * 512:(ht + 1) * 512], f"lbr{bi}")
        W("t", "s", g_b4)
        W("t", f"lbr{bi}", t_brow)
        tt_gb = O("t", lambda t_, b=bi: t_.matmul(
            out=bank(4), lhsT=onesrow_f[:], rhs=f2_gbr[b][:],
            start=True, stop=True, skip_group_check=True), inc="t")
        W("s", "t", tt_gb)
        W("s", "g", g_bias_g[bi])
        ts_gb = O("s", lambda s_, b=bi: s_.copy(out=f2_gbb[b][:], in_=bank(4)), inc="s")
        W("t", "s", ts_gb)
        tt_vb = O("t", lambda t_, b=bi: t_.matmul(
            out=bank(4), lhsT=onesrow_f[:], rhs=f2_vbr[b][:],
            start=True, stop=True, skip_group_check=True), inc="t")
        g_brow_t[bi] = tt_vb
        W("s", "t", tt_vb)
        g_b4 = O("s", lambda s_, b=bi: s_.copy(out=f2_vbb[b][:], in_=bank(4)), inc="s")
        ts_vb = g_b4

        for stt in range(NOT):
            pb = stt % 2
            for bb, wbuf in ((0, f2_gw[bi]), (2, f2_vw[bi])):
                bk = bb + pb
                W("t", "s", g_psum[bk])
                if stt == 0 and bb == 0:
                    W("t", "v", tv_clip)
                    if ht == 0:
                        W("t", "lxq", txq)
                    W("t", "s", ts_vb)
                tt = None
                for c in range(NCH):
                    tt = O("t", lambda t_, c_=c, st_=stt, wb=wbuf, bk_=bk: t_.matmul(
                        out=bank(bk_),
                        lhsT=f2_xq[:, c_ * OWN + st_ * 128: c_ * OWN + (st_ + 1) * 128],
                        rhs=wb[:, c_ * 512:(c_ + 1) * 512],
                        start=(c_ == 0), stop=(c_ == NCH - 1), skip_group_check=True),
                        inc="t" if c == NCH - 1 else None)
                if bb == 2 and stt == NOT - 1:
                    g_w[bi] = tt
                if bb == 0:
                    tt_g = tt
                else:
                    tt_v = tt
            W("s", "t", tt_g)
            W("s", "g", g_g1[pb])
            ts_g1 = O("s", lambda s_, st_=stt, bk=0 + pb, p=pb: s_.activation(
                out=f2_g1[p][:], in_=bank(bk), func=AF.Identity, bias=c_zero[:],
                scale=sc_g[:, st_:st_ + 1]), inc="s")
            g_psum[0 + pb] = ts_g1
            W("s", "t", tt_v)
            ts_v1 = O("s", lambda s_, st_=stt, bk=2 + pb, p=pb: s_.activation(
                out=f2_v1[p][:], in_=bank(bk), func=AF.Identity, bias=c_zero[:],
                scale=sc_v[:, st_:st_ + 1]), inc="s")
            g_psum[2 + pb] = ts_v1
            W("g", "s", ts_v1)
            W("g", "s", g_g2[pb])
            tg_g2 = O("g", lambda g_, b=bi, p=pb: g_.tensor_tensor(
                out=f2_g2[p][:], in0=f2_g1[p][:], in1=f2_gbb[b][:], op=ALU.add), inc="g")
            W("g", "v", g_gs[pb])
            tg_v2 = O("g", lambda g_, b=bi, p=pb: g_.tensor_tensor(
                out=f2_v2[p][:], in0=f2_v1[p][:], in1=f2_vbb[b][:], op=ALU.add), inc="g")
            g_g1[pb] = tg_v2
            if stt == NOT - 1:
                g_bias_g[bi] = tg_v2
            W("s", "g", tg_g2)
            W("s", "v", g_gs[pb])
            g_g2[pb] = O("s", lambda s_, p=pb: s_.activation(
                out=f2_gs[p][:], in_=f2_g2[p][:], func=AF.Silu, bias=c_zero[:],
                scale=1.0), inc="s")
            ts_sl = g_g2[pb]
            W("v", "s", ts_sl)
            W("v", "g", tg_v2)
            W("v", f"sgv{pb}", g_gv_ld[pb])
            gvb = f2_gvb[pb]
            tv_gv = em.seq("v", lambda v, gv=gvb, p=pb: v.tensor_tensor(
                out=gv[:], in0=f2_gs[p][:], in1=f2_v2[p][:], op=ALU.mult))
            em.seq("v", lambda v, gv=gvb: v.tensor_reduce(
                out=f2_am[:], in_=gv[:], axis=AX.X,
                op=ALU.max, apply_absolute_value=True))
            tv_am = O("v", lambda v, st_=stt: v.tensor_tensor(
                out=gh_run[:, st_:st_ + 1], in0=gh_run[:, st_:st_ + 1],
                in1=f2_am[:], op=ALU.max), inc="v")
            g_gs[pb] = tv_am
            W("y", "v", tv_gv)
            g_gv_ld[pb] = DMA("y", gv_sc[stt * 128:(stt + 1) * 128,
                                         ht * 512:(ht + 1) * 512], gvb[:], f"sgv{pb}")
    em.barrier()


def emit_f3a(nc, em, L):
    O, W, DMA = em.op, em.wait, em.dma
    bank_bf, bank, bankpair = L["bank_bf"], L["bank"], L["bankpair"]
    ps = L["ps"]
    gh_run, sc_h127, c_geps, c_inv127 = L["gh_run"], L["sc_h127"], L["c_geps"], L["c_inv127"]
    ghrow_t, shrow, g127_w, idn_f, idn_b = L["ghrow_t"], L["shrow"], L["g127_w"], L["idn_f"], L["idn_b"]
    f3_gv, f3_i32, f3_hb, f3_stg = L["f3_gv"], L["f3_i32"], L["f3_hb"], L["f3_stg"]
    gv_sc, hq_sc, row_sc = L["gv_sc"], L["hq_sc"], L["row_sc"]
    c_zero, rc16 = L["c_zero"], L["rc16"]

    em.seq("v", lambda v: v.tensor_scalar(out=gh_run[:], in0=gh_run[:], scalar1=c_geps[:],
                                           scalar2=None, op0=ALU.max))
    em.seq("v", lambda v: v.tensor_scalar(out=rc16[:], in0=gh_run[:], scalar1=c_inv127[:],
                                          scalar2=None, op0=ALU.mult))
    em.seq("v", lambda v: v.reciprocal(out=sc_h127[:], in_=rc16[:]))
    tv1 = O("v", lambda v: v.tensor_copy(out=rc16[:], in_=sc_h127[:]), inc="v")
    W("t", "v", tv1)
    tt1 = O("t", lambda t_: t_.transpose(out=ps[0:NOT, 0:128], in_=gh_run[:],
                                         identity=idn_f[:]), inc="t")
    W("v", "t", tt1)
    tv2 = O("v", lambda v: v.tensor_copy(out=ghrow_t[:], in_=ps[0:NOT, 0:128]), inc="v")
    W("y", "v", tv2)
    td1 = DMA("y", row_sc[:, :].rearrange("one (t ss) -> (one t) ss", ss=128), ghrow_t[:], "lrow")
    W("y", "lrow", td1)
    td2 = DMA("y", shrow[:], row_sc[:, :], "lrow")
    W("s", "lrow", td2)
    ts1 = O("s", lambda s_: s_.activation(out=shrow[:], in_=shrow[:], func=AF.Identity,
                                          bias=c_zero[0:1, :],
                                          scale=g127_w["ow"][0:1, :]), inc="s")
    L.update(f3_shrow_s=ts1)

    g_gv = 0
    g_pb = [0, 0]
    g_stg_ld = 0
    for sbk in range(4):
        for st4 in range(4):
            sti = sbk * 4 + st4
            W("y", "v", g_gv)
            tgv = DMA("y", f3_gv[0][:], gv_sc[sti * 128:(sti + 1) * 128, :], "lgv0")
            W("v", "lgv0", tgv)
            if st4 == 0:
                W("v", "shq", g_stg_ld)
            em.seq("v", lambda v, st_=sti: v.tensor_scalar(
                out=f3_i32[:], in0=f3_gv[0][:], scalar1=sc_h127[:, st_:st_ + 1],
                scalar2=None, op0=ALU.mult))
            tv_hb = O("v", lambda v: v.tensor_copy(out=f3_hb[:], in_=f3_i32[:]), inc="v")
            g_gv = tv_hb
            W("t", "v", tv_hb)
            for cb in range(8):
                bnk = cb % 2
                W("t", "v", g_pb[bnk])
                tt = None
                for k in range(8):
                    c = cb * 8 + k
                    tt = O("t", lambda t_, c_=c, k_=k, b=bnk: t_.transpose(
                        out=bankpair(2 * b)[:, k_ * 128:(k_ + 1) * 128],
                        in_=f3_hb[:, c_ * 128:(c_ + 1) * 128], identity=idn_f[:]),
                        inc="t" if k == 7 else None)
                W("v", "t", tt)
                g_pb[bnk] = O("v", lambda v, cb_=cb, s4=st4, b=bnk: v.tensor_copy(
                    out=f3_stg[:].rearrange("dd (c ss) -> dd c ss", ss=512)[
                        :, cb_ * 8:(cb_ + 1) * 8, s4 * 128:(s4 + 1) * 128],
                    in_=bankpair(2 * b)[:]), inc="v")
        W("y", "v", em.cnt["v"])
        last = 0
        for dmac in range(8):
            last = DMA("y", hq_sc[sbk, dmac * 8:(dmac + 1) * 8, :, :].rearrange(
                "c hh ss -> hh c ss"),
                f3_stg[:, dmac * 8 * 512:(dmac + 1) * 8 * 512].rearrange(
                    "dd (c ss) -> dd c ss", ss=512), "shq")
        g_stg_ld = last
    em.barrier()


def emit_f3b(nc, em, L):
    O, W, DMA = em.op, em.wait, em.dma
    bank = L["bank"]
    f3_hq, f3_ow, f3_x, f3_ao, f3_t, f3_o, f3_sbc = (
        L["f3_hq"], L["f3_ow"], L["f3_x"], L["f3_ao"], L["f3_t"], L["f3_o"], L["f3_sbc"])
    c_one, c_mone = L["c_one"], L["c_mone"]
    onesrow_f, shrow = L["onesrow_f"], L["shrow"]
    hq_sc, ow_t8, xT_own, ao_sc, out_fm = (
        L["hq_sc"], L["ow_t8"], L["xT_own"], L["ao_sc"], L["out_fm"])

    g_hq = 0
    g_ow = [0, 0]
    g_x = [0, 0]
    g_ao = [0, 0]
    g_o = [0, 0]
    g_banks_v = 0
    g_sbc_v = 0
    for sbk in range(4):
        W("g", "t", g_hq)
        thq = DMA("g", f3_hq[:].rearrange("hh (c ss) -> hh c ss", ss=512),
              hq_sc[sbk, :, :, :].rearrange("c hh ss -> hh c ss"), "lhq")
        W("t", "v", g_sbc_v)
        tt_sbc = O("t", lambda t_, sb_=sbk: t_.matmul(
            out=bank(0), lhsT=onesrow_f[:], rhs=shrow[0:1, sb_ * 512:(sb_ + 1) * 512],
            start=True, stop=True, skip_group_check=True), inc="t")
        W("s", "t", tt_sbc)
        ts_sbc = O("s", lambda s_: s_.copy(out=f3_sbc[:], in_=bank(0)), inc="s")
        for dh in range(2):
            W("t", "s", ts_sbc)
            W("t", "v", g_banks_v)
            tt_ffn = None
            for grp in range(8):
                bi = grp % 2
                W("g", "t", g_ow[bi])
                tow = DMA("g", f3_ow[bi][:].rearrange("hh (c d) -> hh c d", d=1024),
                    ow_t8[grp * 1024:(grp + 1) * 1024,
                          dh * 1024:(dh + 1) * 1024].rearrange(
                    "(c hh) d -> hh c d", hh=128), f"low{bi}")
                W("v", f"low{bi}", tow)
                tv_owc = O("v", lambda v, b=bi: v.tensor_scalar(
                    out=f3_ow[b][:], in0=f3_ow[b][:], scalar1=c_one[:], scalar2=c_mone[:],
                    op0=ALU.min, op1=ALU.max), inc="v")
                W("t", "v", tv_owc)
                if grp == 0:
                    W("t", "lhq", thq)
                for dt in range(8):
                    for k in range(8):
                        c = grp * 8 + k
                        tt_ffn = O("t", lambda t_, b=bi, k_=k, dt_=dt, c_=c,
                                   s0=(grp == 0 and k == 0), s1=(grp == 7 and k == 7): t_.matmul(
                            out=bank(dt_),
                            lhsT=f3_ow[b][:, k_ * 1024 + dt_ * 128: k_ * 1024 + (dt_ + 1) * 128],
                            rhs=f3_hq[:, c_ * 512:(c_ + 1) * 512],
                            start=s0, stop=s1, skip_group_check=True),
                            inc="t" if (dt == 7 and k == 7) else None)
                g_ow[bi] = tt_ffn
            for dt in range(8):
                dchunk = dh * 8 + dt
                bi2 = dt % 2
                W("y", "v", g_x[bi2])
                tx = DMA("y", f3_x[bi2][:], xT_own[dchunk * 128:(dchunk + 1) * 128,
                                                   sbk * 512:(sbk + 1) * 512], f"lx3{bi2}")
                W("y", "v", g_ao[bi2])
                tao = DMA("y", f3_ao[bi2][:], ao_sc[dchunk * 128:(dchunk + 1) * 128,
                                                    sbk * 512:(sbk + 1) * 512], f"lao{bi2}")
                W("v", "t", tt_ffn)
                W("v", f"lao{bi2}", tao)
                W("v", f"lx3{bi2}", tx)
                W("v", f"so{bi2}", g_o[bi2])
                em.seq("v", lambda v, dt_=dt, b=bi2: v.tensor_tensor(
                    out=f3_t[b][:], in0=bank(dt_), in1=f3_sbc[:], op=ALU.mult))
                em.seq("v", lambda v, b=bi2: v.tensor_tensor(
                    out=f3_t[b][:], in0=f3_t[b][:], in1=f3_ao[b][:], op=ALU.add))
                tv_o = O("v", lambda v, b=bi2: v.tensor_tensor(
                    out=f3_o[b][:], in0=f3_t[b][:], in1=f3_x[b][:], op=ALU.add), inc="v")
                g_x[bi2] = tv_o
                g_ao[bi2] = tv_o
                W("y", "v", tv_o)
                g_o[bi2] = DMA("y", out_fm[dchunk * 128:(dchunk + 1) * 128,
                                           sbk * 512:(sbk + 1) * 512], f3_o[bi2][:],
                               f"so{bi2}")
            g_banks_v = em.cnt["v"]
            g_sbc_v = g_banks_v
        g_hq = em.cnt["t"]


def finish_all(nc, em, es):
    finals = {k: v for k, v in em.cnt.items() if k not in ("t", "v", "s", "g") and v > 0}
    for name in em.cnt:
        if name not in em.sems:
            em.sems[name] = es.enter_context(nc.semaphore(name))
    blk = es.enter_context(nc.Block())

    @blk.tensor
    def _(t):
        for f in em.ops["t"]:
            f(t)

    @blk.vector
    def _(v):
        for f in em.ops["v"]:
            f(v)

    @blk.scalar
    def _(s):
        for f in em.ops["s"]:
            f(s)

    @blk.gpsimd
    def _(g):
        for f in em.ops["g"]:
            f(g)

    @blk.sync
    def _(y):
        for f in em.ops["y"]:
            f(y)
        for name, v in finals.items():
            y.wait_ge(em.sems[name], v)

    es.close()


_CACHE = {}


def _get_nc():
    if "nc" not in _CACHE:
        _CACHE["nc"] = build_program()
    return _CACHE["nc"]


def kernel(L0_attn_x, L0_attn_W_Q, L0_attn_W_K, L0_attn_W_V, L0_attn_W_O,
           L0_ffn_gate_w, L0_ffn_gate_b, L0_ffn_val_w, L0_ffn_val_b,
           L0_ffn_out_w, L0_ffn_out_b):
    x = np.asarray(L0_attn_x, dtype=np.float32)
    wq = np.ascontiguousarray(np.asarray(L0_attn_W_Q, np.float32).transpose(0, 2, 1))
    wk = np.ascontiguousarray(np.asarray(L0_attn_W_K, np.float32).transpose(0, 2, 1))
    wv = np.ascontiguousarray(np.asarray(L0_attn_W_V, np.float32).transpose(0, 2, 1))
    wo = np.ascontiguousarray(np.asarray(L0_attn_W_O, np.float32).transpose(0, 2, 1))
    gw = np.ascontiguousarray(np.asarray(L0_ffn_gate_w, np.float32).T)
    vw = np.ascontiguousarray(np.asarray(L0_ffn_val_w, np.float32).T)
    ow = np.ascontiguousarray(np.asarray(L0_ffn_out_w, np.float32).T)
    gb = np.ascontiguousarray(np.asarray(L0_ffn_gate_b, np.float32).reshape(1, HID))
    vb = np.ascontiguousarray(np.asarray(L0_ffn_val_b, np.float32).reshape(1, HID))
    obc = np.ascontiguousarray(np.asarray(L0_ffn_out_b, np.float32).reshape(NCH, 128).T)
    iden_bf = np.eye(128, dtype=ml_dtypes.bfloat16)
    iden_f = np.eye(128, dtype=np.float32)

    in_maps = []
    for c in range(NCORES):
        b, r = c // 2, c % 2
        perm = np.arange(NT)
        if r == 1:
            perm = perm ^ 1
        xb = np.ascontiguousarray(
            x[b].reshape(NT, 128, D)[perm].reshape(S, D))
        own_tiles = [2 * t + r for t in range(NOT)]
        own_cols = np.concatenate([np.arange(g_ * 128, (g_ + 1) * 128) for g_ in own_tiles])
        xT = np.ascontiguousarray(x[b].T[:, own_cols])
        masks = np.zeros((8, 128, 512), dtype=ml_dtypes.bfloat16)
        kk = np.arange(128)[:, None]
        qq = np.arange(128)[None, :]
        for m in range(8):
            gk = (m + r) if m % 2 == 0 else (m - r)
            for g_ in range(4):
                masks[m][:, g_ * 128:(g_ + 1) * 128] = (
                    ((2 * g_ + r - gk) * 128 + qq) >= kk).astype(ml_dtypes.bfloat16)
        in_maps.append({
            "x_tm": xb, "xT_own": xT, "wqT": wq, "wkT": wk, "wvT": wv, "woT": wo,
            "gwT": gw, "vwT": vw, "owT": ow, "gb_in": gb, "vb_in": vb, "obc_in": obc,
            "iden_bf_in": iden_bf, "iden_f_in": iden_f, "masks_in": masks,
        })

    nc = _get_nc()
    res = run_bass_kernel_spmd(nc, in_maps, core_ids=list(range(NCORES)))

    out = np.empty((B, S, D), dtype=np.float32)
    for c in range(NCORES):
        b, r = c // 2, c % 2
        o = np.asarray(res.results[c]["out_fm"])
        oT = np.ascontiguousarray(o.T)
        for t in range(NOT):
            g_ = 2 * t + r
            out[b, g_ * 128:(g_ + 1) * 128, :] = oT[t * 128:(t + 1) * 128, :]
    return out


if __name__ == "__main__":
    nc = build_program()
    print("build ok")


# version 15 (final)
# speedup vs baseline: 1.0421x; 1.0421x over previous
import math
import numpy as np
import ml_dtypes
import concourse.bass as bass
import concourse.mybir as mybir
from concourse.bass_utils import run_bass_kernel_spmd
from contextlib import ExitStack

F32 = mybir.dt.float32
BF16 = mybir.dt.bfloat16
I32 = mybir.dt.int32
I8 = mybir.dt.int8
AF = mybir.ActivationFunctionType
ALU = mybir.AluOpType
AX = mybir.AxisListType

B, S, D = 4, 4096, 2048
H, E = 16, 128
HID = 8192
NCORES = 8
OWN = 2048
NT = S // 128
NOT = OWN // 128
NCH = D // 128
NHCH = HID // 128
NQB = 4
SCALE = 1.0 / math.sqrt(E)
EXP_BIAS = -12.0

PHASES = 6
DBG_SCRATCH = False


class Emitter:
    ENGS = ("t", "v", "s", "g", "y")

    def __init__(self):
        self.ops = {k: [] for k in self.ENGS}
        self.cnt = {}
        self.sems = {}

    def new(self, name):
        self.cnt[name] = 0

    def wait(self, eng, sem, val):
        if val and val > 0:
            self.ops[eng].append(lambda e, s=sem, v=val: e.wait_ge(self.sems[s], v))

    def op(self, eng, fn, inc=None, by=1):
        tgt = None
        if inc is not None:
            self.cnt[inc] += by
            tgt = self.cnt[inc]
            self.ops[eng].append(lambda e, f=fn, s=inc, b=by: f(e).then_inc(self.sems[s], b))
        else:
            self.ops[eng].append(lambda e, f=fn: f(e))
        return tgt

    def dma(self, eng, out, in_, sem):
        if sem not in self.cnt:
            self.cnt[sem] = 0
        self.cnt[sem] += 16
        tgt = self.cnt[sem]
        self.ops[eng].append(lambda e, o=out, i=in_, s=sem:
                             e.dma_start(out=o, in_=i).then_inc(self.sems[s], 16))
        return tgt

    def seq(self, eng, fn):
        t = self.op(eng, fn, inc=eng)
        self.wait(eng, eng, t)
        return t

    def barrier(self):
        snap = dict(self.cnt)
        for eng in self.ENGS:
            for s, v in snap.items():
                self.wait(eng, s, v)


def build_program():
    nc = bass.Bass("TRN2", target_bir_lowering=False)

    x_tm = nc.dram_tensor("x_tm", [S, D], F32, kind="ExternalInput").ap()
    xT_own = nc.dram_tensor("xT_own", [D, OWN], F32, kind="ExternalInput").ap()
    wqT = nc.dram_tensor("wqT", [H, D, E], F32, kind="ExternalInput").ap()
    wkT = nc.dram_tensor("wkT", [H, D, E], F32, kind="ExternalInput").ap()
    wvT = nc.dram_tensor("wvT", [H, D, E], F32, kind="ExternalInput").ap()
    woT = nc.dram_tensor("woT", [H, E, D], F32, kind="ExternalInput").ap()
    gwT = nc.dram_tensor("gwT", [D, HID], F32, kind="ExternalInput").ap()
    vwT = nc.dram_tensor("vwT", [D, HID], F32, kind="ExternalInput").ap()
    owT = nc.dram_tensor("owT", [HID, D], F32, kind="ExternalInput").ap()
    gb_in = nc.dram_tensor("gb_in", [1, HID], F32, kind="ExternalInput").ap()
    vb_in = nc.dram_tensor("vb_in", [1, HID], F32, kind="ExternalInput").ap()
    obc_in = nc.dram_tensor("obc_in", [128, NCH], F32, kind="ExternalInput").ap()
    iden_bf_in = nc.dram_tensor("iden_bf_in", [128, 128], BF16, kind="ExternalInput").ap()
    iden_f_in = nc.dram_tensor("iden_f_in", [128, 128], F32, kind="ExternalInput").ap()
    masks_in = nc.dram_tensor("masks_in", [8, 128, 512], BF16, kind="ExternalInput").ap()

    out_fm = nc.dram_tensor("out_fm", [D, OWN], F32, kind="ExternalOutput").ap()

    sk = "ExternalOutput" if DBG_SCRATCH else "Internal"
    xn_sc = nc.dram_tensor("xn_sc", [D, S], BF16, kind=sk).ap()
    xq_sc = nc.dram_tensor("xq_sc", [D, OWN], BF16, kind=sk).ap()
    z_sc = nc.dram_tensor("z_sc", [H * E, OWN], BF16, kind=sk).ap()
    ao_sc = nc.dram_tensor("ao_sc", [D, OWN], F32, kind=sk).ap()
    gw_t8 = nc.dram_tensor("gw_t8", [D, HID], I8, kind=sk).ap()
    vw_t8 = nc.dram_tensor("vw_t8", [D, HID], I8, kind=sk).ap()
    ow_t8 = nc.dram_tensor("ow_t8", [HID, D], I8, kind=sk).ap()
    gv_sc = nc.dram_tensor("gv_sc", [OWN, HID], F32, kind=sk).ap()
    hq_sc = nc.dram_tensor("hq_sc", [4, NHCH, 128, 512], I8, kind=sk).ap()
    row_sc = nc.dram_tensor("row_sc", [1, OWN], F32, kind="Internal").ap()
    xn_tm_dbg = nc.dram_tensor("xn_tm_dbg", [S, D], BF16, kind=sk).ap() if DBG_SCRATCH else None
    st_dbg = nc.dram_tensor("st_dbg", [NT, 128, 8], F32, kind=sk).ap() if DBG_SCRATCH else None

    em = Emitter()
    for s in ("t", "v", "s", "g"):
        em.new(s)

    es = ExitStack()
    e = es.enter_context

    def sb(name, cols, dtype, parts=128):
        return e(nc.sbuf_tensor(name, [parts, cols], dtype))

    idn_b = sb("idn_b", 128, BF16)
    idn_f = sb("idn_f", 128, F32)
    masks = sb("masks", 8 * 512, BF16)
    ones_b = sb("ones_b", 1, BF16)
    onesrow_f = sb("onesrow_f", 128, F32, parts=1)
    csc = sb("csc", 16, F32)
    ones_f = csc[:, 0:1]
    c_zero = csc[:, 1:2]
    c_one = csc[:, 2:3]
    c_mone = csc[:, 3:4]
    c_eps = csc[:, 4:5]
    c_geps = csc[:, 5:6]
    c_invD = csc[:, 6:7]
    c_inv127 = csc[:, 7:8]
    c_expb = csc[:, 8:9]
    rg_w = {"gw": csc[:, 9:10], "vw": csc[:, 10:11], "ow": csc[:, 11:12]}
    g127_w = {"gw": csc[:, 12:13], "vw": csc[:, 13:14], "ow": csc[:, 14:15]}
    obc_sb = sb("obc_sb", NCH, F32)
    gsc = sb("gsc", 6 * NOT + 2, F32)
    gx_all = gsc[:, 0 * NOT:1 * NOT]
    gxr127 = gsc[:, 1 * NOT:2 * NOT]
    gh_run = gsc[:, 2 * NOT:3 * NOT]
    sc_g = gsc[:, 3 * NOT:4 * NOT]
    sc_v = gsc[:, 4 * NOT:5 * NOT]
    sc_h127 = gsc[:, 5 * NOT:6 * NOT]
    f0tmp_a = gsc[:, 6 * NOT:6 * NOT + 1]
    f0tmp_b = gsc[:, 6 * NOT + 1:6 * NOT + 2]
    ghrow_t = sb("ghrow_t", 128, F32, parts=NOT)
    shrow = sb("shrow", OWN, F32, parts=1)
    wsum = sb("wsum", 384, F32)
    gtmp = sb("gtmp", 8, F32, parts=1)
    rca = sb("rca", 16, F32)
    rcb = sb("rcb", 16, F32)
    rc16 = sb("rc16", NOT, F32)

    ARENA = 192320
    AR = e(nc.sbuf_tensor("AR", [128, ARENA], mybir.dt.uint8))
    AR_ADDR = nc.lookup_mloc(AR).addr
    _cvn = [0]

    class Carver:

        def __init__(self, base, limit):
            self.off, self.limit = base, limit

        def take(self, cols, dtype, parts=128):
            nb = cols * mybir.dt.size(dtype)
            off = (self.off + 63) // 64 * 64
            self.off = off + nb
            assert self.off <= self.limit, (off, nb, self.limit)
            _cvn[0] += 1
            h = nc.alloc_sbuf_tensor_at(f"cv{_cvn[0]}", [parts, cols], dtype,
                                        offset=AR_ADDR + off)
            return h.ap()

    F0_SZ = 22528
    NF0 = 3
    cf0 = Carver(0, F0_SZ)
    FTW = 1024
    f0_in = [cf0.take(FTW, F32) for i in range(NF0)]
    f0_t1 = [cf0.take(FTW, F32) for i in range(1)]
    f0_ab = cf0.take(FTW, BF16)
    f0_i8 = [cf0.take(FTW, I8) for i in range(NF0)]

    c1 = Carver(F0_SZ, ARENA)
    p1_x = [c1.take(D, F32) for i in range(2)]
    p1_sq = [c1.take(D, F32) for i in range(2)]
    p1_nf = [c1.take(D, F32) for i in range(2)]
    p1_i32 = c1.take(D, I32)
    p1_qb = c1.take(D, F32)
    p1_st = [c1.take(NCH * 128, BF16) for i in range(2)]
    p1_stq = c1.take(NCH * 128, BF16)
    p1_m = [c1.take(1, F32) for i in range(2)]
    p1_m2 = [c1.take(1, F32) for i in range(2)]
    p1_v = [c1.take(1, F32) for i in range(2)]
    p1_sd = [c1.take(1, F32) for i in range(2)]
    p1_rs = [c1.take(1, F32) for i in range(2)]
    p1_g0 = c1.take(1, F32)

    c2 = Carver(F0_SZ, ARENA)
    p2_xn = [c2.take(NCH * 256, BF16) for i in range(2)]
    p2_K = [c2.take(S, BF16) for h in range(4)]
    p2_Q = [c2.take(OWN, BF16) for h in range(4)]
    p2_V = c2.take(NT * 512, BF16)
    p2_wk = [c2.take(NCH * 128, BF16) for h in range(4)]
    p2_wq = [c2.take(NCH * 128, BF16) for h in range(4)]
    p2_wv = c2.take(NCH * 512, BF16)
    p2_P = [c2.take(1024, BF16) for i in range(2)]
    p2_z = [c2.take(512, BF16) for i in range(2)]
    p2_rd = [c2.take(512, F32, parts=1) for i in range(2)]
    p2_dn = [c2.take(512, F32, parts=1) for i in range(2)]
    p2_rb = [c2.take(512, F32) for i in range(2)]

    c3 = Carver(0, ARENA)
    f1_wo = c3.take(H * 1024, BF16)
    f1_z = c3.take(H * 512, BF16)
    f1_ao = [c3.take(512, F32) for i in range(2)]

    c4 = Carver(0, ARENA)
    f2_xq = c4.take(NCH * OWN, BF16)
    f2_gw = [c4.take(NCH * 512, BF16) for i in range(2)]
    f2_vw = [c4.take(NCH * 512, BF16) for i in range(2)]
    f2_gbb = [c4.take(512, F32) for i in range(2)]
    f2_vbb = [c4.take(512, F32) for i in range(2)]
    f2_g1 = [c4.take(512, F32) for i in range(2)]
    f2_g2 = [c4.take(512, F32) for i in range(2)]
    f2_gs = [c4.take(512, F32) for i in range(2)]
    f2_v1 = [c4.take(512, F32) for i in range(2)]
    f2_v2 = [c4.take(512, F32) for i in range(2)]
    f2_gvb = [c4.take(512, F32) for i in range(2)]
    f2_am = c4.take(1, F32)
    f2_gbr = [c4.take(512, F32, parts=1) for i in range(2)]
    f2_vbr = [c4.take(512, F32, parts=1) for i in range(2)]

    c5 = Carver(0, ARENA)
    f3_gv = [c5.take(HID, F32) for i in range(1)]
    f3_i32 = c5.take(HID, I32)
    f3_hb = c5.take(HID, F32)
    f3_stg = c5.take(NHCH * 512, I8)

    c6 = Carver(0, ARENA)
    f3_hq = c6.take(NHCH * 512, BF16)
    f3_ow = [c6.take(8 * 1024, BF16) for i in range(2)]
    f3_x = [c6.take(512, F32) for i in range(2)]
    f3_ao = [c6.take(512, F32) for i in range(2)]
    f3_t = [c6.take(512, F32) for i in range(2)]
    f3_o = [c6.take(512, F32) for i in range(2)]
    f3_sbc = c6.take(512, F32)

    ps = e(nc.psum_tensor("ps", [128, 4096], F32))

    def bank(i):
        return ps[:, 512 * i:512 * (i + 1)]

    def bankpair(i):
        return ps[:, 512 * i:512 * (i + 2)]

    def bank_bf(i):
        return ps[:, 512 * i:512 * (i + 1)].bitcast(BF16)

    em.alloc_sems = lambda: None

    O, W, DMA = em.op, em.wait, em.dma
    SQ = em.seq

    DMA("y", idn_b[:], iden_bf_in[:], "lset")
    DMA("y", idn_f[:], iden_f_in[:], "lset")
    DMA("y", masks[:].rearrange("k (m q) -> k m q", m=8),
        masks_in[:].rearrange("m k q -> k m q"), "lset")
    t_setup = DMA("y", obc_sb[:], obc_in[:], "lset")
    for tile, val in ((ones_b, 1.0), (ones_f, 1.0), (onesrow_f, 1.0), (c_zero, 0.0),
                      (c_one, 1.0), (c_mone, -1.0), (c_eps, 1e-5), (c_geps, 1e-5),
                      (c_invD, 1.0 / D), (c_inv127, 1.0 / 127.0), (c_expb, EXP_BIAS)):
        O("v", lambda v, tl=tile, vv=val: v.memset(tl[:], vv))
    O("v", lambda v: v.memset(gh_run[:], 0.0))
    t_msets = SQ("v", lambda v: v.memset(wsum[:], 0.0))
    for _e in ("t", "s", "g", "y"):
        W(_e, "v", t_msets)
    W("t", "lset", t_setup)
    W("s", "lset", t_setup)

    f0_tiles = []
    for m, ap_ in (("gw", gwT), ("vw", vwT), ("ow", owT)):
        nr = NCH if m != "ow" else NHCH
        ncols = HID if m != "ow" else D
        for ct in range(ncols // FTW):
            for rt in range(nr):
                f0_tiles.append((m, ap_, rt, ct, FTW))
    assert len(f0_tiles) == 384
    f0 = {"i": 0, "guard": [("s", 0)] * NF0}

    def emit_f0(n):
        for _ in range(n):
            i = f0["i"]
            if i >= len(f0_tiles):
                return
            f0["i"] += 1
            m, ap_, rt, ct, tw = f0_tiles[i]
            buf = f0_in[i % NF0]
            eng = "s" if i % 2 == 0 else "v"
            ge, gc = f0["guard"][i % NF0]
            W("y", ge, gc)
            td = DMA("y", buf[:, 0:tw], ap_[rt * 128:(rt + 1) * 128, ct * tw:(ct + 1) * tw],
                     f"lf{i % NF0}")
            W(eng, f"lf{i % NF0}", td)
            if eng == "s":
                W("s", "s", em.cnt["s"])
                f0["guard"][i % NF0] = ("s", O("s", lambda s_, b=buf, tw_=tw, col=i:
                                               s_.activation(
                    out=f0_ab[:, 0:tw_], in_=b[:, 0:tw_], func=AF.Abs, bias=c_zero[:],
                    scale=1.0, accum_out=wsum[:, col:col + 1]), inc="s"))
            else:
                f0["guard"][i % NF0] = ("v", O("v", lambda v, b=buf, tw_=tw, col=i:
                                               v.tensor_reduce(
                    out=wsum[:, col:col + 1], in_=b[:, 0:tw_], axis=AX.X, op=ALU.add,
                    apply_absolute_value=True), inc="v"))

    def finish_f0():
        emit_f0(len(f0_tiles))
        W("v", "s", em.cnt["s"])
        W("v", "v", em.cnt["v"])
        spans = {"gw": [], "vw": [], "ow": []}
        for i, (m, *_r) in enumerate(f0_tiles):
            spans[m].append(i)
        for mi, m in enumerate(("gw", "vw", "ow")):
            lo, hi = min(spans[m]), max(spans[m]) + 1
            em.seq("v", lambda v, lo=lo, hi=hi: v.tensor_reduce(
                out=f0tmp_a[:], in_=wsum[:, lo:hi], axis=AX.X, op=ALU.add))
            tv = O("v", lambda v: v.tensor_copy(out=f0tmp_b[:], in_=f0tmp_a[:]), inc="v")
            W("t", "v", tv)
            tt = O("t", lambda t, b=mi: t.matmul(out=bank(7)[0:1, b:b + 1], lhsT=ones_f[:],
                                                 rhs=f0tmp_b[:], start=True, stop=True,
                                                 skip_group_check=True), inc="t")
            W("s", "t", tt)
            ts1 = O("s", lambda s_, b=mi: s_.activation(
                out=gtmp[0:1, 2 * b:2 * b + 1], in_=bank(7)[0:1, b:b + 1], func=AF.Identity,
                bias=c_zero[0:1, :], scale=1.0 / (D * HID)), inc="s")
            W("v", "s", ts1)
            em.seq("v", lambda v, b=mi: v.tensor_scalar(
                out=rca[0:1, 0:1], in0=gtmp[0:1, 2 * b:2 * b + 1],
                scalar1=c_geps[0:1, :], scalar2=None, op0=ALU.max))
            em.seq("v", lambda v: v.reciprocal(out=rcb[0:1, 0:1], in_=rca[0:1, 0:1]))
            em.seq("v", lambda v, b=mi: v.tensor_copy(
                out=gtmp[0:1, 2 * b + 1:2 * b + 2], in_=rcb[0:1, 0:1]))
            tv2 = O("v", lambda v, b=mi: v.tensor_scalar(
                out=gtmp[0:1, 2 * b:2 * b + 1], in0=rca[0:1, 0:1],
                scalar1=c_inv127[0:1, :], scalar2=None, op0=ALU.mult), inc="v")
            W("t", "v", tv2)
            tt2 = O("t", lambda t, b=mi: t.matmul(
                out=bank(7)[:, 8 + 2 * b:8 + 2 * b + 2], lhsT=onesrow_f[:],
                rhs=gtmp[0:1, 2 * b:2 * b + 2], start=True, stop=True,
                skip_group_check=True), inc="t")
            W("s", "t", tt2)
            O("s", lambda s_, m_=m, b=mi: s_.copy(out=g127_w[m_][:],
                                                  in_=bank(7)[:, 8 + 2 * b:8 + 2 * b + 1]))
            O("s", lambda s_, m_=m, b=mi: s_.copy(out=rg_w[m_][:],
                                                  in_=bank(7)[:, 8 + 2 * b + 1:8 + 2 * b + 2]),
              inc="s")
        return em.cnt["s"]

    f0b_tiles = []
    for m, ap_, o8 in (("gw", gwT, gw_t8), ("vw", vwT, vw_t8), ("ow", owT, ow_t8)):
        nr = NCH if m != "ow" else NHCH
        ncols = HID if m != "ow" else D
        for ct in range(ncols // FTW):
            for rt in range(nr):
                f0b_tiles.append((m, ap_, o8, rt, ct, FTW))
    assert len(f0b_tiles) == 384
    f0b = {"i": 0, "gv": [0] * NF0, "gd": [0] * NF0, "gamma": 0, "f0sg": [0] * NF0,
           "t1g": [0, 0]}

    def emit_f0b(n):
        for _ in range(n):
            i = f0b["i"]
            if i >= len(f0b_tiles):
                return
            f0b["i"] += 1
            m, ap_, o8, rt, ct, tw = f0b_tiles[i]
            buf, ob = f0_in[i % NF0], f0_i8[i % NF0]
            if i < NF0:
                ge, gc = f0["guard"][i % NF0]
                W("y", ge, gc)
            W("y", "v", f0b["gv"][i % NF0])
            td = DMA("y", buf[:, 0:tw], ap_[rt * 128:(rt + 1) * 128, ct * tw:(ct + 1) * tw],
                     f"lf{i % NF0}")
            W("v", f"lf{i % NF0}", td)
            W("v", "s", f0b["gamma"])
            W("v", f"sf{i % NF0}", f0b["gd"][i % NF0])
            tv = O("v", lambda v, b=buf, ob_=ob, tw_=tw, m_=m: v.tensor_scalar(
                out=ob_[:, 0:tw_], in0=b[:, 0:tw_], scalar1=rg_w[m_][:],
                scalar2=None, op0=ALU.mult), inc="v")
            f0b["gv"][i % NF0] = tv
            W("y", "v", tv)
            f0b["gd"][i % NF0] = DMA(
                "y", o8[rt * 128:(rt + 1) * 128, ct * tw:(ct + 1) * tw], ob[:, 0:tw],
                f"sf{i % NF0}")

    g_xbuf = [0, 0]
    g_nb = [0, 0]
    g_qb = 0
    g_bank = [0, 0]
    g_stdma = [0, 0]
    g_stqdma = 0
    g_sv = [0, 0]
    own_idx = 0
    t_p1_ld = 0
    for t in range(NT):
        bi = t % 2
        xbuf, nbuf, stg = p1_x[bi], p1_nf[bi], p1_st[bi]
        is_own = (t % 2 == 0)
        W("y", "v", g_xbuf[bi])
        td = DMA("y", xbuf[:], x_tm[t * 128:(t + 1) * 128, :], f"lx{bi}")
        W("s", f"lx{bi}", td)
        W("s", "v", g_sv[bi])
        ts_sq = O("s", lambda s_, xb=xbuf, b=bi: s_.activation(
            out=p1_sq[b][:], in_=xb[:], func=AF.Square, bias=c_zero[:], scale=1.0), inc="s")
        W("v", f"lx{bi}", td)
        W("v", "s", ts_sq)
        O("v", lambda v, xb=xbuf, b=bi: v.tensor_reduce(out=p1_m[b][:], in_=xb[:],
                                                        axis=AX.X, op=ALU.add))
        O("v", lambda v, b=bi: v.tensor_reduce(out=p1_m2[b][:], in_=p1_sq[b][:],
                                               axis=AX.X, op=ALU.add))
        O("v", lambda v, b=bi: v.tensor_scalar(out=p1_m[b][:], in0=p1_m[b][:],
                                               scalar1=c_invD[:], scalar2=None, op0=ALU.mult))
        SQ("v", lambda v, b=bi: v.tensor_scalar(out=p1_m2[b][:], in0=p1_m2[b][:],
                                                scalar1=c_invD[:], scalar2=None, op0=ALU.mult))
        tv_var = O("v", lambda v, b=bi: v.scalar_tensor_tensor(out=p1_v[b][:], in0=p1_m[b][:],
                                                               scalar=p1_m[b][:], in1=p1_m2[b][:],
                                                               op0=ALU.mult, op1=ALU.subtract),
                   inc="v")
        W("s", "v", tv_var)
        ts_sd = O("s", lambda s_, b=bi: s_.activation(out=p1_sd[b][:], in_=p1_v[b][:],
                                                      func=AF.Sqrt,
                                                      bias=c_eps[:], scale=-1.0), inc="s")
        W("v", "s", ts_sd)
        g_sv[bi] = SQ("v", lambda v, b=bi: v.reciprocal(out=p1_rs[b][:], in_=p1_sd[b][:]))
        W("v", "t", g_nb[bi])
        tv_nb = O("v", lambda v, xb=xbuf, nb=nbuf, b=bi: v.tensor_scalar(
            out=nb[:], in0=xb[:], scalar1=p1_m[b][:], scalar2=p1_rs[b][:],
            op0=ALU.subtract, op1=ALU.mult), inc="v")
        tv_done = tv_nb
        if is_own:
            oi = own_idx
            O("v", lambda v, nb=nbuf: v.tensor_reduce(out=p1_g0[:], in_=nb[:], axis=AX.X,
                                                      op=ALU.max, apply_absolute_value=True))
            O("v", lambda v, o=oi: v.tensor_scalar(out=gx_all[:, o:o + 1], in0=p1_g0[:],
                                                   scalar1=c_geps[:], scalar2=None, op0=ALU.max))
            O("v", lambda v: v.tensor_scalar(out=p1_g0[:], in0=p1_g0[:],
                                             scalar1=c_geps[:], scalar2=None, op0=ALU.max))
            SQ("v", lambda v: v.tensor_scalar(out=p1_g0[:], in0=p1_g0[:],
                                              scalar1=c_inv127[:], scalar2=None,
                                              op0=ALU.mult))
            SQ("v", lambda v: v.reciprocal(out=rca[:, 0:1], in_=p1_g0[:]))
            O("v", lambda v, o=oi: v.tensor_copy(out=gxr127[:, o:o + 1], in_=rca[:, 0:1]))
            SQ("v", lambda v, nb=nbuf: v.tensor_scalar(out=p1_i32[:], in0=nb[:],
                                                       scalar1=rca[:, 0:1],
                                                       scalar2=None, op0=ALU.mult))
            W("v", "t", g_qb)
            tv_done = O("v", lambda v: v.tensor_copy(out=p1_qb[:], in_=p1_i32[:]), inc="v")
        g_xbuf[bi] = tv_done
        if DBG_SCRATCH:
            W("y", "v", tv_done)
            DMA("y", xn_tm_dbg[t * 128:(t + 1) * 128, :], nbuf[:], "dbg")
            for di, (dst, src_) in enumerate([(st_dbg[t, :, 0:1], p1_m), (st_dbg[t, :, 1:2], p1_m2),
                                              (st_dbg[t, :, 2:3], p1_v), (st_dbg[t, :, 3:4], p1_sd),
                                              (st_dbg[t, :, 4:5], p1_rs)]):
                if "dbg" not in em.cnt:
                    em.cnt["dbg"] = 0
                em.cnt["dbg"] += 16
                em.ops["y"].append(lambda e, o=dst, i=src_, s="dbg": e.dma_start(
                    out=o, in_=i[:], allow_slow_non_contiguous=True).then_inc(em.sems[s], 16))

        srcs = [(nbuf, 0, stg, 0), (nbuf, 8, stg, 1)]
        if is_own:
            srcs += [(p1_qb, 0, p1_stq, 0), (p1_qb, 8, p1_stq, 1)]
        W("t", "v", tv_done)
        for ci, (srcbuf, c0, dstg, half) in enumerate(srcs):
            bnk = ci % 2
            W("t", "v", g_bank[bnk])
            tt = None
            for k in range(8):
                tt = O("t", lambda t_, sb_=srcbuf, c=c0, k_=k, b=bnk: t_.transpose(
                    out=bankpair(2 * b)[:, k_ * 128:(k_ + 1) * 128],
                    in_=sb_[:, (c + k_) * 128:(c + k_ + 1) * 128],
                    identity=idn_f[:]), inc="t" if k == 7 else None)
            if ci == 1:
                g_nb[bi] = tt
            if ci == 3:
                g_qb = tt
            W("v", "t", tt)
            W("v", "sstq" if dstg is p1_stq else f"sst{bi}",
              g_stqdma if dstg is p1_stq else g_stdma[bi])
            tv_ev = O("v", lambda v, b=bnk, h=half, d=dstg: v.tensor_copy(
                out=d[:, h * 1024:(h + 1) * 1024], in_=bankpair(2 * b)[:]), inc="v")
            g_bank[bnk] = tv_ev
            if ci == 1:
                W("y", "v", tv_ev)
                g_stdma[bi] = DMA("y", xn_sc[:, t * 128:(t + 1) * 128].rearrange(
                    "(c dd) s -> dd c s", dd=128),
                    stg[:].rearrange("dd (c s) -> dd c s", s=128), f"sst{bi}")
                t_p1_ld = g_stdma[bi]
            if ci == 3:
                W("y", "v", tv_ev)
                g_stqdma = DMA("y", xq_sc[:, own_idx * 128:(own_idx + 1) * 128].rearrange(
                    "(c dd) s -> dd c s", dd=128),
                    p1_stq[:].rearrange("dd (c s) -> dd c s", s=128), "sstq")
                t_p1_ld = g_stqdma
        if is_own:
            own_idx += 1
        emit_f0(12)
    assert own_idx == NOT

    L = dict(locals())
    if PHASES >= 2:
        emit_p2(nc, em, L)
    if PHASES >= 3:
        emit_f1z(nc, em, L)
    if PHASES >= 4:
        emit_f2(nc, em, L)
    if PHASES >= 5:
        emit_f3a(nc, em, L)
    if PHASES >= 6:
        emit_f3b(nc, em, L)

    finish_all(nc, em, es)
    return nc


def emit_p2(nc, em, L):
    O, W, DMA = em.op, em.wait, em.dma
    g = L
    bank, bankpair, bank_bf = g["bank"], g["bankpair"], g["bank_bf"]
    p2_xn, p2_K, p2_Q, p2_V = g["p2_xn"], g["p2_K"], g["p2_Q"], g["p2_V"]
    p2_wk, p2_wq, p2_wv = g["p2_wk"], g["p2_wq"], g["p2_wv"]
    p2_P, p2_z, p2_rd, p2_rb = g["p2_P"], g["p2_z"], g["p2_rd"], g["p2_rb"]
    p2_dn = g["p2_dn"]
    masks, ones_b, onesrow_f = g["masks"], g["ones_b"], g["onesrow_f"]
    c_expb = g["c_expb"]
    wkT, wqT, wvT = g["wkT"], g["wqT"], g["wvT"]
    xn_sc, z_sc = g["xn_sc"], g["z_sc"]
    emit_f0, finish_f0, emit_f0b = g["emit_f0"], g["finish_f0"], g["emit_f0b"]
    f0b = g["f0b"]
    t_p1_ld = g["t_p1_ld"]
    t_setup = g["t_setup"]

    st = {
        "xn": [0, 0],
        "wkq": [0, 0, 0, 0],
        "wv": 0,
        "kqv_t": 0,
        "pb67": [0, 0, 0, 0],
        "P_s": [0, 0],
        "P_t": [0, 0],
        "sc_s": [0, 0],
        "z_t": [0, 0],
        "d_s": 0,
        "rb_s": [0, 0],
        "rb_v": [0, 0],
        "z_ld": [0, 0],
        "bc_s": 0,
    }
    W("v", "lset", t_setup)
    em.barrier()
    f0b["gamma"] = finish_f0()

    for grp in range(4):
        W("s", "t", st["kqv_t"])
        for hl in range(4):
            W("g", "t", st["wkq"][hl])
            DMA("g", p2_wk[hl][:].rearrange("dd (c e) -> dd c e", e=128),
                wkT[4 * grp + hl, :, :].rearrange("(c dd) e -> dd c e", dd=128), "lw")
        for hl in range(4):
            DMA("g", p2_wq[hl][:].rearrange("dd (c e) -> dd c e", e=128),
                wqT[4 * grp + hl, :, :].rearrange("(c dd) e -> dd c e", dd=128), "lw")

        W("g", "t", st["wv"])
        for hl in range(4):
            tw_v = DMA("g",
                       p2_wv[:].rearrange("dd (c h e) -> dd c h e", h=4, e=128)[:, :, hl, :],
                       wvT[4 * grp + hl, :, :].rearrange("(c dd) e -> dd c e", dd=128), "lw")

        W("t", "v", st["z_t"][0])
        W("t", "v", st["z_t"][1])
        W("t", "s", st["bc_s"])
        W("t", "s", st["d_s"])
        ecnt = 0
        NSLAB = S // 256
        for sbk in range(NSLAB):
            bi = sbk % 2
            W("y", "t", st["xn"][bi])
            txn = DMA("y", p2_xn[bi][:].rearrange("dd (c s) -> dd c s", s=256),
                xn_sc[:, sbk * 256:(sbk + 1) * 256].rearrange(
                "(c dd) s -> dd c s", dd=128), f"lxn{bi}")
            W("t", f"lxn{bi}", txn)
            if sbk == 0:
                W("t", "lw", tw_v)
            for hl in range(4):
                bnk = 4 + (ecnt % 4)
                W("t", "s", st["pb67"][ecnt % 4])
                for c in range(NCH):
                    ttk = O("t", lambda t_, hl_=hl, c_=c, b=bi, bk=bnk: t_.matmul(
                        out=bank(bk)[:, 0:256],
                        lhsT=p2_wk[hl_][:, c_ * 128:(c_ + 1) * 128],
                        rhs=p2_xn[b][:, c_ * 256:(c_ + 1) * 256],
                        start=(c_ == 0), stop=(c_ == NCH - 1), skip_group_check=True),
                        inc="t" if c == NCH - 1 else None)
                W("s", "t", ttk)
                st["pb67"][ecnt % 4] = O("s", lambda s_, hl_=hl, sb_=sbk, bk=bnk: s_.copy(
                    out=p2_K[hl_][:, sb_ * 256:(sb_ + 1) * 256], in_=bank(bk)[:, 0:256]),
                    inc="s")
                ecnt += 1
                bnk = 4 + (ecnt % 4)
                W("t", "s", st["pb67"][ecnt % 4])
                for c in range(NCH):
                    ttq = O("t", lambda t_, hl_=hl, c_=c, b=bi, bk=bnk: t_.matmul(
                        out=bank(bk)[:, 0:128],
                        lhsT=p2_wq[hl_][:, c_ * 128:(c_ + 1) * 128],
                        rhs=p2_xn[b][:, c_ * 256:c_ * 256 + 128],
                        start=(c_ == 0), stop=(c_ == NCH - 1), skip_group_check=True),
                        inc="t" if c == NCH - 1 else None)
                W("s", "t", ttq)
                st["pb67"][ecnt % 4] = O("s", lambda s_, hl_=hl, sb_=sbk, bk=bnk: s_.copy(
                    out=p2_Q[hl_][:, sb_ * 128:(sb_ + 1) * 128], in_=bank(bk)[:, 0:128]),
                    inc="s")
                ecnt += 1
                if sbk == NSLAB - 1:
                    st["wkq"][hl] = em.cnt["t"]
            for st4 in range(2):
                kc = sbk * 2 + st4
                bnk = 4 + (ecnt % 4)
                W("t", "s", st["pb67"][ecnt % 4])
                for c in range(NCH):
                    ttv = O("t", lambda t_, s4=st4, c_=c, b=bi, bk=bnk: t_.matmul(
                        out=bank(bk),
                        lhsT=p2_xn[b][:, c_ * 256 + s4 * 128: c_ * 256 + (s4 + 1) * 128],
                        rhs=p2_wv[:, c_ * 512:(c_ + 1) * 512],
                        start=(c_ == 0), stop=(c_ == NCH - 1), skip_group_check=True),
                        inc="t" if c == NCH - 1 else None)
                W("s", "t", ttv)
                st["pb67"][ecnt % 4] = O("s", lambda s_, kc_=kc, bk=bnk: s_.copy(
                    out=p2_V[:, kc_ * 512:(kc_ + 1) * 512], in_=bank(bk)), inc="s")
                ecnt += 1
            st["xn"][bi] = em.cnt["t"]
        st["wv"] = em.cnt["t"]
        t_proj_s = em.cnt["s"]
        W("t", "s", t_proj_s)

        for _i in range(4):
            W("t", "s", st["pb67"][_i])
        pending = [None]

        def flush_pending():
            if pending[0] is not None:
                pending[0]()
                pending[0] = None

        for hl in range(4):
            h = 4 * grp + hl
            for j in range(NQB):
                nch_j = 8 * (j + 1)
                npair = nch_j // 2
                ui = h * NQB + j
                zb = ui % 2
                zbank = 4 if zb == 0 else 7
                W("t", "v", st["z_t"][zb])
                W("t", "s", st["d_s"])

                def sc_pair(mp):
                    pr = mp % 2
                    W("t", "s", st["sc_s"][pr])
                    t2 = None
                    for mm, half in ((2 * mp, 0), (2 * mp + 1, 1)):
                        t2 = O("t", lambda t_, hl_=hl, j_=j, m_=mm, pr_=pr, hf=half: t_.matmul(
                            out=bank(2 * pr_ + hf),
                            lhsT=p2_K[hl_][:, m_ * 128:(m_ + 1) * 128],
                            rhs=p2_Q[hl_][:, j_ * 512:(j_ + 1) * 512],
                            start=True, stop=True, skip_group_check=True),
                            inc="t" if half == 1 else None)
                    return t2

                tsc = [None] * npair
                tsc[0] = sc_pair(0)
                flush_pending()
                tt_pv = 0
                for mp in range(npair):
                    pr = mp % 2
                    W("s", "t", tsc[mp])
                    W("s", "t", st["P_t"][pr])
                    ts_e = O("s", lambda s_, pr_=pr: s_.activation(
                        out=p2_P[pr_][:], in_=bankpair(2 * pr_), func=AF.Exp,
                        bias=c_expb[:], scale=SCALE), inc="s")
                    st["P_s"][pr] = ts_e
                    st["sc_s"][pr] = ts_e
                    if mp + 1 < npair:
                        tsc[mp + 1] = sc_pair(mp + 1)
                    tvm = [None, None]
                    for mm, half in ((2 * mp, 0), (2 * mp + 1, 1)):
                        mprime = mm - 8 * j
                        if mprime >= 0:
                            wgrp = min(mprime // 2 + 1, 4)
                            W("v", "s", ts_e)
                            tvm[half] = O("v", lambda v, pr_=pr, hf=half, mp_=mprime, w=wgrp:
                                          v.tensor_tensor(
                                out=p2_P[pr_][:, hf * 512: hf * 512 + w * 128],
                                in0=p2_P[pr_][:, hf * 512: hf * 512 + w * 128],
                                in1=masks[:, mp_ * 512: mp_ * 512 + w * 128],
                                op=ALU.mult), inc="v")
                    for mm, half in ((2 * mp, 0), (2 * mp + 1, 1)):
                        if tvm[half] is not None:
                            W("t", "v", tvm[half])
                        else:
                            W("t", "s", ts_e)
                        O("t", lambda t_, hl_=hl, m_=mm, pr_=pr, hf=half,
                          s0=(mm == 0), s1=(mm == nch_j - 1), zk=zbank: t_.matmul(
                            out=bank(zk),
                            lhsT=p2_V[:, m_ * 512 + hl_ * 128: m_ * 512 + (hl_ + 1) * 128],
                            rhs=p2_P[pr_][:, hf * 512:(hf + 1) * 512],
                            start=s0, stop=s1, skip_group_check=True))
                        tt_pv = O("t", lambda t_, m_=mm, pr_=pr, hf=half,
                                  s0=(mm == 0), s1=(mm == nch_j - 1): t_.matmul(
                            out=bank(5)[0:1, :], lhsT=ones_b[:],
                            rhs=p2_P[pr_][:, hf * 512:(hf + 1) * 512],
                            start=s0, stop=s1, skip_group_check=True), inc="t")
                    st["P_t"][pr] = tt_pv
                    emit_f0b(1)
                W("s", "t", tt_pv)
                ts_dn = O("s", lambda s_, u=ui % 2: s_.copy(out=p2_dn[u][:],
                                                            in_=bank(5)[0:1, :]), inc="s")
                st["d_s"] = ts_dn

                def mk_norm(h=h, j=j, zb=zb, zbank=zbank, ui=ui, ts_dn=ts_dn):
                    def emit_norm():
                        u2 = ui % 2
                        W("v", "s", ts_dn)
                        tv_rd = O("v", lambda v: v.reciprocal(out=p2_rd[u2][:],
                                                              in_=p2_dn[u2][:]), inc="v")
                        W("t", "v", tv_rd)
                        W("t", "s", st["bc_s"])
                        tt_bc = O("t", lambda t_: t_.matmul(out=bank(6), lhsT=onesrow_f[:],
                                                            rhs=p2_rd[u2][:], start=True,
                                                            stop=True,
                                                            skip_group_check=True), inc="t")
                        W("s", "t", tt_bc)
                        W("s", "v", st["rb_v"][u2])
                        ts_rb = O("s", lambda s_: s_.copy(out=p2_rb[u2][:], in_=bank(6)),
                                  inc="s")
                        st["bc_s"] = ts_rb
                        W("v", "s", ts_rb)
                        W("v", f"sz{zb}", st["z_ld"][zb])
                        tv_z = O("v", lambda v: v.tensor_tensor(
                            out=p2_z[zb][:], in0=bank(zbank), in1=p2_rb[u2][:],
                            op=ALU.mult), inc="v")
                        st["rb_v"][u2] = tv_z
                        st["z_t"][zb] = tv_z
                        W("y", "v", tv_z)
                        st["z_ld"][zb] = DMA("y", z_sc[h * 128:(h + 1) * 128,
                                                       j * 512:(j + 1) * 512],
                                             p2_z[zb][:], f"sz{zb}")
                    return emit_norm
                pending[0] = mk_norm()
        flush_pending()
        st["kqv_t"] = em.cnt["t"]
    emit_f0b(384)
    em.barrier()


def emit_f1z(nc, em, L):
    O, W, DMA = em.op, em.wait, em.dma
    bank = L["bank"]
    f1_wo, f1_z, f1_ao = L["f1_wo"], L["f1_z"], L["f1_ao"]
    woT, z_sc, ao_sc, obc_sb = L["woT"], L["z_sc"], L["ao_sc"], L["obc_sb"]

    g_wo = 0
    g_z = 0
    g_ao_s = [0, 0]
    g_ao_ld = [0, 0]
    for dh in range(2):
        W("g", "t", g_wo)
        two = DMA("g", f1_wo[:].rearrange("ee (h d) -> ee h d", h=H),
            woT[:, :, dh * 1024:(dh + 1) * 1024].rearrange(
            "h ee d -> ee h d"), "lwo")
        for sbk in range(4):
            W("y", "t", g_z)
            tz = DMA("y", f1_z[:].rearrange("ee (h s) -> ee h s", h=H),
                z_sc[:, sbk * 512:(sbk + 1) * 512].rearrange(
                "(h ee) s -> ee h s", ee=128), "lz")
            for dt in range(8):
                bnk = dt % 2
                W("t", "s", g_ao_s[bnk])
                if dt == 0:
                    W("t", "lwo", two)
                    W("t", "lz", tz)
                tt = None
                for h in range(H):
                    tt = O("t", lambda t_, h_=h, dt_=dt, b=bnk: t_.matmul(
                        out=bank(b),
                        lhsT=f1_wo[:, h_ * 1024 + dt_ * 128: h_ * 1024 + (dt_ + 1) * 128],
                        rhs=f1_z[:, h_ * 512:(h_ + 1) * 512],
                        start=(h_ == 0), stop=(h_ == H - 1), skip_group_check=True),
                        inc="t" if h == H - 1 else None)
                W("s", "t", tt)
                W("s", f"sao{bnk}", g_ao_ld[bnk])
                dchunk = dh * 8 + dt
                g_ao_s[bnk] = O("s", lambda s_, b=bnk, dc=dchunk: s_.activation(
                    out=f1_ao[b][:], in_=bank(b), func=AF.Identity,
                    bias=obc_sb[:, dc:dc + 1], scale=1.0), inc="s")
                W("y", "s", g_ao_s[bnk])
                g_ao_ld[bnk] = DMA("y", ao_sc[dchunk * 128:(dchunk + 1) * 128,
                                              sbk * 512:(sbk + 1) * 512], f1_ao[bnk][:],
                                   f"sao{bnk}")
            g_z = em.cnt["t"]
        g_wo = em.cnt["t"]
    em.barrier()


def emit_f2(nc, em, L):
    O, W, DMA = em.op, em.wait, em.dma
    bank = L["bank"]
    f2_xq, f2_gw, f2_vw = L["f2_xq"], L["f2_gw"], L["f2_vw"]
    f2_gbb, f2_vbb = L["f2_gbb"], L["f2_vbb"]
    f2_g1, f2_g2, f2_gs, f2_v1, f2_v2 = L["f2_g1"], L["f2_g2"], L["f2_gs"], L["f2_v1"], L["f2_v2"]
    f2_gvb, f2_am = L["f2_gvb"], L["f2_am"]
    f2_gbr, f2_vbr = L["f2_gbr"], L["f2_vbr"]
    onesrow_f = L["onesrow_f"]
    sc_g, sc_v, gx_all, gh_run = L["sc_g"], L["sc_v"], L["gx_all"], L["gh_run"]
    g127_w, c_zero = L["g127_w"], L["c_zero"]
    c_one, c_mone = L["c_one"], L["c_mone"]
    gw_t8, vw_t8, xq_sc, gv_sc = L["gw_t8"], L["vw_t8"], L["xq_sc"], L["gv_sc"]

    gb_in, vb_in = L["gb_in"], L["vb_in"]
    em.seq("v", lambda v: v.tensor_scalar(out=sc_g[:], in0=gx_all[:], scalar1=g127_w["gw"][:],
                                           scalar2=None, op0=ALU.mult))
    t_scv = O("v", lambda v: v.tensor_scalar(out=sc_v[:], in0=gx_all[:], scalar1=g127_w["vw"][:],
                                             scalar2=None, op0=ALU.mult), inc="v")
    W("s", "v", t_scv)

    txq = DMA("g", f2_xq[:].rearrange("dd (c s) -> dd c s", s=OWN),
              xq_sc[:, :].rearrange("(c dd) s -> dd c s", dd=128), "lxq")

    NHT = HID // 512
    g_w = [0, 0]
    g_psum = [0, 0, 0, 0]
    g_b4 = 0
    g_brow_t = [0, 0]
    g_bias_g = [0, 0]
    g_g1 = [0, 0]
    g_g2 = [0, 0]
    g_gs = [0, 0]
    g_gv_ld = [0, 0]
    for ht in range(NHT):
        bi = ht % 2
        W("g", "t", g_w[bi])
        tw1 = DMA("g", f2_gw[bi][:].rearrange("dd (c hh) -> dd c hh", hh=512),
            gw_t8[:, ht * 512:(ht + 1) * 512].rearrange(
            "(c dd) hh -> dd c hh", dd=128), f"ltw{bi}")
        tw2 = DMA("g", f2_vw[bi][:].rearrange("dd (c hh) -> dd c hh", hh=512),
            vw_t8[:, ht * 512:(ht + 1) * 512].rearrange(
            "(c dd) hh -> dd c hh", dd=128), f"ltw{bi}")
        W("v", f"ltw{bi}", tw2)
        O("v", lambda v, b=bi: v.tensor_scalar(out=f2_gw[b][:], in0=f2_gw[b][:],
                                               scalar1=c_one[:], scalar2=c_mone[:],
                                               op0=ALU.min, op1=ALU.max))
        tv_clip = O("v", lambda v, b=bi: v.tensor_scalar(out=f2_vw[b][:], in0=f2_vw[b][:],
                                                         scalar1=c_one[:], scalar2=c_mone[:],
                                                         op0=ALU.min, op1=ALU.max), inc="v")
        W("y", "t", g_brow_t[bi])
        DMA("y", f2_gbr[bi][:], gb_in[0:1, ht * 512:(ht + 1) * 512], f"lbr{bi}")
        t_brow = DMA("y", f2_vbr[bi][:], vb_in[0:1, ht * 512:(ht + 1) * 512], f"lbr{bi}")
        W("t", "s", g_b4)
        W("t", f"lbr{bi}", t_brow)
        tt_gb = O("t", lambda t_, b=bi: t_.matmul(
            out=bank(4), lhsT=onesrow_f[:], rhs=f2_gbr[b][:],
            start=True, stop=True, skip_group_check=True), inc="t")
        W("s", "t", tt_gb)
        W("s", "g", g_bias_g[bi])
        ts_gb = O("s", lambda s_, b=bi: s_.copy(out=f2_gbb[b][:], in_=bank(4)), inc="s")
        W("t", "s", ts_gb)
        tt_vb = O("t", lambda t_, b=bi: t_.matmul(
            out=bank(4), lhsT=onesrow_f[:], rhs=f2_vbr[b][:],
            start=True, stop=True, skip_group_check=True), inc="t")
        g_brow_t[bi] = tt_vb
        W("s", "t", tt_vb)
        g_b4 = O("s", lambda s_, b=bi: s_.copy(out=f2_vbb[b][:], in_=bank(4)), inc="s")
        ts_vb = g_b4

        for stt in range(NOT):
            pb = stt % 2
            for bb, wbuf in ((0, f2_gw[bi]), (2, f2_vw[bi])):
                bk = bb + pb
                W("t", "s", g_psum[bk])
                if stt == 0 and bb == 0:
                    W("t", "v", tv_clip)
                    if ht == 0:
                        W("t", "lxq", txq)
                    W("t", "s", ts_vb)
                tt = None
                for c in range(NCH):
                    tt = O("t", lambda t_, c_=c, st_=stt, wb=wbuf, bk_=bk: t_.matmul(
                        out=bank(bk_),
                        lhsT=f2_xq[:, c_ * OWN + st_ * 128: c_ * OWN + (st_ + 1) * 128],
                        rhs=wb[:, c_ * 512:(c_ + 1) * 512],
                        start=(c_ == 0), stop=(c_ == NCH - 1), skip_group_check=True),
                        inc="t" if c == NCH - 1 else None)
                if bb == 2 and stt == NOT - 1:
                    g_w[bi] = tt
                if bb == 0:
                    tt_g = tt
                else:
                    tt_v = tt
            W("s", "t", tt_g)
            W("s", "g", g_g1[pb])
            ts_g1 = O("s", lambda s_, st_=stt, bk=0 + pb, p=pb: s_.activation(
                out=f2_g1[p][:], in_=bank(bk), func=AF.Identity, bias=c_zero[:],
                scale=sc_g[:, st_:st_ + 1]), inc="s")
            g_psum[0 + pb] = ts_g1
            W("s", "t", tt_v)
            ts_v1 = O("s", lambda s_, st_=stt, bk=2 + pb, p=pb: s_.activation(
                out=f2_v1[p][:], in_=bank(bk), func=AF.Identity, bias=c_zero[:],
                scale=sc_v[:, st_:st_ + 1]), inc="s")
            g_psum[2 + pb] = ts_v1
            W("g", "s", ts_v1)
            W("g", "s", g_g2[pb])
            tg_g2 = O("g", lambda g_, b=bi, p=pb: g_.tensor_tensor(
                out=f2_g2[p][:], in0=f2_g1[p][:], in1=f2_gbb[b][:], op=ALU.add), inc="g")
            W("g", "v", g_gs[pb])
            tg_v2 = O("g", lambda g_, b=bi, p=pb: g_.tensor_tensor(
                out=f2_v2[p][:], in0=f2_v1[p][:], in1=f2_vbb[b][:], op=ALU.add), inc="g")
            g_g1[pb] = tg_v2
            if stt == NOT - 1:
                g_bias_g[bi] = tg_v2
            W("s", "g", tg_g2)
            W("s", "v", g_gs[pb])
            g_g2[pb] = O("s", lambda s_, p=pb: s_.activation(
                out=f2_gs[p][:], in_=f2_g2[p][:], func=AF.Silu, bias=c_zero[:],
                scale=1.0), inc="s")
            ts_sl = g_g2[pb]
            W("v", "s", ts_sl)
            W("v", "g", tg_v2)
            W("v", f"sgv{pb}", g_gv_ld[pb])
            gvb = f2_gvb[pb]
            tv_gv = em.seq("v", lambda v, gv=gvb, p=pb: v.tensor_tensor(
                out=gv[:], in0=f2_gs[p][:], in1=f2_v2[p][:], op=ALU.mult))
            em.seq("v", lambda v, gv=gvb: v.tensor_reduce(
                out=f2_am[:], in_=gv[:], axis=AX.X,
                op=ALU.max, apply_absolute_value=True))
            tv_am = O("v", lambda v, st_=stt: v.tensor_tensor(
                out=gh_run[:, st_:st_ + 1], in0=gh_run[:, st_:st_ + 1],
                in1=f2_am[:], op=ALU.max), inc="v")
            g_gs[pb] = tv_am
            W("y", "v", tv_gv)
            g_gv_ld[pb] = DMA("y", gv_sc[stt * 128:(stt + 1) * 128,
                                         ht * 512:(ht + 1) * 512], gvb[:], f"sgv{pb}")
    em.barrier()


def emit_f3a(nc, em, L):
    O, W, DMA = em.op, em.wait, em.dma
    bank_bf, bank, bankpair = L["bank_bf"], L["bank"], L["bankpair"]
    ps = L["ps"]
    gh_run, sc_h127, c_geps, c_inv127 = L["gh_run"], L["sc_h127"], L["c_geps"], L["c_inv127"]
    ghrow_t, shrow, g127_w, idn_f, idn_b = L["ghrow_t"], L["shrow"], L["g127_w"], L["idn_f"], L["idn_b"]
    f3_gv, f3_i32, f3_hb, f3_stg = L["f3_gv"], L["f3_i32"], L["f3_hb"], L["f3_stg"]
    gv_sc, hq_sc, row_sc = L["gv_sc"], L["hq_sc"], L["row_sc"]
    c_zero, rc16 = L["c_zero"], L["rc16"]

    em.seq("v", lambda v: v.tensor_scalar(out=gh_run[:], in0=gh_run[:], scalar1=c_geps[:],
                                           scalar2=None, op0=ALU.max))
    em.seq("v", lambda v: v.tensor_scalar(out=rc16[:], in0=gh_run[:], scalar1=c_inv127[:],
                                          scalar2=None, op0=ALU.mult))
    em.seq("v", lambda v: v.reciprocal(out=sc_h127[:], in_=rc16[:]))
    tv1 = O("v", lambda v: v.tensor_copy(out=rc16[:], in_=sc_h127[:]), inc="v")
    W("t", "v", tv1)
    tt1 = O("t", lambda t_: t_.transpose(out=ps[0:NOT, 0:128], in_=gh_run[:],
                                         identity=idn_f[:]), inc="t")
    W("v", "t", tt1)
    tv2 = O("v", lambda v: v.tensor_copy(out=ghrow_t[:], in_=ps[0:NOT, 0:128]), inc="v")
    W("y", "v", tv2)
    td1 = DMA("y", row_sc[:, :].rearrange("one (t ss) -> (one t) ss", ss=128), ghrow_t[:], "lrow")
    W("y", "lrow", td1)
    td2 = DMA("y", shrow[:], row_sc[:, :], "lrow")
    W("s", "lrow", td2)
    ts1 = O("s", lambda s_: s_.activation(out=shrow[:], in_=shrow[:], func=AF.Identity,
                                          bias=c_zero[0:1, :],
                                          scale=g127_w["ow"][0:1, :]), inc="s")
    L.update(f3_shrow_s=ts1)

    g_gv = 0
    g_pb = [0, 0]
    g_stg_ld = 0
    for sbk in range(4):
        for st4 in range(4):
            sti = sbk * 4 + st4
            W("y", "v", g_gv)
            tgv = DMA("y", f3_gv[0][:], gv_sc[sti * 128:(sti + 1) * 128, :], "lgv0")
            W("v", "lgv0", tgv)
            if st4 == 0:
                W("v", "shq", g_stg_ld)
            em.seq("v", lambda v, st_=sti: v.tensor_scalar(
                out=f3_i32[:], in0=f3_gv[0][:], scalar1=sc_h127[:, st_:st_ + 1],
                scalar2=None, op0=ALU.mult))
            tv_hb = O("v", lambda v: v.tensor_copy(out=f3_hb[:], in_=f3_i32[:]), inc="v")
            g_gv = tv_hb
            W("t", "v", tv_hb)
            for cb in range(8):
                bnk = cb % 2
                W("t", "v", g_pb[bnk])
                tt = None
                for k in range(8):
                    c = cb * 8 + k
                    tt = O("t", lambda t_, c_=c, k_=k, b=bnk: t_.transpose(
                        out=bankpair(2 * b)[:, k_ * 128:(k_ + 1) * 128],
                        in_=f3_hb[:, c_ * 128:(c_ + 1) * 128], identity=idn_f[:]),
                        inc="t" if k == 7 else None)
                W("v", "t", tt)
                g_pb[bnk] = O("v", lambda v, cb_=cb, s4=st4, b=bnk: v.tensor_copy(
                    out=f3_stg[:].rearrange("dd (c ss) -> dd c ss", ss=512)[
                        :, cb_ * 8:(cb_ + 1) * 8, s4 * 128:(s4 + 1) * 128],
                    in_=bankpair(2 * b)[:]), inc="v")
        W("y", "v", em.cnt["v"])
        last = 0
        for dmac in range(8):
            last = DMA("y", hq_sc[sbk, dmac * 8:(dmac + 1) * 8, :, :].rearrange(
                "c hh ss -> hh c ss"),
                f3_stg[:, dmac * 8 * 512:(dmac + 1) * 8 * 512].rearrange(
                    "dd (c ss) -> dd c ss", ss=512), "shq")
        g_stg_ld = last
    em.barrier()


def emit_f3b(nc, em, L):
    O, W, DMA = em.op, em.wait, em.dma
    bank = L["bank"]
    f3_hq, f3_ow, f3_x, f3_ao, f3_t, f3_o, f3_sbc = (
        L["f3_hq"], L["f3_ow"], L["f3_x"], L["f3_ao"], L["f3_t"], L["f3_o"], L["f3_sbc"])
    c_one, c_mone = L["c_one"], L["c_mone"]
    onesrow_f, shrow = L["onesrow_f"], L["shrow"]
    hq_sc, ow_t8, xT_own, ao_sc, out_fm = (
        L["hq_sc"], L["ow_t8"], L["xT_own"], L["ao_sc"], L["out_fm"])

    g_hq = 0
    g_ow = [0, 0]
    g_x = [0, 0]
    g_ao = [0, 0]
    g_o = [0, 0]
    g_banks_v = 0
    g_sbc_v = 0
    for sbk in range(4):
        W("g", "t", g_hq)
        thq = DMA("g", f3_hq[:].rearrange("hh (c ss) -> hh c ss", ss=512),
              hq_sc[sbk, :, :, :].rearrange("c hh ss -> hh c ss"), "lhq")
        W("t", "v", g_sbc_v)
        tt_sbc = O("t", lambda t_, sb_=sbk: t_.matmul(
            out=bank(0), lhsT=onesrow_f[:], rhs=shrow[0:1, sb_ * 512:(sb_ + 1) * 512],
            start=True, stop=True, skip_group_check=True), inc="t")
        W("s", "t", tt_sbc)
        ts_sbc = O("s", lambda s_: s_.copy(out=f3_sbc[:], in_=bank(0)), inc="s")
        for dh in range(2):
            W("t", "s", ts_sbc)
            W("t", "v", g_banks_v)
            tt_ffn = None
            for grp in range(8):
                bi = grp % 2
                W("g", "t", g_ow[bi])
                tow = DMA("g", f3_ow[bi][:].rearrange("hh (c d) -> hh c d", d=1024),
                    ow_t8[grp * 1024:(grp + 1) * 1024,
                          dh * 1024:(dh + 1) * 1024].rearrange(
                    "(c hh) d -> hh c d", hh=128), f"low{bi}")
                W("v", f"low{bi}", tow)
                tv_owc = O("v", lambda v, b=bi: v.tensor_scalar(
                    out=f3_ow[b][:], in0=f3_ow[b][:], scalar1=c_one[:], scalar2=c_mone[:],
                    op0=ALU.min, op1=ALU.max), inc="v")
                W("t", "v", tv_owc)
                if grp == 0:
                    W("t", "lhq", thq)
                for dt in range(8):
                    for k in range(8):
                        c = grp * 8 + k
                        tt_ffn = O("t", lambda t_, b=bi, k_=k, dt_=dt, c_=c,
                                   s0=(grp == 0 and k == 0), s1=(grp == 7 and k == 7): t_.matmul(
                            out=bank(dt_),
                            lhsT=f3_ow[b][:, k_ * 1024 + dt_ * 128: k_ * 1024 + (dt_ + 1) * 128],
                            rhs=f3_hq[:, c_ * 512:(c_ + 1) * 512],
                            start=s0, stop=s1, skip_group_check=True),
                            inc="t" if (dt == 7 and k == 7) else None)
                g_ow[bi] = tt_ffn
            for dt in range(8):
                dchunk = dh * 8 + dt
                bi2 = dt % 2
                W("y", "v", g_x[bi2])
                tx = DMA("y", f3_x[bi2][:], xT_own[dchunk * 128:(dchunk + 1) * 128,
                                                   sbk * 512:(sbk + 1) * 512], f"lx3{bi2}")
                W("y", "v", g_ao[bi2])
                tao = DMA("y", f3_ao[bi2][:], ao_sc[dchunk * 128:(dchunk + 1) * 128,
                                                    sbk * 512:(sbk + 1) * 512], f"lao{bi2}")
                W("v", "t", tt_ffn)
                W("v", f"lao{bi2}", tao)
                W("v", f"lx3{bi2}", tx)
                W("v", f"so{bi2}", g_o[bi2])
                em.seq("v", lambda v, dt_=dt, b=bi2: v.tensor_tensor(
                    out=f3_t[b][:], in0=bank(dt_), in1=f3_sbc[:], op=ALU.mult))
                em.seq("v", lambda v, b=bi2: v.tensor_tensor(
                    out=f3_t[b][:], in0=f3_t[b][:], in1=f3_ao[b][:], op=ALU.add))
                tv_o = O("v", lambda v, b=bi2: v.tensor_tensor(
                    out=f3_o[b][:], in0=f3_t[b][:], in1=f3_x[b][:], op=ALU.add), inc="v")
                g_x[bi2] = tv_o
                g_ao[bi2] = tv_o
                W("y", "v", tv_o)
                g_o[bi2] = DMA("y", out_fm[dchunk * 128:(dchunk + 1) * 128,
                                           sbk * 512:(sbk + 1) * 512], f3_o[bi2][:],
                               f"so{bi2}")
            g_banks_v = em.cnt["v"]
            g_sbc_v = g_banks_v
        g_hq = em.cnt["t"]


def finish_all(nc, em, es):
    finals = {k: v for k, v in em.cnt.items() if k not in ("t", "v", "s", "g") and v > 0}
    for name in em.cnt:
        if name not in em.sems:
            em.sems[name] = es.enter_context(nc.semaphore(name))
    blk = es.enter_context(nc.Block())

    @blk.tensor
    def _(t):
        for f in em.ops["t"]:
            f(t)

    @blk.vector
    def _(v):
        for f in em.ops["v"]:
            f(v)

    @blk.scalar
    def _(s):
        for f in em.ops["s"]:
            f(s)

    @blk.gpsimd
    def _(g):
        for f in em.ops["g"]:
            f(g)

    @blk.sync
    def _(y):
        for f in em.ops["y"]:
            f(y)
        for name, v in finals.items():
            y.wait_ge(em.sems[name], v)

    es.close()


_CACHE = {}


def _get_nc():
    if "nc" not in _CACHE:
        _CACHE["nc"] = build_program()
    return _CACHE["nc"]


def kernel(L0_attn_x, L0_attn_W_Q, L0_attn_W_K, L0_attn_W_V, L0_attn_W_O,
           L0_ffn_gate_w, L0_ffn_gate_b, L0_ffn_val_w, L0_ffn_val_b,
           L0_ffn_out_w, L0_ffn_out_b):
    x = np.asarray(L0_attn_x, dtype=np.float32)
    wq = np.ascontiguousarray(np.asarray(L0_attn_W_Q, np.float32).transpose(0, 2, 1))
    wk = np.ascontiguousarray(np.asarray(L0_attn_W_K, np.float32).transpose(0, 2, 1))
    wv = np.ascontiguousarray(np.asarray(L0_attn_W_V, np.float32).transpose(0, 2, 1))
    wo = np.ascontiguousarray(np.asarray(L0_attn_W_O, np.float32).transpose(0, 2, 1))
    gw = np.ascontiguousarray(np.asarray(L0_ffn_gate_w, np.float32).T)
    vw = np.ascontiguousarray(np.asarray(L0_ffn_val_w, np.float32).T)
    ow = np.ascontiguousarray(np.asarray(L0_ffn_out_w, np.float32).T)
    gb = np.ascontiguousarray(np.asarray(L0_ffn_gate_b, np.float32).reshape(1, HID))
    vb = np.ascontiguousarray(np.asarray(L0_ffn_val_b, np.float32).reshape(1, HID))
    obc = np.ascontiguousarray(np.asarray(L0_ffn_out_b, np.float32).reshape(NCH, 128).T)
    iden_bf = np.eye(128, dtype=ml_dtypes.bfloat16)
    iden_f = np.eye(128, dtype=np.float32)

    in_maps = []
    for c in range(NCORES):
        b, r = c // 2, c % 2
        perm = np.arange(NT)
        if r == 1:
            perm = perm ^ 1
        xb = np.ascontiguousarray(
            x[b].reshape(NT, 128, D)[perm].reshape(S, D))
        own_tiles = [2 * t + r for t in range(NOT)]
        own_cols = np.concatenate([np.arange(g_ * 128, (g_ + 1) * 128) for g_ in own_tiles])
        xT = np.ascontiguousarray(x[b].T[:, own_cols])
        masks = np.zeros((8, 128, 512), dtype=ml_dtypes.bfloat16)
        kk = np.arange(128)[:, None]
        qq = np.arange(128)[None, :]
        for m in range(8):
            gk = (m + r) if m % 2 == 0 else (m - r)
            for g_ in range(4):
                masks[m][:, g_ * 128:(g_ + 1) * 128] = (
                    ((2 * g_ + r - gk) * 128 + qq) >= kk).astype(ml_dtypes.bfloat16)
        in_maps.append({
            "x_tm": xb, "xT_own": xT, "wqT": wq, "wkT": wk, "wvT": wv, "woT": wo,
            "gwT": gw, "vwT": vw, "owT": ow, "gb_in": gb, "vb_in": vb, "obc_in": obc,
            "iden_bf_in": iden_bf, "iden_f_in": iden_f, "masks_in": masks,
        })

    nc = _get_nc()
    res = run_bass_kernel_spmd(nc, in_maps, core_ids=list(range(NCORES)))

    out = np.empty((B, S, D), dtype=np.float32)
    for c in range(NCORES):
        b, r = c // 2, c % 2
        o = np.asarray(res.results[c]["out_fm"])
        oT = np.ascontiguousarray(o.T)
        for t in range(NOT):
            g_ = 2 * t + r
            out[b, g_ * 128:(g_ + 1) * 128, :] = oT[t * 128:(t + 1) * 128, :]
    return out


if __name__ == "__main__":
    nc = build_program()
    print("build ok")


# version 16
# speedup vs baseline: 1.0471x; 1.0049x over previous
import math
import numpy as np
import ml_dtypes
import concourse.bass as bass
import concourse.mybir as mybir
from concourse.bass_utils import run_bass_kernel_spmd
from contextlib import ExitStack

F32 = mybir.dt.float32
BF16 = mybir.dt.bfloat16
I32 = mybir.dt.int32
I8 = mybir.dt.int8
AF = mybir.ActivationFunctionType
ALU = mybir.AluOpType
AX = mybir.AxisListType

B, S, D = 4, 4096, 2048
H, E = 16, 128
HID = 8192
NCORES = 8
OWN = 2048
NT = S // 128
NOT = OWN // 128
NCH = D // 128
NHCH = HID // 128
NQB = 4
SCALE = 1.0 / math.sqrt(E)
EXP_BIAS = -12.0

PHASES = 6
DBG_SCRATCH = False


class Emitter:
    ENGS = ("t", "v", "s", "g", "y")

    def __init__(self):
        self.ops = {k: [] for k in self.ENGS}
        self.cnt = {}
        self.sems = {}

    def new(self, name):
        self.cnt[name] = 0

    def wait(self, eng, sem, val):
        if val and val > 0:
            self.ops[eng].append(lambda e, s=sem, v=val: e.wait_ge(self.sems[s], v))

    def op(self, eng, fn, inc=None, by=1):
        tgt = None
        if inc is not None:
            self.cnt[inc] += by
            tgt = self.cnt[inc]
            self.ops[eng].append(lambda e, f=fn, s=inc, b=by: f(e).then_inc(self.sems[s], b))
        else:
            self.ops[eng].append(lambda e, f=fn: f(e))
        return tgt

    def dma(self, eng, out, in_, sem):
        if sem not in self.cnt:
            self.cnt[sem] = 0
        self.cnt[sem] += 16
        tgt = self.cnt[sem]
        self.ops[eng].append(lambda e, o=out, i=in_, s=sem:
                             e.dma_start(out=o, in_=i).then_inc(self.sems[s], 16))
        return tgt

    def seq(self, eng, fn):
        t = self.op(eng, fn, inc=eng)
        self.wait(eng, eng, t)
        return t

    def barrier(self):
        snap = dict(self.cnt)
        for eng in self.ENGS:
            for s, v in snap.items():
                self.wait(eng, s, v)


def build_program():
    nc = bass.Bass("TRN2", target_bir_lowering=False)

    x_tm = nc.dram_tensor("x_tm", [S, D], F32, kind="ExternalInput").ap()
    xT_own = nc.dram_tensor("xT_own", [D, OWN], F32, kind="ExternalInput").ap()
    wqT = nc.dram_tensor("wqT", [H, D, E], F32, kind="ExternalInput").ap()
    wkT = nc.dram_tensor("wkT", [H, D, E], F32, kind="ExternalInput").ap()
    wvT = nc.dram_tensor("wvT", [H, D, E], F32, kind="ExternalInput").ap()
    woT = nc.dram_tensor("woT", [H, E, D], F32, kind="ExternalInput").ap()
    gwT = nc.dram_tensor("gwT", [D, HID], F32, kind="ExternalInput").ap()
    vwT = nc.dram_tensor("vwT", [D, HID], F32, kind="ExternalInput").ap()
    owT = nc.dram_tensor("owT", [HID, D], F32, kind="ExternalInput").ap()
    gb_in = nc.dram_tensor("gb_in", [1, HID], F32, kind="ExternalInput").ap()
    vb_in = nc.dram_tensor("vb_in", [1, HID], F32, kind="ExternalInput").ap()
    obc_in = nc.dram_tensor("obc_in", [128, NCH], F32, kind="ExternalInput").ap()
    iden_bf_in = nc.dram_tensor("iden_bf_in", [128, 128], BF16, kind="ExternalInput").ap()
    iden_f_in = nc.dram_tensor("iden_f_in", [128, 128], F32, kind="ExternalInput").ap()
    masks_in = nc.dram_tensor("masks_in", [8, 128, 512], BF16, kind="ExternalInput").ap()

    out_fm = nc.dram_tensor("out_fm", [D, OWN], F32, kind="ExternalOutput").ap()

    sk = "ExternalOutput" if DBG_SCRATCH else "Internal"
    xn_sc = nc.dram_tensor("xn_sc", [D, S], BF16, kind=sk).ap()
    xq_sc = nc.dram_tensor("xq_sc", [D, OWN], BF16, kind=sk).ap()
    z_sc = nc.dram_tensor("z_sc", [H * E, OWN], BF16, kind=sk).ap()
    ao_sc = nc.dram_tensor("ao_sc", [D, OWN], F32, kind=sk).ap()
    gw_t8 = nc.dram_tensor("gw_t8", [D, HID], I8, kind=sk).ap()
    vw_t8 = nc.dram_tensor("vw_t8", [D, HID], I8, kind=sk).ap()
    ow_t8 = nc.dram_tensor("ow_t8", [HID, D], I8, kind=sk).ap()
    gv_sc = nc.dram_tensor("gv_sc", [OWN, HID], F32, kind=sk).ap()
    hq_sc = nc.dram_tensor("hq_sc", [4, NHCH, 128, 512], I8, kind=sk).ap()
    row_sc = nc.dram_tensor("row_sc", [1, OWN], F32, kind="Internal").ap()
    xn_tm_dbg = nc.dram_tensor("xn_tm_dbg", [S, D], BF16, kind=sk).ap() if DBG_SCRATCH else None
    st_dbg = nc.dram_tensor("st_dbg", [NT, 128, 8], F32, kind=sk).ap() if DBG_SCRATCH else None

    em = Emitter()
    for s in ("t", "v", "s", "g"):
        em.new(s)

    es = ExitStack()
    e = es.enter_context

    def sb(name, cols, dtype, parts=128):
        return e(nc.sbuf_tensor(name, [parts, cols], dtype))

    idn_b = sb("idn_b", 128, BF16)
    idn_f = sb("idn_f", 128, F32)
    masks = sb("masks", 8 * 512, BF16)
    ones_b = sb("ones_b", 1, BF16)
    onesrow_f = sb("onesrow_f", 128, F32, parts=1)
    csc = sb("csc", 16, F32)
    ones_f = csc[:, 0:1]
    c_zero = csc[:, 1:2]
    c_one = csc[:, 2:3]
    c_mone = csc[:, 3:4]
    c_eps = csc[:, 4:5]
    c_geps = csc[:, 5:6]
    c_invD = csc[:, 6:7]
    c_inv127 = csc[:, 7:8]
    c_expb = csc[:, 8:9]
    rg_w = {"gw": csc[:, 9:10], "vw": csc[:, 10:11], "ow": csc[:, 11:12]}
    g127_w = {"gw": csc[:, 12:13], "vw": csc[:, 13:14], "ow": csc[:, 14:15]}
    obc_sb = sb("obc_sb", NCH, F32)
    gsc = sb("gsc", 6 * NOT + 2, F32)
    gx_all = gsc[:, 0 * NOT:1 * NOT]
    gxr127 = gsc[:, 1 * NOT:2 * NOT]
    gh_run = gsc[:, 2 * NOT:3 * NOT]
    sc_g = gsc[:, 3 * NOT:4 * NOT]
    sc_v = gsc[:, 4 * NOT:5 * NOT]
    sc_h127 = gsc[:, 5 * NOT:6 * NOT]
    f0tmp_a = gsc[:, 6 * NOT:6 * NOT + 1]
    f0tmp_b = gsc[:, 6 * NOT + 1:6 * NOT + 2]
    ghrow_t = sb("ghrow_t", 128, F32, parts=NOT)
    shrow = sb("shrow", OWN, F32, parts=1)
    wsum = sb("wsum", 384, F32)
    gtmp = sb("gtmp", 8, F32, parts=1)
    rca = sb("rca", 16, F32)
    rcb = sb("rcb", 16, F32)
    rc16 = sb("rc16", NOT, F32)

    ARENA = 192320
    AR = e(nc.sbuf_tensor("AR", [128, ARENA], mybir.dt.uint8))
    AR_ADDR = nc.lookup_mloc(AR).addr
    _cvn = [0]

    class Carver:

        def __init__(self, base, limit):
            self.off, self.limit = base, limit

        def take(self, cols, dtype, parts=128):
            nb = cols * mybir.dt.size(dtype)
            off = (self.off + 63) // 64 * 64
            self.off = off + nb
            assert self.off <= self.limit, (off, nb, self.limit)
            _cvn[0] += 1
            h = nc.alloc_sbuf_tensor_at(f"cv{_cvn[0]}", [parts, cols], dtype,
                                        offset=AR_ADDR + off)
            return h.ap()

    F0_SZ = 22528
    NF0 = 3
    cf0 = Carver(0, F0_SZ)
    FTW = 1024
    f0_in = [cf0.take(FTW, F32) for i in range(NF0)]
    f0_t1 = [cf0.take(FTW, F32) for i in range(1)]
    f0_ab = cf0.take(FTW, BF16)
    f0_i8 = [cf0.take(FTW, I8) for i in range(NF0)]

    c1 = Carver(F0_SZ, ARENA)
    p1_x = [c1.take(D, F32) for i in range(2)]
    p1_sq = [c1.take(D, F32) for i in range(2)]
    p1_nf = [c1.take(D, F32) for i in range(2)]
    p1_i32 = c1.take(D, I32)
    p1_qb = c1.take(D, F32)
    p1_st = [c1.take(NCH * 128, BF16) for i in range(2)]
    p1_stq = c1.take(NCH * 128, BF16)
    p1_m = [c1.take(1, F32) for i in range(2)]
    p1_m2 = [c1.take(1, F32) for i in range(2)]
    p1_v = [c1.take(1, F32) for i in range(2)]
    p1_sd = [c1.take(1, F32) for i in range(2)]
    p1_rs = [c1.take(1, F32) for i in range(2)]
    p1_g0 = c1.take(1, F32)

    c2 = Carver(F0_SZ, ARENA)
    p2_xn = [c2.take(NCH * 256, BF16) for i in range(2)]
    p2_K = [c2.take(S, BF16) for h in range(4)]
    p2_Q = [c2.take(OWN, BF16) for h in range(4)]
    p2_V = c2.take(NT * 512, BF16)
    p2_wk = [c2.take(NCH * 128, BF16) for h in range(4)]
    p2_wq = [c2.take(NCH * 128, BF16) for h in range(4)]
    p2_wv = c2.take(NCH * 512, BF16)
    p2_P = [c2.take(1024, BF16) for i in range(2)]
    p2_z = [c2.take(512, BF16) for i in range(2)]
    p2_rd = [c2.take(512, F32, parts=1) for i in range(2)]
    p2_dn = [c2.take(512, F32, parts=1) for i in range(2)]
    p2_rb = [c2.take(512, F32) for i in range(2)]

    c3 = Carver(0, ARENA)
    f1_wo = c3.take(H * 1024, BF16)
    f1_z = c3.take(H * 512, BF16)
    f1_ao = [c3.take(512, F32) for i in range(2)]

    c4 = Carver(0, ARENA)
    f2_xq = c4.take(NCH * OWN, BF16)
    f2_gw = [c4.take(NCH * 512, BF16) for i in range(2)]
    f2_vw = [c4.take(NCH * 512, BF16) for i in range(2)]
    f2_gbb = [c4.take(512, F32) for i in range(2)]
    f2_vbb = [c4.take(512, F32) for i in range(2)]
    f2_g1 = [c4.take(512, F32) for i in range(2)]
    f2_g2 = [c4.take(512, F32) for i in range(2)]
    f2_gs = [c4.take(512, F32) for i in range(2)]
    f2_v1 = [c4.take(512, F32) for i in range(2)]
    f2_v2 = [c4.take(512, F32) for i in range(2)]
    f2_gvb = [c4.take(512, F32) for i in range(2)]
    f2_am = c4.take(1, F32)
    f2_gbr = [c4.take(512, F32, parts=1) for i in range(2)]
    f2_vbr = [c4.take(512, F32, parts=1) for i in range(2)]

    c5 = Carver(0, ARENA)
    f3_gv = [c5.take(HID, F32) for i in range(1)]
    f3_i32 = c5.take(HID, I32)
    f3_hb = c5.take(HID, F32)
    f3_stg = c5.take(NHCH * 512, I8)

    c6 = Carver(0, ARENA)
    f3_hq = c6.take(NHCH * 512, BF16)
    f3_ow = [c6.take(8 * 1024, BF16) for i in range(2)]
    f3_x = [c6.take(512, F32) for i in range(4)]
    f3_ao = [c6.take(512, F32) for i in range(4)]
    f3_t = [c6.take(512, F32) for i in range(2)]
    f3_t2 = [c6.take(512, F32) for i in range(2)]
    f3_o = [c6.take(512, F32) for i in range(4)]
    f3_sbc = c6.take(512, F32)

    ps = e(nc.psum_tensor("ps", [128, 4096], F32))

    def bank(i):
        return ps[:, 512 * i:512 * (i + 1)]

    def bankpair(i):
        return ps[:, 512 * i:512 * (i + 2)]

    def bank_bf(i):
        return ps[:, 512 * i:512 * (i + 1)].bitcast(BF16)

    em.alloc_sems = lambda: None

    O, W, DMA = em.op, em.wait, em.dma
    SQ = em.seq

    DMA("y", idn_b[:], iden_bf_in[:], "lset")
    DMA("y", idn_f[:], iden_f_in[:], "lset")
    DMA("y", masks[:].rearrange("k (m q) -> k m q", m=8),
        masks_in[:].rearrange("m k q -> k m q"), "lset")
    t_setup = DMA("y", obc_sb[:], obc_in[:], "lset")
    for tile, val in ((ones_b, 1.0), (ones_f, 1.0), (onesrow_f, 1.0), (c_zero, 0.0),
                      (c_one, 1.0), (c_mone, -1.0), (c_eps, 1e-5), (c_geps, 1e-5),
                      (c_invD, 1.0 / D), (c_inv127, 1.0 / 127.0), (c_expb, EXP_BIAS)):
        O("v", lambda v, tl=tile, vv=val: v.memset(tl[:], vv))
    O("v", lambda v: v.memset(gh_run[:], 0.0))
    t_msets = SQ("v", lambda v: v.memset(wsum[:], 0.0))
    for _e in ("t", "s", "g", "y"):
        W(_e, "v", t_msets)
    W("t", "lset", t_setup)
    W("s", "lset", t_setup)

    f0_tiles = []
    for m, ap_ in (("gw", gwT), ("vw", vwT), ("ow", owT)):
        nr = NCH if m != "ow" else NHCH
        ncols = HID if m != "ow" else D
        for ct in range(ncols // FTW):
            for rt in range(nr):
                f0_tiles.append((m, ap_, rt, ct, FTW))
    assert len(f0_tiles) == 384
    f0 = {"i": 0, "guard": [("s", 0)] * NF0}

    def emit_f0(n):
        for _ in range(n):
            i = f0["i"]
            if i >= len(f0_tiles):
                return
            f0["i"] += 1
            m, ap_, rt, ct, tw = f0_tiles[i]
            buf = f0_in[i % NF0]
            eng = "s" if i % 2 == 0 else "v"
            ge, gc = f0["guard"][i % NF0]
            W("y", ge, gc)
            td = DMA("y", buf[:, 0:tw], ap_[rt * 128:(rt + 1) * 128, ct * tw:(ct + 1) * tw],
                     f"lf{i % NF0}")
            W(eng, f"lf{i % NF0}", td)
            if eng == "s":
                W("s", "s", em.cnt["s"])
                f0["guard"][i % NF0] = ("s", O("s", lambda s_, b=buf, tw_=tw, col=i:
                                               s_.activation(
                    out=f0_ab[:, 0:tw_], in_=b[:, 0:tw_], func=AF.Abs, bias=c_zero[:],
                    scale=1.0, accum_out=wsum[:, col:col + 1]), inc="s"))
            else:
                f0["guard"][i % NF0] = ("v", O("v", lambda v, b=buf, tw_=tw, col=i:
                                               v.tensor_reduce(
                    out=wsum[:, col:col + 1], in_=b[:, 0:tw_], axis=AX.X, op=ALU.add,
                    apply_absolute_value=True), inc="v"))

    def finish_f0():
        emit_f0(len(f0_tiles))
        W("v", "s", em.cnt["s"])
        W("v", "v", em.cnt["v"])
        spans = {"gw": [], "vw": [], "ow": []}
        for i, (m, *_r) in enumerate(f0_tiles):
            spans[m].append(i)
        for mi, m in enumerate(("gw", "vw", "ow")):
            lo, hi = min(spans[m]), max(spans[m]) + 1
            em.seq("v", lambda v, lo=lo, hi=hi: v.tensor_reduce(
                out=f0tmp_a[:], in_=wsum[:, lo:hi], axis=AX.X, op=ALU.add))
            tv = O("v", lambda v: v.tensor_copy(out=f0tmp_b[:], in_=f0tmp_a[:]), inc="v")
            W("t", "v", tv)
            tt = O("t", lambda t, b=mi: t.matmul(out=bank(7)[0:1, b:b + 1], lhsT=ones_f[:],
                                                 rhs=f0tmp_b[:], start=True, stop=True,
                                                 skip_group_check=True), inc="t")
            W("s", "t", tt)
            ts1 = O("s", lambda s_, b=mi: s_.activation(
                out=gtmp[0:1, 2 * b:2 * b + 1], in_=bank(7)[0:1, b:b + 1], func=AF.Identity,
                bias=c_zero[0:1, :], scale=1.0 / (D * HID)), inc="s")
            W("v", "s", ts1)
            em.seq("v", lambda v, b=mi: v.tensor_scalar(
                out=rca[0:1, 0:1], in0=gtmp[0:1, 2 * b:2 * b + 1],
                scalar1=c_geps[0:1, :], scalar2=None, op0=ALU.max))
            em.seq("v", lambda v: v.reciprocal(out=rcb[0:1, 0:1], in_=rca[0:1, 0:1]))
            em.seq("v", lambda v, b=mi: v.tensor_copy(
                out=gtmp[0:1, 2 * b + 1:2 * b + 2], in_=rcb[0:1, 0:1]))
            tv2 = O("v", lambda v, b=mi: v.tensor_scalar(
                out=gtmp[0:1, 2 * b:2 * b + 1], in0=rca[0:1, 0:1],
                scalar1=c_inv127[0:1, :], scalar2=None, op0=ALU.mult), inc="v")
            W("t", "v", tv2)
            tt2 = O("t", lambda t, b=mi: t.matmul(
                out=bank(7)[:, 8 + 2 * b:8 + 2 * b + 2], lhsT=onesrow_f[:],
                rhs=gtmp[0:1, 2 * b:2 * b + 2], start=True, stop=True,
                skip_group_check=True), inc="t")
            W("s", "t", tt2)
            O("s", lambda s_, m_=m, b=mi: s_.copy(out=g127_w[m_][:],
                                                  in_=bank(7)[:, 8 + 2 * b:8 + 2 * b + 1]))
            O("s", lambda s_, m_=m, b=mi: s_.copy(out=rg_w[m_][:],
                                                  in_=bank(7)[:, 8 + 2 * b + 1:8 + 2 * b + 2]),
              inc="s")
        return em.cnt["s"]

    f0b_tiles = []
    for m, ap_, o8 in (("gw", gwT, gw_t8), ("vw", vwT, vw_t8), ("ow", owT, ow_t8)):
        nr = NCH if m != "ow" else NHCH
        ncols = HID if m != "ow" else D
        for ct in range(ncols // FTW):
            for rt in range(nr):
                f0b_tiles.append((m, ap_, o8, rt, ct, FTW))
    assert len(f0b_tiles) == 384
    f0b = {"i": 0, "gv": [0] * NF0, "gd": [0] * NF0, "gamma": 0, "f0sg": [0] * NF0,
           "t1g": [0, 0]}

    def emit_f0b(n):
        for _ in range(n):
            i = f0b["i"]
            if i >= len(f0b_tiles):
                return
            f0b["i"] += 1
            m, ap_, o8, rt, ct, tw = f0b_tiles[i]
            buf, ob = f0_in[i % NF0], f0_i8[i % NF0]
            if i < NF0:
                ge, gc = f0["guard"][i % NF0]
                W("y", ge, gc)
            W("y", "v", f0b["gv"][i % NF0])
            td = DMA("y", buf[:, 0:tw], ap_[rt * 128:(rt + 1) * 128, ct * tw:(ct + 1) * tw],
                     f"lf{i % NF0}")
            W("v", f"lf{i % NF0}", td)
            W("v", "s", f0b["gamma"])
            W("v", f"sf{i % NF0}", f0b["gd"][i % NF0])
            tv = O("v", lambda v, b=buf, ob_=ob, tw_=tw, m_=m: v.tensor_scalar(
                out=ob_[:, 0:tw_], in0=b[:, 0:tw_], scalar1=rg_w[m_][:],
                scalar2=None, op0=ALU.mult), inc="v")
            f0b["gv"][i % NF0] = tv
            W("y", "v", tv)
            f0b["gd"][i % NF0] = DMA(
                "y", o8[rt * 128:(rt + 1) * 128, ct * tw:(ct + 1) * tw], ob[:, 0:tw],
                f"sf{i % NF0}")

    g_xbuf = [0, 0]
    g_nb = [0, 0]
    g_qb = 0
    g_bank = [0, 0]
    g_stdma = [0, 0]
    g_stqdma = 0
    g_sv = [0, 0]
    own_idx = 0
    t_p1_ld = 0
    for t in range(NT):
        bi = t % 2
        xbuf, nbuf, stg = p1_x[bi], p1_nf[bi], p1_st[bi]
        is_own = (t % 2 == 0)
        W("y", "v", g_xbuf[bi])
        td = DMA("y", xbuf[:], x_tm[t * 128:(t + 1) * 128, :], f"lx{bi}")
        W("s", f"lx{bi}", td)
        W("s", "v", g_sv[bi])
        ts_sq = O("s", lambda s_, xb=xbuf, b=bi: s_.activation(
            out=p1_sq[b][:], in_=xb[:], func=AF.Square, bias=c_zero[:], scale=1.0), inc="s")
        W("v", f"lx{bi}", td)
        W("v", "s", ts_sq)
        O("v", lambda v, xb=xbuf, b=bi: v.tensor_reduce(out=p1_m[b][:], in_=xb[:],
                                                        axis=AX.X, op=ALU.add))
        O("v", lambda v, b=bi: v.tensor_reduce(out=p1_m2[b][:], in_=p1_sq[b][:],
                                               axis=AX.X, op=ALU.add))
        O("v", lambda v, b=bi: v.tensor_scalar(out=p1_m[b][:], in0=p1_m[b][:],
                                               scalar1=c_invD[:], scalar2=None, op0=ALU.mult))
        SQ("v", lambda v, b=bi: v.tensor_scalar(out=p1_m2[b][:], in0=p1_m2[b][:],
                                                scalar1=c_invD[:], scalar2=None, op0=ALU.mult))
        tv_var = O("v", lambda v, b=bi: v.scalar_tensor_tensor(out=p1_v[b][:], in0=p1_m[b][:],
                                                               scalar=p1_m[b][:], in1=p1_m2[b][:],
                                                               op0=ALU.mult, op1=ALU.subtract),
                   inc="v")
        W("s", "v", tv_var)
        ts_sd = O("s", lambda s_, b=bi: s_.activation(out=p1_sd[b][:], in_=p1_v[b][:],
                                                      func=AF.Sqrt,
                                                      bias=c_eps[:], scale=-1.0), inc="s")
        W("v", "s", ts_sd)
        g_sv[bi] = SQ("v", lambda v, b=bi: v.reciprocal(out=p1_rs[b][:], in_=p1_sd[b][:]))
        W("v", "t", g_nb[bi])
        tv_nb = O("v", lambda v, xb=xbuf, nb=nbuf, b=bi: v.tensor_scalar(
            out=nb[:], in0=xb[:], scalar1=p1_m[b][:], scalar2=p1_rs[b][:],
            op0=ALU.subtract, op1=ALU.mult), inc="v")
        tv_done = tv_nb
        if is_own:
            oi = own_idx
            O("v", lambda v, nb=nbuf: v.tensor_reduce(out=p1_g0[:], in_=nb[:], axis=AX.X,
                                                      op=ALU.max, apply_absolute_value=True))
            O("v", lambda v, o=oi: v.tensor_scalar(out=gx_all[:, o:o + 1], in0=p1_g0[:],
                                                   scalar1=c_geps[:], scalar2=None, op0=ALU.max))
            O("v", lambda v: v.tensor_scalar(out=p1_g0[:], in0=p1_g0[:],
                                             scalar1=c_geps[:], scalar2=None, op0=ALU.max))
            SQ("v", lambda v: v.tensor_scalar(out=p1_g0[:], in0=p1_g0[:],
                                              scalar1=c_inv127[:], scalar2=None,
                                              op0=ALU.mult))
            SQ("v", lambda v: v.reciprocal(out=rca[:, 0:1], in_=p1_g0[:]))
            O("v", lambda v, o=oi: v.tensor_copy(out=gxr127[:, o:o + 1], in_=rca[:, 0:1]))
            SQ("v", lambda v, nb=nbuf: v.tensor_scalar(out=p1_i32[:], in0=nb[:],
                                                       scalar1=rca[:, 0:1],
                                                       scalar2=None, op0=ALU.mult))
            W("v", "t", g_qb)
            tv_done = O("v", lambda v: v.tensor_copy(out=p1_qb[:], in_=p1_i32[:]), inc="v")
        g_xbuf[bi] = tv_done
        if DBG_SCRATCH:
            W("y", "v", tv_done)
            DMA("y", xn_tm_dbg[t * 128:(t + 1) * 128, :], nbuf[:], "dbg")
            for di, (dst, src_) in enumerate([(st_dbg[t, :, 0:1], p1_m), (st_dbg[t, :, 1:2], p1_m2),
                                              (st_dbg[t, :, 2:3], p1_v), (st_dbg[t, :, 3:4], p1_sd),
                                              (st_dbg[t, :, 4:5], p1_rs)]):
                if "dbg" not in em.cnt:
                    em.cnt["dbg"] = 0
                em.cnt["dbg"] += 16
                em.ops["y"].append(lambda e, o=dst, i=src_, s="dbg": e.dma_start(
                    out=o, in_=i[:], allow_slow_non_contiguous=True).then_inc(em.sems[s], 16))

        srcs = [(nbuf, 0, stg, 0), (nbuf, 8, stg, 1)]
        if is_own:
            srcs += [(p1_qb, 0, p1_stq, 0), (p1_qb, 8, p1_stq, 1)]
        W("t", "v", tv_done)
        for ci, (srcbuf, c0, dstg, half) in enumerate(srcs):
            bnk = ci % 2
            W("t", "v", g_bank[bnk])
            tt = None
            for k in range(8):
                tt = O("t", lambda t_, sb_=srcbuf, c=c0, k_=k, b=bnk: t_.transpose(
                    out=bankpair(2 * b)[:, k_ * 128:(k_ + 1) * 128],
                    in_=sb_[:, (c + k_) * 128:(c + k_ + 1) * 128],
                    identity=idn_f[:]), inc="t" if k == 7 else None)
            if ci == 1:
                g_nb[bi] = tt
            if ci == 3:
                g_qb = tt
            W("v", "t", tt)
            W("v", "sstq" if dstg is p1_stq else f"sst{bi}",
              g_stqdma if dstg is p1_stq else g_stdma[bi])
            tv_ev = O("v", lambda v, b=bnk, h=half, d=dstg: v.tensor_copy(
                out=d[:, h * 1024:(h + 1) * 1024], in_=bankpair(2 * b)[:]), inc="v")
            g_bank[bnk] = tv_ev
            if ci == 1:
                W("y", "v", tv_ev)
                g_stdma[bi] = DMA("y", xn_sc[:, t * 128:(t + 1) * 128].rearrange(
                    "(c dd) s -> dd c s", dd=128),
                    stg[:].rearrange("dd (c s) -> dd c s", s=128), f"sst{bi}")
                t_p1_ld = g_stdma[bi]
            if ci == 3:
                W("y", "v", tv_ev)
                g_stqdma = DMA("y", xq_sc[:, own_idx * 128:(own_idx + 1) * 128].rearrange(
                    "(c dd) s -> dd c s", dd=128),
                    p1_stq[:].rearrange("dd (c s) -> dd c s", s=128), "sstq")
                t_p1_ld = g_stqdma
        if is_own:
            own_idx += 1
        emit_f0(12)
    assert own_idx == NOT

    L = dict(locals())
    if PHASES >= 2:
        emit_p2(nc, em, L)
    if PHASES >= 3:
        emit_f1z(nc, em, L)
    if PHASES >= 4:
        emit_f2(nc, em, L)
    if PHASES >= 5:
        emit_f3a(nc, em, L)
    if PHASES >= 6:
        emit_f3b(nc, em, L)

    finish_all(nc, em, es)
    return nc


def emit_p2(nc, em, L):
    O, W, DMA = em.op, em.wait, em.dma
    g = L
    bank, bankpair, bank_bf = g["bank"], g["bankpair"], g["bank_bf"]
    p2_xn, p2_K, p2_Q, p2_V = g["p2_xn"], g["p2_K"], g["p2_Q"], g["p2_V"]
    p2_wk, p2_wq, p2_wv = g["p2_wk"], g["p2_wq"], g["p2_wv"]
    p2_P, p2_z, p2_rd, p2_rb = g["p2_P"], g["p2_z"], g["p2_rd"], g["p2_rb"]
    p2_dn = g["p2_dn"]
    masks, ones_b, onesrow_f = g["masks"], g["ones_b"], g["onesrow_f"]
    c_expb = g["c_expb"]
    wkT, wqT, wvT = g["wkT"], g["wqT"], g["wvT"]
    xn_sc, z_sc = g["xn_sc"], g["z_sc"]
    emit_f0, finish_f0, emit_f0b = g["emit_f0"], g["finish_f0"], g["emit_f0b"]
    f0b = g["f0b"]
    t_p1_ld = g["t_p1_ld"]
    t_setup = g["t_setup"]

    st = {
        "xn": [0, 0],
        "wkq": [0, 0, 0, 0],
        "wv": 0,
        "kqv_t": 0,
        "pb67": [0, 0, 0, 0],
        "P_s": [0, 0],
        "P_t": [0, 0],
        "sc_s": [0, 0],
        "z_t": [0, 0],
        "d_s": 0,
        "rb_s": [0, 0],
        "rb_v": [0, 0],
        "z_ld": [0, 0],
        "bc_s": 0,
    }
    W("v", "lset", t_setup)
    em.barrier()
    f0b["gamma"] = finish_f0()

    for grp in range(4):
        W("s", "t", st["kqv_t"])
        for hl in range(4):
            W("g", "t", st["wkq"][hl])
            DMA("g", p2_wk[hl][:].rearrange("dd (c e) -> dd c e", e=128),
                wkT[4 * grp + hl, :, :].rearrange("(c dd) e -> dd c e", dd=128), "lw")
        for hl in range(4):
            DMA("g", p2_wq[hl][:].rearrange("dd (c e) -> dd c e", e=128),
                wqT[4 * grp + hl, :, :].rearrange("(c dd) e -> dd c e", dd=128), "lw")

        W("g", "t", st["wv"])
        for hl in range(4):
            tw_v = DMA("g",
                       p2_wv[:].rearrange("dd (c h e) -> dd c h e", h=4, e=128)[:, :, hl, :],
                       wvT[4 * grp + hl, :, :].rearrange("(c dd) e -> dd c e", dd=128), "lw")

        W("t", "v", st["z_t"][0])
        W("t", "v", st["z_t"][1])
        W("t", "s", st["bc_s"])
        W("t", "s", st["d_s"])
        ecnt = 0
        NSLAB = S // 256
        for sbk in range(NSLAB):
            bi = sbk % 2
            W("y", "t", st["xn"][bi])
            txn = DMA("y", p2_xn[bi][:].rearrange("dd (c s) -> dd c s", s=256),
                xn_sc[:, sbk * 256:(sbk + 1) * 256].rearrange(
                "(c dd) s -> dd c s", dd=128), f"lxn{bi}")
            W("t", f"lxn{bi}", txn)
            if sbk == 0:
                W("t", "lw", tw_v)
            for hl in range(4):
                bnk = 4 + (ecnt % 4)
                W("t", "s", st["pb67"][ecnt % 4])
                for c in range(NCH):
                    ttk = O("t", lambda t_, hl_=hl, c_=c, b=bi, bk=bnk: t_.matmul(
                        out=bank(bk)[:, 0:256],
                        lhsT=p2_wk[hl_][:, c_ * 128:(c_ + 1) * 128],
                        rhs=p2_xn[b][:, c_ * 256:(c_ + 1) * 256],
                        start=(c_ == 0), stop=(c_ == NCH - 1), skip_group_check=True),
                        inc="t" if c == NCH - 1 else None)
                W("s", "t", ttk)
                st["pb67"][ecnt % 4] = O("s", lambda s_, hl_=hl, sb_=sbk, bk=bnk: s_.copy(
                    out=p2_K[hl_][:, sb_ * 256:(sb_ + 1) * 256], in_=bank(bk)[:, 0:256]),
                    inc="s")
                ecnt += 1
                bnk = 4 + (ecnt % 4)
                W("t", "s", st["pb67"][ecnt % 4])
                for c in range(NCH):
                    ttq = O("t", lambda t_, hl_=hl, c_=c, b=bi, bk=bnk: t_.matmul(
                        out=bank(bk)[:, 0:128],
                        lhsT=p2_wq[hl_][:, c_ * 128:(c_ + 1) * 128],
                        rhs=p2_xn[b][:, c_ * 256:c_ * 256 + 128],
                        start=(c_ == 0), stop=(c_ == NCH - 1), skip_group_check=True),
                        inc="t" if c == NCH - 1 else None)
                W("s", "t", ttq)
                st["pb67"][ecnt % 4] = O("s", lambda s_, hl_=hl, sb_=sbk, bk=bnk: s_.copy(
                    out=p2_Q[hl_][:, sb_ * 128:(sb_ + 1) * 128], in_=bank(bk)[:, 0:128]),
                    inc="s")
                ecnt += 1
                if sbk == NSLAB - 1:
                    st["wkq"][hl] = em.cnt["t"]
            for st4 in range(2):
                kc = sbk * 2 + st4
                bnk = 4 + (ecnt % 4)
                W("t", "s", st["pb67"][ecnt % 4])
                for c in range(NCH):
                    ttv = O("t", lambda t_, s4=st4, c_=c, b=bi, bk=bnk: t_.matmul(
                        out=bank(bk),
                        lhsT=p2_xn[b][:, c_ * 256 + s4 * 128: c_ * 256 + (s4 + 1) * 128],
                        rhs=p2_wv[:, c_ * 512:(c_ + 1) * 512],
                        start=(c_ == 0), stop=(c_ == NCH - 1), skip_group_check=True),
                        inc="t" if c == NCH - 1 else None)
                W("s", "t", ttv)
                st["pb67"][ecnt % 4] = O("s", lambda s_, kc_=kc, bk=bnk: s_.copy(
                    out=p2_V[:, kc_ * 512:(kc_ + 1) * 512], in_=bank(bk)), inc="s")
                ecnt += 1
            st["xn"][bi] = em.cnt["t"]
        st["wv"] = em.cnt["t"]
        t_proj_s = em.cnt["s"]
        W("t", "s", t_proj_s)

        for _i in range(4):
            W("t", "s", st["pb67"][_i])
        pending = [None]

        def flush_pending():
            if pending[0] is not None:
                pending[0]()
                pending[0] = None

        for hl in range(4):
            h = 4 * grp + hl
            for j in range(NQB):
                nch_j = 8 * (j + 1)
                npair = nch_j // 2
                ui = h * NQB + j
                zb = ui % 2
                zbank = 4 if zb == 0 else 7
                W("t", "v", st["z_t"][zb])
                W("t", "s", st["d_s"])

                def sc_pair(mp):
                    pr = mp % 2
                    W("t", "s", st["sc_s"][pr])
                    t2 = None
                    for mm, half in ((2 * mp, 0), (2 * mp + 1, 1)):
                        t2 = O("t", lambda t_, hl_=hl, j_=j, m_=mm, pr_=pr, hf=half: t_.matmul(
                            out=bank(2 * pr_ + hf),
                            lhsT=p2_K[hl_][:, m_ * 128:(m_ + 1) * 128],
                            rhs=p2_Q[hl_][:, j_ * 512:(j_ + 1) * 512],
                            start=True, stop=True, skip_group_check=True),
                            inc="t" if half == 1 else None)
                    return t2

                tsc = [None] * npair
                tsc[0] = sc_pair(0)
                flush_pending()
                tt_pv = 0
                for mp in range(npair):
                    pr = mp % 2
                    W("s", "t", tsc[mp])
                    W("s", "t", st["P_t"][pr])
                    ts_e = O("s", lambda s_, pr_=pr: s_.activation(
                        out=p2_P[pr_][:], in_=bankpair(2 * pr_), func=AF.Exp,
                        bias=c_expb[:], scale=SCALE), inc="s")
                    st["P_s"][pr] = ts_e
                    st["sc_s"][pr] = ts_e
                    if mp + 1 < npair:
                        tsc[mp + 1] = sc_pair(mp + 1)
                    tvm = [None, None]
                    for mm, half in ((2 * mp, 0), (2 * mp + 1, 1)):
                        mprime = mm - 8 * j
                        if mprime >= 0:
                            wgrp = min(mprime // 2 + 1, 4)
                            W("v", "s", ts_e)
                            tvm[half] = O("v", lambda v, pr_=pr, hf=half, mp_=mprime, w=wgrp:
                                          v.tensor_tensor(
                                out=p2_P[pr_][:, hf * 512: hf * 512 + w * 128],
                                in0=p2_P[pr_][:, hf * 512: hf * 512 + w * 128],
                                in1=masks[:, mp_ * 512: mp_ * 512 + w * 128],
                                op=ALU.mult), inc="v")
                    for mm, half in ((2 * mp, 0), (2 * mp + 1, 1)):
                        if tvm[half] is not None:
                            W("t", "v", tvm[half])
                        else:
                            W("t", "s", ts_e)
                        O("t", lambda t_, hl_=hl, m_=mm, pr_=pr, hf=half,
                          s0=(mm == 0), s1=(mm == nch_j - 1), zk=zbank: t_.matmul(
                            out=bank(zk),
                            lhsT=p2_V[:, m_ * 512 + hl_ * 128: m_ * 512 + (hl_ + 1) * 128],
                            rhs=p2_P[pr_][:, hf * 512:(hf + 1) * 512],
                            start=s0, stop=s1, skip_group_check=True))
                        tt_pv = O("t", lambda t_, m_=mm, pr_=pr, hf=half,
                                  s0=(mm == 0), s1=(mm == nch_j - 1): t_.matmul(
                            out=bank(5)[0:1, :], lhsT=ones_b[:],
                            rhs=p2_P[pr_][:, hf * 512:(hf + 1) * 512],
                            start=s0, stop=s1, skip_group_check=True), inc="t")
                    st["P_t"][pr] = tt_pv
                    emit_f0b(1)
                W("s", "t", tt_pv)
                ts_dn = O("s", lambda s_, u=ui % 2: s_.copy(out=p2_dn[u][:],
                                                            in_=bank(5)[0:1, :]), inc="s")
                st["d_s"] = ts_dn

                def mk_norm(h=h, j=j, zb=zb, zbank=zbank, ui=ui, ts_dn=ts_dn):
                    def emit_norm():
                        u2 = ui % 2
                        W("v", "s", ts_dn)
                        tv_rd = O("v", lambda v: v.reciprocal(out=p2_rd[u2][:],
                                                              in_=p2_dn[u2][:]), inc="v")
                        W("t", "v", tv_rd)
                        W("t", "s", st["bc_s"])
                        tt_bc = O("t", lambda t_: t_.matmul(out=bank(6), lhsT=onesrow_f[:],
                                                            rhs=p2_rd[u2][:], start=True,
                                                            stop=True,
                                                            skip_group_check=True), inc="t")
                        W("s", "t", tt_bc)
                        W("s", "v", st["rb_v"][u2])
                        ts_rb = O("s", lambda s_: s_.copy(out=p2_rb[u2][:], in_=bank(6)),
                                  inc="s")
                        st["bc_s"] = ts_rb
                        W("v", "s", ts_rb)
                        W("v", f"sz{zb}", st["z_ld"][zb])
                        tv_z = O("v", lambda v: v.tensor_tensor(
                            out=p2_z[zb][:], in0=bank(zbank), in1=p2_rb[u2][:],
                            op=ALU.mult), inc="v")
                        st["rb_v"][u2] = tv_z
                        st["z_t"][zb] = tv_z
                        W("y", "v", tv_z)
                        st["z_ld"][zb] = DMA("y", z_sc[h * 128:(h + 1) * 128,
                                                       j * 512:(j + 1) * 512],
                                             p2_z[zb][:], f"sz{zb}")
                    return emit_norm
                pending[0] = mk_norm()
        flush_pending()
        st["kqv_t"] = em.cnt["t"]
    emit_f0b(384)
    em.barrier()


def emit_f1z(nc, em, L):
    O, W, DMA = em.op, em.wait, em.dma
    bank = L["bank"]
    f1_wo, f1_z, f1_ao = L["f1_wo"], L["f1_z"], L["f1_ao"]
    woT, z_sc, ao_sc, obc_sb = L["woT"], L["z_sc"], L["ao_sc"], L["obc_sb"]

    g_wo = 0
    g_z = 0
    g_ao_s = [0, 0]
    g_ao_ld = [0, 0]
    for dh in range(2):
        W("g", "t", g_wo)
        two = DMA("g", f1_wo[:].rearrange("ee (h d) -> ee h d", h=H),
            woT[:, :, dh * 1024:(dh + 1) * 1024].rearrange(
            "h ee d -> ee h d"), "lwo")
        for sbk in range(4):
            W("y", "t", g_z)
            tz = DMA("y", f1_z[:].rearrange("ee (h s) -> ee h s", h=H),
                z_sc[:, sbk * 512:(sbk + 1) * 512].rearrange(
                "(h ee) s -> ee h s", ee=128), "lz")
            for dt in range(8):
                bnk = dt % 2
                W("t", "s", g_ao_s[bnk])
                if dt == 0:
                    W("t", "lwo", two)
                    W("t", "lz", tz)
                tt = None
                for h in range(H):
                    tt = O("t", lambda t_, h_=h, dt_=dt, b=bnk: t_.matmul(
                        out=bank(b),
                        lhsT=f1_wo[:, h_ * 1024 + dt_ * 128: h_ * 1024 + (dt_ + 1) * 128],
                        rhs=f1_z[:, h_ * 512:(h_ + 1) * 512],
                        start=(h_ == 0), stop=(h_ == H - 1), skip_group_check=True),
                        inc="t" if h == H - 1 else None)
                W("s", "t", tt)
                W("s", f"sao{bnk}", g_ao_ld[bnk])
                dchunk = dh * 8 + dt
                g_ao_s[bnk] = O("s", lambda s_, b=bnk, dc=dchunk: s_.activation(
                    out=f1_ao[b][:], in_=bank(b), func=AF.Identity,
                    bias=obc_sb[:, dc:dc + 1], scale=1.0), inc="s")
                W("y", "s", g_ao_s[bnk])
                g_ao_ld[bnk] = DMA("y", ao_sc[dchunk * 128:(dchunk + 1) * 128,
                                              sbk * 512:(sbk + 1) * 512], f1_ao[bnk][:],
                                   f"sao{bnk}")
            g_z = em.cnt["t"]
        g_wo = em.cnt["t"]
    em.barrier()


def emit_f2(nc, em, L):
    O, W, DMA = em.op, em.wait, em.dma
    bank = L["bank"]
    f2_xq, f2_gw, f2_vw = L["f2_xq"], L["f2_gw"], L["f2_vw"]
    f2_gbb, f2_vbb = L["f2_gbb"], L["f2_vbb"]
    f2_g1, f2_g2, f2_gs, f2_v1, f2_v2 = L["f2_g1"], L["f2_g2"], L["f2_gs"], L["f2_v1"], L["f2_v2"]
    f2_gvb, f2_am = L["f2_gvb"], L["f2_am"]
    f2_gbr, f2_vbr = L["f2_gbr"], L["f2_vbr"]
    onesrow_f = L["onesrow_f"]
    sc_g, sc_v, gx_all, gh_run = L["sc_g"], L["sc_v"], L["gx_all"], L["gh_run"]
    g127_w, c_zero = L["g127_w"], L["c_zero"]
    c_one, c_mone = L["c_one"], L["c_mone"]
    gw_t8, vw_t8, xq_sc, gv_sc = L["gw_t8"], L["vw_t8"], L["xq_sc"], L["gv_sc"]

    gb_in, vb_in = L["gb_in"], L["vb_in"]
    em.seq("v", lambda v: v.tensor_scalar(out=sc_g[:], in0=gx_all[:], scalar1=g127_w["gw"][:],
                                           scalar2=None, op0=ALU.mult))
    t_scv = O("v", lambda v: v.tensor_scalar(out=sc_v[:], in0=gx_all[:], scalar1=g127_w["vw"][:],
                                             scalar2=None, op0=ALU.mult), inc="v")
    W("s", "v", t_scv)

    txq = DMA("g", f2_xq[:].rearrange("dd (c s) -> dd c s", s=OWN),
              xq_sc[:, :].rearrange("(c dd) s -> dd c s", dd=128), "lxq")

    NHT = HID // 512
    g_w = [0, 0]
    g_psum = [0, 0, 0, 0]
    g_b4 = 0
    g_brow_t = [0, 0]
    g_bias_g = [0, 0]
    g_g1 = [0, 0]
    g_g2 = [0, 0]
    g_gs = [0, 0]
    g_gv_ld = [0, 0]
    for ht in range(NHT):
        bi = ht % 2
        W("g", "t", g_w[bi])
        tw1 = DMA("g", f2_gw[bi][:].rearrange("dd (c hh) -> dd c hh", hh=512),
            gw_t8[:, ht * 512:(ht + 1) * 512].rearrange(
            "(c dd) hh -> dd c hh", dd=128), f"ltw{bi}")
        tw2 = DMA("g", f2_vw[bi][:].rearrange("dd (c hh) -> dd c hh", hh=512),
            vw_t8[:, ht * 512:(ht + 1) * 512].rearrange(
            "(c dd) hh -> dd c hh", dd=128), f"ltw{bi}")
        W("v", f"ltw{bi}", tw2)
        O("v", lambda v, b=bi: v.tensor_scalar(out=f2_gw[b][:], in0=f2_gw[b][:],
                                               scalar1=c_one[:], scalar2=c_mone[:],
                                               op0=ALU.min, op1=ALU.max))
        tv_clip = O("v", lambda v, b=bi: v.tensor_scalar(out=f2_vw[b][:], in0=f2_vw[b][:],
                                                         scalar1=c_one[:], scalar2=c_mone[:],
                                                         op0=ALU.min, op1=ALU.max), inc="v")
        W("y", "t", g_brow_t[bi])
        DMA("y", f2_gbr[bi][:], gb_in[0:1, ht * 512:(ht + 1) * 512], f"lbr{bi}")
        t_brow = DMA("y", f2_vbr[bi][:], vb_in[0:1, ht * 512:(ht + 1) * 512], f"lbr{bi}")
        W("t", "s", g_b4)
        W("t", f"lbr{bi}", t_brow)
        tt_gb = O("t", lambda t_, b=bi: t_.matmul(
            out=bank(4), lhsT=onesrow_f[:], rhs=f2_gbr[b][:],
            start=True, stop=True, skip_group_check=True), inc="t")
        W("s", "t", tt_gb)
        W("s", "g", g_bias_g[bi])
        ts_gb = O("s", lambda s_, b=bi: s_.copy(out=f2_gbb[b][:], in_=bank(4)), inc="s")
        W("t", "s", ts_gb)
        tt_vb = O("t", lambda t_, b=bi: t_.matmul(
            out=bank(4), lhsT=onesrow_f[:], rhs=f2_vbr[b][:],
            start=True, stop=True, skip_group_check=True), inc="t")
        g_brow_t[bi] = tt_vb
        W("s", "t", tt_vb)
        g_b4 = O("s", lambda s_, b=bi: s_.copy(out=f2_vbb[b][:], in_=bank(4)), inc="s")
        ts_vb = g_b4

        for stt in range(NOT):
            pb = stt % 2
            for bb, wbuf in ((0, f2_gw[bi]), (2, f2_vw[bi])):
                bk = bb + pb
                W("t", "s", g_psum[bk])
                if stt == 0 and bb == 0:
                    W("t", "v", tv_clip)
                    if ht == 0:
                        W("t", "lxq", txq)
                    W("t", "s", ts_vb)
                tt = None
                for c in range(NCH):
                    tt = O("t", lambda t_, c_=c, st_=stt, wb=wbuf, bk_=bk: t_.matmul(
                        out=bank(bk_),
                        lhsT=f2_xq[:, c_ * OWN + st_ * 128: c_ * OWN + (st_ + 1) * 128],
                        rhs=wb[:, c_ * 512:(c_ + 1) * 512],
                        start=(c_ == 0), stop=(c_ == NCH - 1), skip_group_check=True),
                        inc="t" if c == NCH - 1 else None)
                if bb == 2 and stt == NOT - 1:
                    g_w[bi] = tt
                if bb == 0:
                    tt_g = tt
                else:
                    tt_v = tt
            W("s", "t", tt_g)
            W("s", "g", g_g1[pb])
            ts_g1 = O("s", lambda s_, st_=stt, bk=0 + pb, p=pb: s_.activation(
                out=f2_g1[p][:], in_=bank(bk), func=AF.Identity, bias=c_zero[:],
                scale=sc_g[:, st_:st_ + 1]), inc="s")
            g_psum[0 + pb] = ts_g1
            W("s", "t", tt_v)
            ts_v1 = O("s", lambda s_, st_=stt, bk=2 + pb, p=pb: s_.activation(
                out=f2_v1[p][:], in_=bank(bk), func=AF.Identity, bias=c_zero[:],
                scale=sc_v[:, st_:st_ + 1]), inc="s")
            g_psum[2 + pb] = ts_v1
            W("g", "s", ts_v1)
            W("g", "s", g_g2[pb])
            tg_g2 = O("g", lambda g_, b=bi, p=pb: g_.tensor_tensor(
                out=f2_g2[p][:], in0=f2_g1[p][:], in1=f2_gbb[b][:], op=ALU.add), inc="g")
            W("g", "v", g_gs[pb])
            tg_v2 = O("g", lambda g_, b=bi, p=pb: g_.tensor_tensor(
                out=f2_v2[p][:], in0=f2_v1[p][:], in1=f2_vbb[b][:], op=ALU.add), inc="g")
            g_g1[pb] = tg_v2
            if stt == NOT - 1:
                g_bias_g[bi] = tg_v2
            W("s", "g", tg_g2)
            W("s", "v", g_gs[pb])
            g_g2[pb] = O("s", lambda s_, p=pb: s_.activation(
                out=f2_gs[p][:], in_=f2_g2[p][:], func=AF.Silu, bias=c_zero[:],
                scale=1.0), inc="s")
            ts_sl = g_g2[pb]
            W("v", "s", ts_sl)
            W("v", "g", tg_v2)
            W("v", f"sgv{pb}", g_gv_ld[pb])
            gvb = f2_gvb[pb]
            tv_gv = em.seq("v", lambda v, gv=gvb, p=pb: v.tensor_tensor(
                out=gv[:], in0=f2_gs[p][:], in1=f2_v2[p][:], op=ALU.mult))
            em.seq("v", lambda v, gv=gvb: v.tensor_reduce(
                out=f2_am[:], in_=gv[:], axis=AX.X,
                op=ALU.max, apply_absolute_value=True))
            tv_am = O("v", lambda v, st_=stt: v.tensor_tensor(
                out=gh_run[:, st_:st_ + 1], in0=gh_run[:, st_:st_ + 1],
                in1=f2_am[:], op=ALU.max), inc="v")
            g_gs[pb] = tv_am
            W("y", "v", tv_gv)
            g_gv_ld[pb] = DMA("y", gv_sc[stt * 128:(stt + 1) * 128,
                                         ht * 512:(ht + 1) * 512], gvb[:], f"sgv{pb}")
    em.barrier()


def emit_f3a(nc, em, L):
    O, W, DMA = em.op, em.wait, em.dma
    bank_bf, bank, bankpair = L["bank_bf"], L["bank"], L["bankpair"]
    ps = L["ps"]
    gh_run, sc_h127, c_geps, c_inv127 = L["gh_run"], L["sc_h127"], L["c_geps"], L["c_inv127"]
    ghrow_t, shrow, g127_w, idn_f, idn_b = L["ghrow_t"], L["shrow"], L["g127_w"], L["idn_f"], L["idn_b"]
    f3_gv, f3_i32, f3_hb, f3_stg = L["f3_gv"], L["f3_i32"], L["f3_hb"], L["f3_stg"]
    gv_sc, hq_sc, row_sc = L["gv_sc"], L["hq_sc"], L["row_sc"]
    c_zero, rc16 = L["c_zero"], L["rc16"]

    em.seq("v", lambda v: v.tensor_scalar(out=gh_run[:], in0=gh_run[:], scalar1=c_geps[:],
                                           scalar2=None, op0=ALU.max))
    em.seq("v", lambda v: v.tensor_scalar(out=rc16[:], in0=gh_run[:], scalar1=c_inv127[:],
                                          scalar2=None, op0=ALU.mult))
    em.seq("v", lambda v: v.reciprocal(out=sc_h127[:], in_=rc16[:]))
    tv1 = O("v", lambda v: v.tensor_copy(out=rc16[:], in_=sc_h127[:]), inc="v")
    W("t", "v", tv1)
    tt1 = O("t", lambda t_: t_.transpose(out=ps[0:NOT, 0:128], in_=gh_run[:],
                                         identity=idn_f[:]), inc="t")
    W("v", "t", tt1)
    tv2 = O("v", lambda v: v.tensor_copy(out=ghrow_t[:], in_=ps[0:NOT, 0:128]), inc="v")
    W("y", "v", tv2)
    td1 = DMA("y", row_sc[:, :].rearrange("one (t ss) -> (one t) ss", ss=128), ghrow_t[:], "lrow")
    W("y", "lrow", td1)
    td2 = DMA("y", shrow[:], row_sc[:, :], "lrow")
    W("s", "lrow", td2)
    ts1 = O("s", lambda s_: s_.activation(out=shrow[:], in_=shrow[:], func=AF.Identity,
                                          bias=c_zero[0:1, :],
                                          scale=g127_w["ow"][0:1, :]), inc="s")
    L.update(f3_shrow_s=ts1)

    g_gv = 0
    g_pb = [0, 0]
    g_stg_ld = 0
    for sbk in range(4):
        for st4 in range(4):
            sti = sbk * 4 + st4
            W("y", "v", g_gv)
            tgv = DMA("y", f3_gv[0][:], gv_sc[sti * 128:(sti + 1) * 128, :], "lgv0")
            W("v", "lgv0", tgv)
            if st4 == 0:
                W("v", "shq", g_stg_ld)
            em.seq("v", lambda v, st_=sti: v.tensor_scalar(
                out=f3_i32[:], in0=f3_gv[0][:], scalar1=sc_h127[:, st_:st_ + 1],
                scalar2=None, op0=ALU.mult))
            tv_hb = O("v", lambda v: v.tensor_copy(out=f3_hb[:], in_=f3_i32[:]), inc="v")
            g_gv = tv_hb
            W("t", "v", tv_hb)
            for cb in range(8):
                bnk = cb % 2
                W("t", "v", g_pb[bnk])
                tt = None
                for k in range(8):
                    c = cb * 8 + k
                    tt = O("t", lambda t_, c_=c, k_=k, b=bnk: t_.transpose(
                        out=bankpair(2 * b)[:, k_ * 128:(k_ + 1) * 128],
                        in_=f3_hb[:, c_ * 128:(c_ + 1) * 128], identity=idn_f[:]),
                        inc="t" if k == 7 else None)
                W("v", "t", tt)
                g_pb[bnk] = O("v", lambda v, cb_=cb, s4=st4, b=bnk: v.tensor_copy(
                    out=f3_stg[:].rearrange("dd (c ss) -> dd c ss", ss=512)[
                        :, cb_ * 8:(cb_ + 1) * 8, s4 * 128:(s4 + 1) * 128],
                    in_=bankpair(2 * b)[:]), inc="v")
        W("y", "v", em.cnt["v"])
        last = 0
        for dmac in range(8):
            last = DMA("y", hq_sc[sbk, dmac * 8:(dmac + 1) * 8, :, :].rearrange(
                "c hh ss -> hh c ss"),
                f3_stg[:, dmac * 8 * 512:(dmac + 1) * 8 * 512].rearrange(
                    "dd (c ss) -> dd c ss", ss=512), "shq")
        g_stg_ld = last
    em.barrier()


def emit_f3b(nc, em, L):
    O, W, DMA = em.op, em.wait, em.dma
    bank = L["bank"]
    f3_hq, f3_ow, f3_x, f3_ao, f3_t, f3_o, f3_sbc = (
        L["f3_hq"], L["f3_ow"], L["f3_x"], L["f3_ao"], L["f3_t"], L["f3_o"], L["f3_sbc"])
    f3_t2 = L["f3_t2"]
    c_one, c_mone = L["c_one"], L["c_mone"]
    onesrow_f, shrow = L["onesrow_f"], L["shrow"]
    hq_sc, ow_t8, xT_own, ao_sc, out_fm = (
        L["hq_sc"], L["ow_t8"], L["xT_own"], L["ao_sc"], L["out_fm"])

    g_hq = 0
    g_ow = [0, 0]
    g_x = [0, 0, 0, 0]
    g_ao = [0, 0, 0, 0]
    g_o = [0, 0, 0, 0]
    g_t2 = [0, 0]
    g_banks_v = 0
    g_sbc_v = 0
    for sbk in range(4):
        W("g", "t", g_hq)
        thq = DMA("g", f3_hq[:].rearrange("hh (c ss) -> hh c ss", ss=512),
              hq_sc[sbk, :, :, :].rearrange("c hh ss -> hh c ss"), "lhq")
        W("t", "v", g_sbc_v)
        tt_sbc = O("t", lambda t_, sb_=sbk: t_.matmul(
            out=bank(0), lhsT=onesrow_f[:], rhs=shrow[0:1, sb_ * 512:(sb_ + 1) * 512],
            start=True, stop=True, skip_group_check=True), inc="t")
        W("s", "t", tt_sbc)
        ts_sbc = O("s", lambda s_: s_.copy(out=f3_sbc[:], in_=bank(0)), inc="s")
        for dh in range(2):
            W("t", "s", ts_sbc)
            W("t", "v", g_banks_v)
            tt_ffn = None
            for grp in range(8):
                bi = grp % 2
                W("g", "t", g_ow[bi])
                tow = DMA("g", f3_ow[bi][:].rearrange("hh (c d) -> hh c d", d=1024),
                    ow_t8[grp * 1024:(grp + 1) * 1024,
                          dh * 1024:(dh + 1) * 1024].rearrange(
                    "(c hh) d -> hh c d", hh=128), f"low{bi}")
                W("v", f"low{bi}", tow)
                tv_owc = O("v", lambda v, b=bi: v.tensor_scalar(
                    out=f3_ow[b][:], in0=f3_ow[b][:], scalar1=c_one[:], scalar2=c_mone[:],
                    op0=ALU.min, op1=ALU.max), inc="v")
                W("t", "v", tv_owc)
                if grp == 0:
                    W("t", "lhq", thq)
                for dt in range(8):
                    for k in range(8):
                        c = grp * 8 + k
                        tt_ffn = O("t", lambda t_, b=bi, k_=k, dt_=dt, c_=c,
                                   s0=(grp == 0 and k == 0), s1=(grp == 7 and k == 7): t_.matmul(
                            out=bank(dt_),
                            lhsT=f3_ow[b][:, k_ * 1024 + dt_ * 128: k_ * 1024 + (dt_ + 1) * 128],
                            rhs=f3_hq[:, c_ * 512:(c_ + 1) * 512],
                            start=s0, stop=s1, skip_group_check=True),
                            inc="t" if (dt == 7 and k == 7) else None)
                g_ow[bi] = tt_ffn
            for dt in range(8):
                dchunk = dh * 8 + dt
                bi2 = dt % 4
                pb2 = dt % 2
                W("y", "g", g_x[bi2])
                tx = DMA("y", f3_x[bi2][:], xT_own[dchunk * 128:(dchunk + 1) * 128,
                                                   sbk * 512:(sbk + 1) * 512], f"lx3{bi2}")
                W("y", "g", g_ao[bi2])
                tao = DMA("y", f3_ao[bi2][:], ao_sc[dchunk * 128:(dchunk + 1) * 128,
                                                    sbk * 512:(sbk + 1) * 512], f"lao{bi2}")
                W("g", f"lao{bi2}", tao)
                W("g", f"lx3{bi2}", tx)
                W("g", "v", g_t2[pb2])
                tg2 = O("g", lambda g_, b=bi2, p=pb2: g_.tensor_tensor(
                    out=f3_t2[p][:], in0=f3_ao[b][:], in1=f3_x[b][:], op=ALU.add), inc="g")
                g_x[bi2] = tg2
                g_ao[bi2] = tg2
                W("v", "t", tt_ffn)
                em.seq("v", lambda v, dt_=dt, p=pb2: v.tensor_tensor(
                    out=f3_t[p][:], in0=bank(dt_), in1=f3_sbc[:], op=ALU.mult))
                W("v", "g", tg2)
                W("v", f"so{bi2}", g_o[bi2])
                tv_o = O("v", lambda v, b=bi2, p=pb2: v.tensor_tensor(
                    out=f3_o[b][:], in0=f3_t[p][:], in1=f3_t2[p][:], op=ALU.add), inc="v")
                g_t2[pb2] = tv_o
                W("y", "v", tv_o)
                g_o[bi2] = DMA("y", out_fm[dchunk * 128:(dchunk + 1) * 128,
                                           sbk * 512:(sbk + 1) * 512], f3_o[bi2][:],
                               f"so{bi2}")
            g_banks_v = em.cnt["v"]
            g_sbc_v = g_banks_v
        g_hq = em.cnt["t"]


def finish_all(nc, em, es):
    finals = {k: v for k, v in em.cnt.items() if k not in ("t", "v", "s", "g") and v > 0}
    for name in em.cnt:
        if name not in em.sems:
            em.sems[name] = es.enter_context(nc.semaphore(name))
    blk = es.enter_context(nc.Block())

    @blk.tensor
    def _(t):
        for f in em.ops["t"]:
            f(t)

    @blk.vector
    def _(v):
        for f in em.ops["v"]:
            f(v)

    @blk.scalar
    def _(s):
        for f in em.ops["s"]:
            f(s)

    @blk.gpsimd
    def _(g):
        for f in em.ops["g"]:
            f(g)

    @blk.sync
    def _(y):
        for f in em.ops["y"]:
            f(y)
        for name, v in finals.items():
            y.wait_ge(em.sems[name], v)

    es.close()


_CACHE = {}


def _get_nc():
    if "nc" not in _CACHE:
        _CACHE["nc"] = build_program()
    return _CACHE["nc"]


def kernel(L0_attn_x, L0_attn_W_Q, L0_attn_W_K, L0_attn_W_V, L0_attn_W_O,
           L0_ffn_gate_w, L0_ffn_gate_b, L0_ffn_val_w, L0_ffn_val_b,
           L0_ffn_out_w, L0_ffn_out_b):
    x = np.asarray(L0_attn_x, dtype=np.float32)
    wq = np.ascontiguousarray(np.asarray(L0_attn_W_Q, np.float32).transpose(0, 2, 1))
    wk = np.ascontiguousarray(np.asarray(L0_attn_W_K, np.float32).transpose(0, 2, 1))
    wv = np.ascontiguousarray(np.asarray(L0_attn_W_V, np.float32).transpose(0, 2, 1))
    wo = np.ascontiguousarray(np.asarray(L0_attn_W_O, np.float32).transpose(0, 2, 1))
    gw = np.ascontiguousarray(np.asarray(L0_ffn_gate_w, np.float32).T)
    vw = np.ascontiguousarray(np.asarray(L0_ffn_val_w, np.float32).T)
    ow = np.ascontiguousarray(np.asarray(L0_ffn_out_w, np.float32).T)
    gb = np.ascontiguousarray(np.asarray(L0_ffn_gate_b, np.float32).reshape(1, HID))
    vb = np.ascontiguousarray(np.asarray(L0_ffn_val_b, np.float32).reshape(1, HID))
    obc = np.ascontiguousarray(np.asarray(L0_ffn_out_b, np.float32).reshape(NCH, 128).T)
    iden_bf = np.eye(128, dtype=ml_dtypes.bfloat16)
    iden_f = np.eye(128, dtype=np.float32)

    in_maps = []
    for c in range(NCORES):
        b, r = c // 2, c % 2
        perm = np.arange(NT)
        if r == 1:
            perm = perm ^ 1
        xb = np.ascontiguousarray(
            x[b].reshape(NT, 128, D)[perm].reshape(S, D))
        own_tiles = [2 * t + r for t in range(NOT)]
        own_cols = np.concatenate([np.arange(g_ * 128, (g_ + 1) * 128) for g_ in own_tiles])
        xT = np.ascontiguousarray(x[b].T[:, own_cols])
        masks = np.zeros((8, 128, 512), dtype=ml_dtypes.bfloat16)
        kk = np.arange(128)[:, None]
        qq = np.arange(128)[None, :]
        for m in range(8):
            gk = (m + r) if m % 2 == 0 else (m - r)
            for g_ in range(4):
                masks[m][:, g_ * 128:(g_ + 1) * 128] = (
                    ((2 * g_ + r - gk) * 128 + qq) >= kk).astype(ml_dtypes.bfloat16)
        in_maps.append({
            "x_tm": xb, "xT_own": xT, "wqT": wq, "wkT": wk, "wvT": wv, "woT": wo,
            "gwT": gw, "vwT": vw, "owT": ow, "gb_in": gb, "vb_in": vb, "obc_in": obc,
            "iden_bf_in": iden_bf, "iden_f_in": iden_f, "masks_in": masks,
        })

    nc = _get_nc()
    res = run_bass_kernel_spmd(nc, in_maps, core_ids=list(range(NCORES)))

    out = np.empty((B, S, D), dtype=np.float32)
    for c in range(NCORES):
        b, r = c // 2, c % 2
        o = np.asarray(res.results[c]["out_fm"])
        oT = np.ascontiguousarray(o.T)
        for t in range(NOT):
            g_ = 2 * t + r
            out[b, g_ * 128:(g_ + 1) * 128, :] = oT[t * 128:(t + 1) * 128, :]
    return out


if __name__ == "__main__":
    nc = build_program()
    print("build ok")


# version 18
# speedup vs baseline: 1.0487x; 1.0015x over previous
import math
import numpy as np
import ml_dtypes
import concourse.bass as bass
import concourse.mybir as mybir
from concourse.bass_utils import run_bass_kernel_spmd
from contextlib import ExitStack

F32 = mybir.dt.float32
BF16 = mybir.dt.bfloat16
I32 = mybir.dt.int32
I8 = mybir.dt.int8
AF = mybir.ActivationFunctionType
ALU = mybir.AluOpType
AX = mybir.AxisListType

B, S, D = 4, 4096, 2048
H, E = 16, 128
HID = 8192
NCORES = 8
OWN = 2048
NT = S // 128
NOT = OWN // 128
NCH = D // 128
NHCH = HID // 128
NQB = 4
SCALE = 1.0 / math.sqrt(E)
EXP_BIAS = -12.0

PHASES = 6
DBG_SCRATCH = False


class Emitter:
    ENGS = ("t", "v", "s", "g", "y")

    def __init__(self):
        self.ops = {k: [] for k in self.ENGS}
        self.cnt = {}
        self.sems = {}

    def new(self, name):
        self.cnt[name] = 0

    def wait(self, eng, sem, val):
        if val and val > 0:
            self.ops[eng].append(lambda e, s=sem, v=val: e.wait_ge(self.sems[s], v))

    def op(self, eng, fn, inc=None, by=1):
        tgt = None
        if inc is not None:
            self.cnt[inc] += by
            tgt = self.cnt[inc]
            self.ops[eng].append(lambda e, f=fn, s=inc, b=by: f(e).then_inc(self.sems[s], b))
        else:
            self.ops[eng].append(lambda e, f=fn: f(e))
        return tgt

    def dma(self, eng, out, in_, sem):
        if sem not in self.cnt:
            self.cnt[sem] = 0
        self.cnt[sem] += 16
        tgt = self.cnt[sem]
        self.ops[eng].append(lambda e, o=out, i=in_, s=sem:
                             e.dma_start(out=o, in_=i).then_inc(self.sems[s], 16))
        return tgt

    def seq(self, eng, fn):
        t = self.op(eng, fn, inc=eng)
        self.wait(eng, eng, t)
        return t

    def barrier(self):
        snap = dict(self.cnt)
        for eng in self.ENGS:
            for s, v in snap.items():
                self.wait(eng, s, v)


def build_program():
    nc = bass.Bass("TRN2", target_bir_lowering=False)

    x_tm = nc.dram_tensor("x_tm", [S, D], F32, kind="ExternalInput").ap()
    xT_own = nc.dram_tensor("xT_own", [D, OWN], F32, kind="ExternalInput").ap()
    wqT = nc.dram_tensor("wqT", [H, D, E], F32, kind="ExternalInput").ap()
    wkT = nc.dram_tensor("wkT", [H, D, E], F32, kind="ExternalInput").ap()
    wvT = nc.dram_tensor("wvT", [H, D, E], F32, kind="ExternalInput").ap()
    woT = nc.dram_tensor("woT", [H, E, D], F32, kind="ExternalInput").ap()
    gwT = nc.dram_tensor("gwT", [D, HID], F32, kind="ExternalInput").ap()
    vwT = nc.dram_tensor("vwT", [D, HID], F32, kind="ExternalInput").ap()
    owT = nc.dram_tensor("owT", [HID, D], F32, kind="ExternalInput").ap()
    gb_in = nc.dram_tensor("gb_in", [1, HID], F32, kind="ExternalInput").ap()
    vb_in = nc.dram_tensor("vb_in", [1, HID], F32, kind="ExternalInput").ap()
    obc_in = nc.dram_tensor("obc_in", [128, NCH], F32, kind="ExternalInput").ap()
    iden_bf_in = nc.dram_tensor("iden_bf_in", [128, 128], BF16, kind="ExternalInput").ap()
    iden_f_in = nc.dram_tensor("iden_f_in", [128, 128], F32, kind="ExternalInput").ap()
    masks_in = nc.dram_tensor("masks_in", [8, 128, 512], BF16, kind="ExternalInput").ap()

    out_fm = nc.dram_tensor("out_fm", [D, OWN], F32, kind="ExternalOutput").ap()

    sk = "ExternalOutput" if DBG_SCRATCH else "Internal"
    xn_sc = nc.dram_tensor("xn_sc", [D, S], BF16, kind=sk).ap()
    xq_sc = nc.dram_tensor("xq_sc", [D, OWN], BF16, kind=sk).ap()
    z_sc = nc.dram_tensor("z_sc", [H * E, OWN], BF16, kind=sk).ap()
    ao_sc = nc.dram_tensor("ao_sc", [D, OWN], F32, kind=sk).ap()
    gw_t8 = nc.dram_tensor("gw_t8", [D, HID], I8, kind=sk).ap()
    vw_t8 = nc.dram_tensor("vw_t8", [D, HID], I8, kind=sk).ap()
    ow_t8 = nc.dram_tensor("ow_t8", [HID, D], I8, kind=sk).ap()
    gv_sc = nc.dram_tensor("gv_sc", [OWN, HID], F32, kind=sk).ap()
    hq_sc = nc.dram_tensor("hq_sc", [4, NHCH, 128, 512], I8, kind=sk).ap()
    row_sc = nc.dram_tensor("row_sc", [1, OWN], F32, kind="Internal").ap()
    xn_tm_dbg = nc.dram_tensor("xn_tm_dbg", [S, D], BF16, kind=sk).ap() if DBG_SCRATCH else None
    st_dbg = nc.dram_tensor("st_dbg", [NT, 128, 8], F32, kind=sk).ap() if DBG_SCRATCH else None

    em = Emitter()
    for s in ("t", "v", "s", "g"):
        em.new(s)

    es = ExitStack()
    e = es.enter_context

    def sb(name, cols, dtype, parts=128):
        return e(nc.sbuf_tensor(name, [parts, cols], dtype))

    idn_b = sb("idn_b", 128, BF16)
    idn_f = sb("idn_f", 128, F32)
    masks = sb("masks", 8 * 512, BF16)
    ones_b = sb("ones_b", 1, BF16)
    onesrow_f = sb("onesrow_f", 128, F32, parts=1)
    csc = sb("csc", 16, F32)
    ones_f = csc[:, 0:1]
    c_zero = csc[:, 1:2]
    c_one = csc[:, 2:3]
    c_mone = csc[:, 3:4]
    c_eps = csc[:, 4:5]
    c_geps = csc[:, 5:6]
    c_invD = csc[:, 6:7]
    c_inv127 = csc[:, 7:8]
    c_expb = csc[:, 8:9]
    rg_w = {"gw": csc[:, 9:10], "vw": csc[:, 10:11], "ow": csc[:, 11:12]}
    g127_w = {"gw": csc[:, 12:13], "vw": csc[:, 13:14], "ow": csc[:, 14:15]}
    obc_sb = sb("obc_sb", NCH, F32)
    gsc = sb("gsc", 6 * NOT + 2, F32)
    gx_all = gsc[:, 0 * NOT:1 * NOT]
    gxr127 = gsc[:, 1 * NOT:2 * NOT]
    gh_run = gsc[:, 2 * NOT:3 * NOT]
    sc_g = gsc[:, 3 * NOT:4 * NOT]
    sc_v = gsc[:, 4 * NOT:5 * NOT]
    sc_h127 = gsc[:, 5 * NOT:6 * NOT]
    f0tmp_a = gsc[:, 6 * NOT:6 * NOT + 1]
    f0tmp_b = gsc[:, 6 * NOT + 1:6 * NOT + 2]
    ghrow_t = sb("ghrow_t", 128, F32, parts=NOT)
    shrow = sb("shrow", OWN, F32, parts=1)
    wsum = sb("wsum", 384, F32)
    gtmp = sb("gtmp", 8, F32, parts=1)
    rca = sb("rca", 16, F32)
    rcb = sb("rcb", 16, F32)
    rc16 = sb("rc16", NOT, F32)

    ARENA = 192320
    AR = e(nc.sbuf_tensor("AR", [128, ARENA], mybir.dt.uint8))
    AR_ADDR = nc.lookup_mloc(AR).addr
    _cvn = [0]

    class Carver:

        def __init__(self, base, limit):
            self.off, self.limit = base, limit

        def take(self, cols, dtype, parts=128):
            nb = cols * mybir.dt.size(dtype)
            off = (self.off + 63) // 64 * 64
            self.off = off + nb
            assert self.off <= self.limit, (off, nb, self.limit)
            _cvn[0] += 1
            h = nc.alloc_sbuf_tensor_at(f"cv{_cvn[0]}", [parts, cols], dtype,
                                        offset=AR_ADDR + off)
            return h.ap()

    F0_SZ = 22528
    NF0 = 3
    cf0 = Carver(0, F0_SZ)
    FTW = 1024
    f0_in = [cf0.take(FTW, F32) for i in range(NF0)]
    f0_t1 = [cf0.take(FTW, F32) for i in range(1)]
    f0_ab = cf0.take(FTW, BF16)
    f0_i8 = [cf0.take(FTW, I8) for i in range(NF0)]

    c1 = Carver(F0_SZ, ARENA)
    p1_x = [c1.take(D, F32) for i in range(2)]
    p1_sq = [c1.take(D, F32) for i in range(2)]
    p1_nf = [c1.take(D, F32) for i in range(2)]
    p1_i32 = c1.take(D, I32)
    p1_qb = c1.take(D, F32)
    p1_st = [c1.take(NCH * 128, BF16) for i in range(2)]
    p1_stq = c1.take(NCH * 128, BF16)
    p1_m = [c1.take(1, F32) for i in range(2)]
    p1_m2 = [c1.take(1, F32) for i in range(2)]
    p1_v = [c1.take(1, F32) for i in range(2)]
    p1_sd = [c1.take(1, F32) for i in range(2)]
    p1_rs = [c1.take(1, F32) for i in range(2)]
    p1_g0 = c1.take(1, F32)

    c2 = Carver(F0_SZ, ARENA)
    p2_xn = [c2.take(NCH * 256, BF16) for i in range(2)]
    p2_K = [c2.take(S, BF16) for h in range(4)]
    p2_Q = [c2.take(OWN, BF16) for h in range(4)]
    p2_V = c2.take(NT * 512, BF16)
    p2_wk = [c2.take(NCH * 128, BF16) for h in range(4)]
    p2_wq = [c2.take(NCH * 128, BF16) for h in range(4)]
    p2_wv = c2.take(NCH * 512, BF16)
    p2_P = [c2.take(1024, BF16) for i in range(2)]
    p2_z = [c2.take(512, BF16) for i in range(2)]
    p2_rd = [c2.take(512, F32, parts=1) for i in range(2)]
    p2_dn = [c2.take(512, F32, parts=1) for i in range(2)]
    p2_rb = [c2.take(512, F32) for i in range(2)]

    c3 = Carver(0, ARENA)
    f1_wo = c3.take(H * 1024, BF16)
    f1_z = c3.take(H * 512, BF16)
    f1_ao = [c3.take(512, F32) for i in range(2)]

    c4 = Carver(0, ARENA)
    f2_xq = c4.take(NCH * OWN, BF16)
    f2_gw = [c4.take(NCH * 512, BF16) for i in range(2)]
    f2_vw = [c4.take(NCH * 512, BF16) for i in range(2)]
    f2_gbb = [c4.take(512, F32) for i in range(2)]
    f2_vbb = [c4.take(512, F32) for i in range(2)]
    f2_g1 = [c4.take(512, F32) for i in range(2)]
    f2_g2 = [c4.take(512, F32) for i in range(2)]
    f2_gs = [c4.take(512, F32) for i in range(2)]
    f2_v1 = [c4.take(512, F32) for i in range(2)]
    f2_v2 = [c4.take(512, F32) for i in range(2)]
    f2_gvb = [c4.take(512, F32) for i in range(2)]
    f2_am = c4.take(1, F32)
    f2_gbr = [c4.take(512, F32, parts=1) for i in range(2)]
    f2_vbr = [c4.take(512, F32, parts=1) for i in range(2)]

    c5 = Carver(0, ARENA)
    f3_gv = [c5.take(HID, F32) for i in range(1)]
    f3_i32 = c5.take(HID, I32)
    f3_hb = c5.take(HID, F32)
    f3_stg = c5.take(NHCH * 512, I8)

    c6 = Carver(0, ARENA)
    f3_hq = c6.take(NHCH * 512, BF16)
    f3_ow = [c6.take(8 * 1024, BF16) for i in range(2)]
    f3_x = [c6.take(512, F32) for i in range(2)]
    f3_ao = [c6.take(512, F32) for i in range(2)]
    f3_t = [c6.take(512, F32) for i in range(2)]
    f3_o = [c6.take(512, F32) for i in range(2)]
    f3_sbc = c6.take(512, F32)

    ps = e(nc.psum_tensor("ps", [128, 4096], F32))

    def bank(i):
        return ps[:, 512 * i:512 * (i + 1)]

    def bankpair(i):
        return ps[:, 512 * i:512 * (i + 2)]

    def bank_bf(i):
        return ps[:, 512 * i:512 * (i + 1)].bitcast(BF16)

    em.alloc_sems = lambda: None

    O, W, DMA = em.op, em.wait, em.dma
    SQ = em.seq

    DMA("y", idn_b[:], iden_bf_in[:], "lset")
    DMA("y", idn_f[:], iden_f_in[:], "lset")
    DMA("y", masks[:].rearrange("k (m q) -> k m q", m=8),
        masks_in[:].rearrange("m k q -> k m q"), "lset")
    t_setup = DMA("y", obc_sb[:], obc_in[:], "lset")
    for tile, val in ((ones_b, 1.0), (ones_f, 1.0), (onesrow_f, 1.0), (c_zero, 0.0),
                      (c_one, 1.0), (c_mone, -1.0), (c_eps, 1e-5), (c_geps, 1e-5),
                      (c_invD, 1.0 / D), (c_inv127, 1.0 / 127.0), (c_expb, EXP_BIAS)):
        O("v", lambda v, tl=tile, vv=val: v.memset(tl[:], vv))
    O("v", lambda v: v.memset(gh_run[:], 0.0))
    t_msets = SQ("v", lambda v: v.memset(wsum[:], 0.0))
    for _e in ("t", "s", "g", "y"):
        W(_e, "v", t_msets)
    W("t", "lset", t_setup)
    W("s", "lset", t_setup)

    f0_tiles = []
    for m, ap_ in (("gw", gwT), ("vw", vwT), ("ow", owT)):
        nr = NCH if m != "ow" else NHCH
        ncols = HID if m != "ow" else D
        for ct in range(ncols // FTW):
            for rt in range(nr):
                f0_tiles.append((m, ap_, rt, ct, FTW))
    assert len(f0_tiles) == 384
    f0 = {"i": 0, "guard": [("s", 0)] * NF0}

    def emit_f0(n):
        for _ in range(n):
            i = f0["i"]
            if i >= len(f0_tiles):
                return
            f0["i"] += 1
            m, ap_, rt, ct, tw = f0_tiles[i]
            buf = f0_in[i % NF0]
            eng = "s" if i % 2 == 0 else "v"
            ge, gc = f0["guard"][i % NF0]
            W("y", ge, gc)
            td = DMA("y", buf[:, 0:tw], ap_[rt * 128:(rt + 1) * 128, ct * tw:(ct + 1) * tw],
                     f"lf{i % NF0}")
            W(eng, f"lf{i % NF0}", td)
            if eng == "s":
                W("s", "s", em.cnt["s"])
                f0["guard"][i % NF0] = ("s", O("s", lambda s_, b=buf, tw_=tw, col=i:
                                               s_.activation(
                    out=f0_ab[:, 0:tw_], in_=b[:, 0:tw_], func=AF.Abs, bias=c_zero[:],
                    scale=1.0, accum_out=wsum[:, col:col + 1]), inc="s"))
            else:
                f0["guard"][i % NF0] = ("v", O("v", lambda v, b=buf, tw_=tw, col=i:
                                               v.tensor_reduce(
                    out=wsum[:, col:col + 1], in_=b[:, 0:tw_], axis=AX.X, op=ALU.add,
                    apply_absolute_value=True), inc="v"))

    def finish_f0():
        emit_f0(len(f0_tiles))
        W("v", "s", em.cnt["s"])
        W("v", "v", em.cnt["v"])
        spans = {"gw": [], "vw": [], "ow": []}
        for i, (m, *_r) in enumerate(f0_tiles):
            spans[m].append(i)
        for mi, m in enumerate(("gw", "vw", "ow")):
            lo, hi = min(spans[m]), max(spans[m]) + 1
            em.seq("v", lambda v, lo=lo, hi=hi: v.tensor_reduce(
                out=f0tmp_a[:], in_=wsum[:, lo:hi], axis=AX.X, op=ALU.add))
            tv = O("v", lambda v: v.tensor_copy(out=f0tmp_b[:], in_=f0tmp_a[:]), inc="v")
            W("t", "v", tv)
            tt = O("t", lambda t, b=mi: t.matmul(out=bank(7)[0:1, b:b + 1], lhsT=ones_f[:],
                                                 rhs=f0tmp_b[:], start=True, stop=True,
                                                 skip_group_check=True), inc="t")
            W("s", "t", tt)
            ts1 = O("s", lambda s_, b=mi: s_.activation(
                out=gtmp[0:1, 2 * b:2 * b + 1], in_=bank(7)[0:1, b:b + 1], func=AF.Identity,
                bias=c_zero[0:1, :], scale=1.0 / (D * HID)), inc="s")
            W("v", "s", ts1)
            em.seq("v", lambda v, b=mi: v.tensor_scalar(
                out=rca[0:1, 0:1], in0=gtmp[0:1, 2 * b:2 * b + 1],
                scalar1=c_geps[0:1, :], scalar2=None, op0=ALU.max))
            em.seq("v", lambda v: v.reciprocal(out=rcb[0:1, 0:1], in_=rca[0:1, 0:1]))
            em.seq("v", lambda v, b=mi: v.tensor_copy(
                out=gtmp[0:1, 2 * b + 1:2 * b + 2], in_=rcb[0:1, 0:1]))
            tv2 = O("v", lambda v, b=mi: v.tensor_scalar(
                out=gtmp[0:1, 2 * b:2 * b + 1], in0=rca[0:1, 0:1],
                scalar1=c_inv127[0:1, :], scalar2=None, op0=ALU.mult), inc="v")
            W("t", "v", tv2)
            tt2 = O("t", lambda t, b=mi: t.matmul(
                out=bank(7)[:, 8 + 2 * b:8 + 2 * b + 2], lhsT=onesrow_f[:],
                rhs=gtmp[0:1, 2 * b:2 * b + 2], start=True, stop=True,
                skip_group_check=True), inc="t")
            W("s", "t", tt2)
            O("s", lambda s_, m_=m, b=mi: s_.copy(out=g127_w[m_][:],
                                                  in_=bank(7)[:, 8 + 2 * b:8 + 2 * b + 1]))
            O("s", lambda s_, m_=m, b=mi: s_.copy(out=rg_w[m_][:],
                                                  in_=bank(7)[:, 8 + 2 * b + 1:8 + 2 * b + 2]),
              inc="s")
        return em.cnt["s"]

    f0b_tiles = []
    for m, ap_, o8 in (("gw", gwT, gw_t8), ("vw", vwT, vw_t8), ("ow", owT, ow_t8)):
        nr = NCH if m != "ow" else NHCH
        ncols = HID if m != "ow" else D
        for ct in range(ncols // FTW):
            for rt in range(nr):
                f0b_tiles.append((m, ap_, o8, rt, ct, FTW))
    assert len(f0b_tiles) == 384
    f0b = {"i": 0, "gv": [0] * NF0, "gd": [0] * NF0, "gamma": 0, "f0sg": [0] * NF0,
           "t1g": [0, 0]}

    def emit_f0b(n):
        for _ in range(n):
            i = f0b["i"]
            if i >= len(f0b_tiles):
                return
            f0b["i"] += 1
            m, ap_, o8, rt, ct, tw = f0b_tiles[i]
            buf, ob = f0_in[i % NF0], f0_i8[i % NF0]
            if i < NF0:
                ge, gc = f0["guard"][i % NF0]
                W("y", ge, gc)
            W("y", "v", f0b["gv"][i % NF0])
            td = DMA("y", buf[:, 0:tw], ap_[rt * 128:(rt + 1) * 128, ct * tw:(ct + 1) * tw],
                     f"lf{i % NF0}")
            W("v", f"lf{i % NF0}", td)
            W("v", "s", f0b["gamma"])
            W("v", f"sf{i % NF0}", f0b["gd"][i % NF0])
            tv = O("v", lambda v, b=buf, ob_=ob, tw_=tw, m_=m: v.tensor_scalar(
                out=ob_[:, 0:tw_], in0=b[:, 0:tw_], scalar1=rg_w[m_][:],
                scalar2=None, op0=ALU.mult), inc="v")
            f0b["gv"][i % NF0] = tv
            W("y", "v", tv)
            f0b["gd"][i % NF0] = DMA(
                "y", o8[rt * 128:(rt + 1) * 128, ct * tw:(ct + 1) * tw], ob[:, 0:tw],
                f"sf{i % NF0}")

    g_xbuf = [0, 0]
    g_nb = [0, 0]
    g_qb = 0
    g_bank = [0, 0]
    g_stdma = [0, 0]
    g_stqdma = 0
    g_sv = [0, 0]
    own_idx = 0
    t_p1_ld = 0
    for t in range(NT):
        bi = t % 2
        xbuf, nbuf, stg = p1_x[bi], p1_nf[bi], p1_st[bi]
        is_own = (t % 2 == 0)
        W("y", "v", g_xbuf[bi])
        td = DMA("y", xbuf[:], x_tm[t * 128:(t + 1) * 128, :], f"lx{bi}")
        W("s", f"lx{bi}", td)
        W("s", "v", g_sv[bi])
        ts_sq = O("s", lambda s_, xb=xbuf, b=bi: s_.activation(
            out=p1_sq[b][:], in_=xb[:], func=AF.Square, bias=c_zero[:], scale=1.0), inc="s")
        W("v", f"lx{bi}", td)
        W("v", "s", ts_sq)
        O("v", lambda v, xb=xbuf, b=bi: v.tensor_reduce(out=p1_m[b][:], in_=xb[:],
                                                        axis=AX.X, op=ALU.add))
        O("v", lambda v, b=bi: v.tensor_reduce(out=p1_m2[b][:], in_=p1_sq[b][:],
                                               axis=AX.X, op=ALU.add))
        O("v", lambda v, b=bi: v.tensor_scalar(out=p1_m[b][:], in0=p1_m[b][:],
                                               scalar1=c_invD[:], scalar2=None, op0=ALU.mult))
        SQ("v", lambda v, b=bi: v.tensor_scalar(out=p1_m2[b][:], in0=p1_m2[b][:],
                                                scalar1=c_invD[:], scalar2=None, op0=ALU.mult))
        tv_var = O("v", lambda v, b=bi: v.scalar_tensor_tensor(out=p1_v[b][:], in0=p1_m[b][:],
                                                               scalar=p1_m[b][:], in1=p1_m2[b][:],
                                                               op0=ALU.mult, op1=ALU.subtract),
                   inc="v")
        W("s", "v", tv_var)
        ts_sd = O("s", lambda s_, b=bi: s_.activation(out=p1_sd[b][:], in_=p1_v[b][:],
                                                      func=AF.Sqrt,
                                                      bias=c_eps[:], scale=-1.0), inc="s")
        W("v", "s", ts_sd)
        g_sv[bi] = SQ("v", lambda v, b=bi: v.reciprocal(out=p1_rs[b][:], in_=p1_sd[b][:]))
        W("v", "t", g_nb[bi])
        tv_nb = O("v", lambda v, xb=xbuf, nb=nbuf, b=bi: v.tensor_scalar(
            out=nb[:], in0=xb[:], scalar1=p1_m[b][:], scalar2=p1_rs[b][:],
            op0=ALU.subtract, op1=ALU.mult), inc="v")
        tv_done = tv_nb
        if is_own:
            oi = own_idx
            O("v", lambda v, nb=nbuf: v.tensor_reduce(out=p1_g0[:], in_=nb[:], axis=AX.X,
                                                      op=ALU.max, apply_absolute_value=True))
            O("v", lambda v, o=oi: v.tensor_scalar(out=gx_all[:, o:o + 1], in0=p1_g0[:],
                                                   scalar1=c_geps[:], scalar2=None, op0=ALU.max))
            O("v", lambda v: v.tensor_scalar(out=p1_g0[:], in0=p1_g0[:],
                                             scalar1=c_geps[:], scalar2=None, op0=ALU.max))
            SQ("v", lambda v: v.tensor_scalar(out=p1_g0[:], in0=p1_g0[:],
                                              scalar1=c_inv127[:], scalar2=None,
                                              op0=ALU.mult))
            SQ("v", lambda v: v.reciprocal(out=rca[:, 0:1], in_=p1_g0[:]))
            O("v", lambda v, o=oi: v.tensor_copy(out=gxr127[:, o:o + 1], in_=rca[:, 0:1]))
            SQ("v", lambda v, nb=nbuf: v.tensor_scalar(out=p1_i32[:], in0=nb[:],
                                                       scalar1=rca[:, 0:1],
                                                       scalar2=None, op0=ALU.mult))
            W("v", "t", g_qb)
            tv_done = O("v", lambda v: v.tensor_copy(out=p1_qb[:], in_=p1_i32[:]), inc="v")
        g_xbuf[bi] = tv_done
        if DBG_SCRATCH:
            W("y", "v", tv_done)
            DMA("y", xn_tm_dbg[t * 128:(t + 1) * 128, :], nbuf[:], "dbg")
            for di, (dst, src_) in enumerate([(st_dbg[t, :, 0:1], p1_m), (st_dbg[t, :, 1:2], p1_m2),
                                              (st_dbg[t, :, 2:3], p1_v), (st_dbg[t, :, 3:4], p1_sd),
                                              (st_dbg[t, :, 4:5], p1_rs)]):
                if "dbg" not in em.cnt:
                    em.cnt["dbg"] = 0
                em.cnt["dbg"] += 16
                em.ops["y"].append(lambda e, o=dst, i=src_, s="dbg": e.dma_start(
                    out=o, in_=i[:], allow_slow_non_contiguous=True).then_inc(em.sems[s], 16))

        srcs = [(nbuf, 0, stg, 0), (nbuf, 8, stg, 1)]
        if is_own:
            srcs += [(p1_qb, 0, p1_stq, 0), (p1_qb, 8, p1_stq, 1)]
        W("t", "v", tv_done)
        for ci, (srcbuf, c0, dstg, half) in enumerate(srcs):
            bnk = ci % 2
            W("t", "s", g_bank[bnk])
            tt = None
            for k in range(8):
                tt = O("t", lambda t_, sb_=srcbuf, c=c0, k_=k, b=bnk: t_.transpose(
                    out=bankpair(2 * b)[:, k_ * 128:(k_ + 1) * 128],
                    in_=sb_[:, (c + k_) * 128:(c + k_ + 1) * 128],
                    identity=idn_f[:]), inc="t" if k == 7 else None)
            if ci == 1:
                g_nb[bi] = tt
            if ci == 3:
                g_qb = tt
            W("s", "t", tt)
            W("s", "sstq" if dstg is p1_stq else f"sst{bi}",
              g_stqdma if dstg is p1_stq else g_stdma[bi])
            tv_ev = O("s", lambda s_, b=bnk, h=half, d=dstg: s_.copy(
                out=d[:, h * 1024:(h + 1) * 1024], in_=bankpair(2 * b)[:]), inc="s")
            g_bank[bnk] = tv_ev
            if ci == 1:
                W("y", "s", tv_ev)
                g_stdma[bi] = DMA("y", xn_sc[:, t * 128:(t + 1) * 128].rearrange(
                    "(c dd) s -> dd c s", dd=128),
                    stg[:].rearrange("dd (c s) -> dd c s", s=128), f"sst{bi}")
                t_p1_ld = g_stdma[bi]
            if ci == 3:
                W("y", "s", tv_ev)
                g_stqdma = DMA("y", xq_sc[:, own_idx * 128:(own_idx + 1) * 128].rearrange(
                    "(c dd) s -> dd c s", dd=128),
                    p1_stq[:].rearrange("dd (c s) -> dd c s", s=128), "sstq")
                t_p1_ld = g_stqdma
        if is_own:
            own_idx += 1
        emit_f0(12)
    assert own_idx == NOT

    L = dict(locals())
    if PHASES >= 2:
        emit_p2(nc, em, L)
    if PHASES >= 3:
        emit_f1z(nc, em, L)
    if PHASES >= 4:
        emit_f2(nc, em, L)
    if PHASES >= 5:
        emit_f3a(nc, em, L)
    if PHASES >= 6:
        emit_f3b(nc, em, L)

    finish_all(nc, em, es)
    return nc


def emit_p2(nc, em, L):
    O, W, DMA = em.op, em.wait, em.dma
    g = L
    bank, bankpair, bank_bf = g["bank"], g["bankpair"], g["bank_bf"]
    p2_xn, p2_K, p2_Q, p2_V = g["p2_xn"], g["p2_K"], g["p2_Q"], g["p2_V"]
    p2_wk, p2_wq, p2_wv = g["p2_wk"], g["p2_wq"], g["p2_wv"]
    p2_P, p2_z, p2_rd, p2_rb = g["p2_P"], g["p2_z"], g["p2_rd"], g["p2_rb"]
    p2_dn = g["p2_dn"]
    masks, ones_b, onesrow_f = g["masks"], g["ones_b"], g["onesrow_f"]
    c_expb = g["c_expb"]
    wkT, wqT, wvT = g["wkT"], g["wqT"], g["wvT"]
    xn_sc, z_sc = g["xn_sc"], g["z_sc"]
    emit_f0, finish_f0, emit_f0b = g["emit_f0"], g["finish_f0"], g["emit_f0b"]
    f0b = g["f0b"]
    t_p1_ld = g["t_p1_ld"]
    t_setup = g["t_setup"]

    st = {
        "xn": [0, 0],
        "wkq": [0, 0, 0, 0],
        "wv": 0,
        "kqv_t": 0,
        "pb67": [0, 0, 0, 0],
        "P_s": [0, 0],
        "P_t": [0, 0],
        "sc_s": [0, 0],
        "z_t": [0, 0],
        "d_s": 0,
        "rb_s": [0, 0],
        "rb_v": [0, 0],
        "z_ld": [0, 0],
        "bc_s": 0,
    }
    W("v", "lset", t_setup)
    em.barrier()
    f0b["gamma"] = finish_f0()

    for grp in range(4):
        W("s", "t", st["kqv_t"])
        for hl in range(4):
            W("g", "t", st["wkq"][hl])
            DMA("g", p2_wk[hl][:].rearrange("dd (c e) -> dd c e", e=128),
                wkT[4 * grp + hl, :, :].rearrange("(c dd) e -> dd c e", dd=128), "lw")
        for hl in range(4):
            DMA("g", p2_wq[hl][:].rearrange("dd (c e) -> dd c e", e=128),
                wqT[4 * grp + hl, :, :].rearrange("(c dd) e -> dd c e", dd=128), "lw")

        W("g", "t", st["wv"])
        for hl in range(4):
            tw_v = DMA("g",
                       p2_wv[:].rearrange("dd (c h e) -> dd c h e", h=4, e=128)[:, :, hl, :],
                       wvT[4 * grp + hl, :, :].rearrange("(c dd) e -> dd c e", dd=128), "lw")

        W("t", "v", st["z_t"][0])
        W("t", "v", st["z_t"][1])
        W("t", "s", st["bc_s"])
        W("t", "s", st["d_s"])
        ecnt = 0
        NSLAB = S // 256
        for sbk in range(NSLAB):
            bi = sbk % 2
            W("y", "t", st["xn"][bi])
            txn = DMA("y", p2_xn[bi][:].rearrange("dd (c s) -> dd c s", s=256),
                xn_sc[:, sbk * 256:(sbk + 1) * 256].rearrange(
                "(c dd) s -> dd c s", dd=128), f"lxn{bi}")
            W("t", f"lxn{bi}", txn)
            if sbk == 0:
                W("t", "lw", tw_v)
            for hl in range(4):
                bnk = 4 + (ecnt % 4)
                W("t", "s", st["pb67"][ecnt % 4])
                for c in range(NCH):
                    ttk = O("t", lambda t_, hl_=hl, c_=c, b=bi, bk=bnk: t_.matmul(
                        out=bank(bk)[:, 0:256],
                        lhsT=p2_wk[hl_][:, c_ * 128:(c_ + 1) * 128],
                        rhs=p2_xn[b][:, c_ * 256:(c_ + 1) * 256],
                        start=(c_ == 0), stop=(c_ == NCH - 1), skip_group_check=True),
                        inc="t" if c == NCH - 1 else None)
                W("s", "t", ttk)
                st["pb67"][ecnt % 4] = O("s", lambda s_, hl_=hl, sb_=sbk, bk=bnk: s_.copy(
                    out=p2_K[hl_][:, sb_ * 256:(sb_ + 1) * 256], in_=bank(bk)[:, 0:256]),
                    inc="s")
                ecnt += 1
                bnk = 4 + (ecnt % 4)
                W("t", "s", st["pb67"][ecnt % 4])
                for c in range(NCH):
                    ttq = O("t", lambda t_, hl_=hl, c_=c, b=bi, bk=bnk: t_.matmul(
                        out=bank(bk)[:, 0:128],
                        lhsT=p2_wq[hl_][:, c_ * 128:(c_ + 1) * 128],
                        rhs=p2_xn[b][:, c_ * 256:c_ * 256 + 128],
                        start=(c_ == 0), stop=(c_ == NCH - 1), skip_group_check=True),
                        inc="t" if c == NCH - 1 else None)
                W("s", "t", ttq)
                st["pb67"][ecnt % 4] = O("s", lambda s_, hl_=hl, sb_=sbk, bk=bnk: s_.copy(
                    out=p2_Q[hl_][:, sb_ * 128:(sb_ + 1) * 128], in_=bank(bk)[:, 0:128]),
                    inc="s")
                ecnt += 1
                if sbk == NSLAB - 1:
                    st["wkq"][hl] = em.cnt["t"]
            for st4 in range(2):
                kc = sbk * 2 + st4
                bnk = 4 + (ecnt % 4)
                W("t", "s", st["pb67"][ecnt % 4])
                for c in range(NCH):
                    ttv = O("t", lambda t_, s4=st4, c_=c, b=bi, bk=bnk: t_.matmul(
                        out=bank(bk),
                        lhsT=p2_xn[b][:, c_ * 256 + s4 * 128: c_ * 256 + (s4 + 1) * 128],
                        rhs=p2_wv[:, c_ * 512:(c_ + 1) * 512],
                        start=(c_ == 0), stop=(c_ == NCH - 1), skip_group_check=True),
                        inc="t" if c == NCH - 1 else None)
                W("s", "t", ttv)
                st["pb67"][ecnt % 4] = O("s", lambda s_, kc_=kc, bk=bnk: s_.copy(
                    out=p2_V[:, kc_ * 512:(kc_ + 1) * 512], in_=bank(bk)), inc="s")
                ecnt += 1
            st["xn"][bi] = em.cnt["t"]
        st["wv"] = em.cnt["t"]
        t_proj_s = em.cnt["s"]
        W("t", "s", t_proj_s)

        for _i in range(4):
            W("t", "s", st["pb67"][_i])
        pending = [None]

        def flush_pending():
            if pending[0] is not None:
                pending[0]()
                pending[0] = None

        for hl in range(4):
            h = 4 * grp + hl
            for j in range(NQB):
                nch_j = 8 * (j + 1)
                npair = nch_j // 2
                ui = h * NQB + j
                zb = ui % 2
                zbank = 4 if zb == 0 else 7
                W("t", "v", st["z_t"][zb])
                W("t", "s", st["d_s"])

                def sc_pair(mp):
                    pr = mp % 2
                    W("t", "s", st["sc_s"][pr])
                    t2 = None
                    for mm, half in ((2 * mp, 0), (2 * mp + 1, 1)):
                        t2 = O("t", lambda t_, hl_=hl, j_=j, m_=mm, pr_=pr, hf=half: t_.matmul(
                            out=bank(2 * pr_ + hf),
                            lhsT=p2_K[hl_][:, m_ * 128:(m_ + 1) * 128],
                            rhs=p2_Q[hl_][:, j_ * 512:(j_ + 1) * 512],
                            start=True, stop=True, skip_group_check=True),
                            inc="t" if half == 1 else None)
                    return t2

                tsc = [None] * npair
                tsc[0] = sc_pair(0)
                flush_pending()
                tt_pv = 0
                for mp in range(npair):
                    pr = mp % 2
                    W("s", "t", tsc[mp])
                    W("s", "t", st["P_t"][pr])
                    ts_e = O("s", lambda s_, pr_=pr: s_.activation(
                        out=p2_P[pr_][:], in_=bankpair(2 * pr_), func=AF.Exp,
                        bias=c_expb[:], scale=SCALE), inc="s")
                    st["P_s"][pr] = ts_e
                    st["sc_s"][pr] = ts_e
                    if mp + 1 < npair:
                        tsc[mp + 1] = sc_pair(mp + 1)
                    tvm = [None, None]
                    for mm, half in ((2 * mp, 0), (2 * mp + 1, 1)):
                        mprime = mm - 8 * j
                        if mprime >= 0:
                            wgrp = min(mprime // 2 + 1, 4)
                            W("v", "s", ts_e)
                            tvm[half] = O("v", lambda v, pr_=pr, hf=half, mp_=mprime, w=wgrp:
                                          v.tensor_tensor(
                                out=p2_P[pr_][:, hf * 512: hf * 512 + w * 128],
                                in0=p2_P[pr_][:, hf * 512: hf * 512 + w * 128],
                                in1=masks[:, mp_ * 512: mp_ * 512 + w * 128],
                                op=ALU.mult), inc="v")
                    for mm, half in ((2 * mp, 0), (2 * mp + 1, 1)):
                        if tvm[half] is not None:
                            W("t", "v", tvm[half])
                        else:
                            W("t", "s", ts_e)
                        O("t", lambda t_, hl_=hl, m_=mm, pr_=pr, hf=half,
                          s0=(mm == 0), s1=(mm == nch_j - 1), zk=zbank: t_.matmul(
                            out=bank(zk),
                            lhsT=p2_V[:, m_ * 512 + hl_ * 128: m_ * 512 + (hl_ + 1) * 128],
                            rhs=p2_P[pr_][:, hf * 512:(hf + 1) * 512],
                            start=s0, stop=s1, skip_group_check=True))
                        tt_pv = O("t", lambda t_, m_=mm, pr_=pr, hf=half,
                                  s0=(mm == 0), s1=(mm == nch_j - 1): t_.matmul(
                            out=bank(5)[0:1, :], lhsT=ones_b[:],
                            rhs=p2_P[pr_][:, hf * 512:(hf + 1) * 512],
                            start=s0, stop=s1, skip_group_check=True), inc="t")
                    st["P_t"][pr] = tt_pv
                    emit_f0b(1)
                W("s", "t", tt_pv)
                ts_dn = O("s", lambda s_, u=ui % 2: s_.copy(out=p2_dn[u][:],
                                                            in_=bank(5)[0:1, :]), inc="s")
                st["d_s"] = ts_dn

                def mk_norm(h=h, j=j, zb=zb, zbank=zbank, ui=ui, ts_dn=ts_dn):
                    def emit_norm():
                        u2 = ui % 2
                        W("v", "s", ts_dn)
                        tv_rd = O("v", lambda v: v.reciprocal(out=p2_rd[u2][:],
                                                              in_=p2_dn[u2][:]), inc="v")
                        W("t", "v", tv_rd)
                        W("t", "s", st["bc_s"])
                        tt_bc = O("t", lambda t_: t_.matmul(out=bank(6), lhsT=onesrow_f[:],
                                                            rhs=p2_rd[u2][:], start=True,
                                                            stop=True,
                                                            skip_group_check=True), inc="t")
                        W("s", "t", tt_bc)
                        W("s", "v", st["rb_v"][u2])
                        ts_rb = O("s", lambda s_: s_.copy(out=p2_rb[u2][:], in_=bank(6)),
                                  inc="s")
                        st["bc_s"] = ts_rb
                        W("v", "s", ts_rb)
                        W("v", f"sz{zb}", st["z_ld"][zb])
                        tv_z = O("v", lambda v: v.tensor_tensor(
                            out=p2_z[zb][:], in0=bank(zbank), in1=p2_rb[u2][:],
                            op=ALU.mult), inc="v")
                        st["rb_v"][u2] = tv_z
                        st["z_t"][zb] = tv_z
                        W("y", "v", tv_z)
                        st["z_ld"][zb] = DMA("y", z_sc[h * 128:(h + 1) * 128,
                                                       j * 512:(j + 1) * 512],
                                             p2_z[zb][:], f"sz{zb}")
                    return emit_norm
                pending[0] = mk_norm()
        flush_pending()
        st["kqv_t"] = em.cnt["t"]
    emit_f0b(384)
    em.barrier()


def emit_f1z(nc, em, L):
    O, W, DMA = em.op, em.wait, em.dma
    bank = L["bank"]
    f1_wo, f1_z, f1_ao = L["f1_wo"], L["f1_z"], L["f1_ao"]
    woT, z_sc, ao_sc, obc_sb = L["woT"], L["z_sc"], L["ao_sc"], L["obc_sb"]

    g_wo = 0
    g_z = 0
    g_ao_s = [0, 0]
    g_ao_ld = [0, 0]
    for dh in range(2):
        W("g", "t", g_wo)
        two = DMA("g", f1_wo[:].rearrange("ee (h d) -> ee h d", h=H),
            woT[:, :, dh * 1024:(dh + 1) * 1024].rearrange(
            "h ee d -> ee h d"), "lwo")
        for sbk in range(4):
            W("y", "t", g_z)
            tz = DMA("y", f1_z[:].rearrange("ee (h s) -> ee h s", h=H),
                z_sc[:, sbk * 512:(sbk + 1) * 512].rearrange(
                "(h ee) s -> ee h s", ee=128), "lz")
            for dt in range(8):
                bnk = dt % 2
                W("t", "s", g_ao_s[bnk])
                if dt == 0:
                    W("t", "lwo", two)
                    W("t", "lz", tz)
                tt = None
                for h in range(H):
                    tt = O("t", lambda t_, h_=h, dt_=dt, b=bnk: t_.matmul(
                        out=bank(b),
                        lhsT=f1_wo[:, h_ * 1024 + dt_ * 128: h_ * 1024 + (dt_ + 1) * 128],
                        rhs=f1_z[:, h_ * 512:(h_ + 1) * 512],
                        start=(h_ == 0), stop=(h_ == H - 1), skip_group_check=True),
                        inc="t" if h == H - 1 else None)
                W("s", "t", tt)
                W("s", f"sao{bnk}", g_ao_ld[bnk])
                dchunk = dh * 8 + dt
                g_ao_s[bnk] = O("s", lambda s_, b=bnk, dc=dchunk: s_.activation(
                    out=f1_ao[b][:], in_=bank(b), func=AF.Identity,
                    bias=obc_sb[:, dc:dc + 1], scale=1.0), inc="s")
                W("y", "s", g_ao_s[bnk])
                g_ao_ld[bnk] = DMA("y", ao_sc[dchunk * 128:(dchunk + 1) * 128,
                                              sbk * 512:(sbk + 1) * 512], f1_ao[bnk][:],
                                   f"sao{bnk}")
            g_z = em.cnt["t"]
        g_wo = em.cnt["t"]
    em.barrier()


def emit_f2(nc, em, L):
    O, W, DMA = em.op, em.wait, em.dma
    bank = L["bank"]
    f2_xq, f2_gw, f2_vw = L["f2_xq"], L["f2_gw"], L["f2_vw"]
    f2_gbb, f2_vbb = L["f2_gbb"], L["f2_vbb"]
    f2_g1, f2_g2, f2_gs, f2_v1, f2_v2 = L["f2_g1"], L["f2_g2"], L["f2_gs"], L["f2_v1"], L["f2_v2"]
    f2_gvb, f2_am = L["f2_gvb"], L["f2_am"]
    f2_gbr, f2_vbr = L["f2_gbr"], L["f2_vbr"]
    onesrow_f = L["onesrow_f"]
    sc_g, sc_v, gx_all, gh_run = L["sc_g"], L["sc_v"], L["gx_all"], L["gh_run"]
    g127_w, c_zero = L["g127_w"], L["c_zero"]
    c_one, c_mone = L["c_one"], L["c_mone"]
    gw_t8, vw_t8, xq_sc, gv_sc = L["gw_t8"], L["vw_t8"], L["xq_sc"], L["gv_sc"]

    gb_in, vb_in = L["gb_in"], L["vb_in"]
    em.seq("v", lambda v: v.tensor_scalar(out=sc_g[:], in0=gx_all[:], scalar1=g127_w["gw"][:],
                                           scalar2=None, op0=ALU.mult))
    t_scv = O("v", lambda v: v.tensor_scalar(out=sc_v[:], in0=gx_all[:], scalar1=g127_w["vw"][:],
                                             scalar2=None, op0=ALU.mult), inc="v")
    W("s", "v", t_scv)

    txq = DMA("g", f2_xq[:].rearrange("dd (c s) -> dd c s", s=OWN),
              xq_sc[:, :].rearrange("(c dd) s -> dd c s", dd=128), "lxq")

    NHT = HID // 512
    g_w = [0, 0]
    g_psum = [0, 0, 0, 0]
    g_b4 = 0
    g_brow_t = [0, 0]
    g_bias_g = [0, 0]
    g_g1 = [0, 0]
    g_g2 = [0, 0]
    g_gs = [0, 0]
    g_gv_ld = [0, 0]
    for ht in range(NHT):
        bi = ht % 2
        W("g", "t", g_w[bi])
        tw1 = DMA("g", f2_gw[bi][:].rearrange("dd (c hh) -> dd c hh", hh=512),
            gw_t8[:, ht * 512:(ht + 1) * 512].rearrange(
            "(c dd) hh -> dd c hh", dd=128), f"ltw{bi}")
        tw2 = DMA("g", f2_vw[bi][:].rearrange("dd (c hh) -> dd c hh", hh=512),
            vw_t8[:, ht * 512:(ht + 1) * 512].rearrange(
            "(c dd) hh -> dd c hh", dd=128), f"ltw{bi}")
        W("v", f"ltw{bi}", tw2)
        O("v", lambda v, b=bi: v.tensor_scalar(out=f2_gw[b][:], in0=f2_gw[b][:],
                                               scalar1=c_one[:], scalar2=c_mone[:],
                                               op0=ALU.min, op1=ALU.max))
        tv_clip = O("v", lambda v, b=bi: v.tensor_scalar(out=f2_vw[b][:], in0=f2_vw[b][:],
                                                         scalar1=c_one[:], scalar2=c_mone[:],
                                                         op0=ALU.min, op1=ALU.max), inc="v")
        W("y", "t", g_brow_t[bi])
        DMA("y", f2_gbr[bi][:], gb_in[0:1, ht * 512:(ht + 1) * 512], f"lbr{bi}")
        t_brow = DMA("y", f2_vbr[bi][:], vb_in[0:1, ht * 512:(ht + 1) * 512], f"lbr{bi}")
        W("t", "s", g_b4)
        W("t", f"lbr{bi}", t_brow)
        tt_gb = O("t", lambda t_, b=bi: t_.matmul(
            out=bank(4), lhsT=onesrow_f[:], rhs=f2_gbr[b][:],
            start=True, stop=True, skip_group_check=True), inc="t")
        W("s", "t", tt_gb)
        W("s", "g", g_bias_g[bi])
        ts_gb = O("s", lambda s_, b=bi: s_.copy(out=f2_gbb[b][:], in_=bank(4)), inc="s")
        W("t", "s", ts_gb)
        tt_vb = O("t", lambda t_, b=bi: t_.matmul(
            out=bank(4), lhsT=onesrow_f[:], rhs=f2_vbr[b][:],
            start=True, stop=True, skip_group_check=True), inc="t")
        g_brow_t[bi] = tt_vb
        W("s", "t", tt_vb)
        g_b4 = O("s", lambda s_, b=bi: s_.copy(out=f2_vbb[b][:], in_=bank(4)), inc="s")
        ts_vb = g_b4

        for stt in range(NOT):
            pb = stt % 2
            for bb, wbuf in ((0, f2_gw[bi]), (2, f2_vw[bi])):
                bk = bb + pb
                W("t", "s", g_psum[bk])
                if stt == 0 and bb == 0:
                    W("t", "v", tv_clip)
                    if ht == 0:
                        W("t", "lxq", txq)
                    W("t", "s", ts_vb)
                tt = None
                for c in range(NCH):
                    tt = O("t", lambda t_, c_=c, st_=stt, wb=wbuf, bk_=bk: t_.matmul(
                        out=bank(bk_),
                        lhsT=f2_xq[:, c_ * OWN + st_ * 128: c_ * OWN + (st_ + 1) * 128],
                        rhs=wb[:, c_ * 512:(c_ + 1) * 512],
                        start=(c_ == 0), stop=(c_ == NCH - 1), skip_group_check=True),
                        inc="t" if c == NCH - 1 else None)
                if bb == 2 and stt == NOT - 1:
                    g_w[bi] = tt
                if bb == 0:
                    tt_g = tt
                else:
                    tt_v = tt
            W("s", "t", tt_g)
            W("s", "g", g_g1[pb])
            ts_g1 = O("s", lambda s_, st_=stt, bk=0 + pb, p=pb: s_.activation(
                out=f2_g1[p][:], in_=bank(bk), func=AF.Identity, bias=c_zero[:],
                scale=sc_g[:, st_:st_ + 1]), inc="s")
            g_psum[0 + pb] = ts_g1
            W("s", "t", tt_v)
            ts_v1 = O("s", lambda s_, st_=stt, bk=2 + pb, p=pb: s_.activation(
                out=f2_v1[p][:], in_=bank(bk), func=AF.Identity, bias=c_zero[:],
                scale=sc_v[:, st_:st_ + 1]), inc="s")
            g_psum[2 + pb] = ts_v1
            W("g", "s", ts_v1)
            W("g", "s", g_g2[pb])
            tg_g2 = O("g", lambda g_, b=bi, p=pb: g_.tensor_tensor(
                out=f2_g2[p][:], in0=f2_g1[p][:], in1=f2_gbb[b][:], op=ALU.add), inc="g")
            W("g", "v", g_gs[pb])
            tg_v2 = O("g", lambda g_, b=bi, p=pb: g_.tensor_tensor(
                out=f2_v2[p][:], in0=f2_v1[p][:], in1=f2_vbb[b][:], op=ALU.add), inc="g")
            g_g1[pb] = tg_v2
            if stt == NOT - 1:
                g_bias_g[bi] = tg_v2
            W("s", "g", tg_g2)
            W("s", "v", g_gs[pb])
            g_g2[pb] = O("s", lambda s_, p=pb: s_.activation(
                out=f2_gs[p][:], in_=f2_g2[p][:], func=AF.Silu, bias=c_zero[:],
                scale=1.0), inc="s")
            ts_sl = g_g2[pb]
            W("v", "s", ts_sl)
            W("v", "g", tg_v2)
            W("v", f"sgv{pb}", g_gv_ld[pb])
            gvb = f2_gvb[pb]
            tv_gv = em.seq("v", lambda v, gv=gvb, p=pb: v.tensor_tensor(
                out=gv[:], in0=f2_gs[p][:], in1=f2_v2[p][:], op=ALU.mult))
            em.seq("v", lambda v, gv=gvb: v.tensor_reduce(
                out=f2_am[:], in_=gv[:], axis=AX.X,
                op=ALU.max, apply_absolute_value=True))
            tv_am = O("v", lambda v, st_=stt: v.tensor_tensor(
                out=gh_run[:, st_:st_ + 1], in0=gh_run[:, st_:st_ + 1],
                in1=f2_am[:], op=ALU.max), inc="v")
            g_gs[pb] = tv_am
            W("y", "v", tv_gv)
            g_gv_ld[pb] = DMA("y", gv_sc[stt * 128:(stt + 1) * 128,
                                         ht * 512:(ht + 1) * 512], gvb[:], f"sgv{pb}")
    em.barrier()


def emit_f3a(nc, em, L):
    O, W, DMA = em.op, em.wait, em.dma
    bank_bf, bank, bankpair = L["bank_bf"], L["bank"], L["bankpair"]
    ps = L["ps"]
    gh_run, sc_h127, c_geps, c_inv127 = L["gh_run"], L["sc_h127"], L["c_geps"], L["c_inv127"]
    ghrow_t, shrow, g127_w, idn_f, idn_b = L["ghrow_t"], L["shrow"], L["g127_w"], L["idn_f"], L["idn_b"]
    f3_gv, f3_i32, f3_hb, f3_stg = L["f3_gv"], L["f3_i32"], L["f3_hb"], L["f3_stg"]
    gv_sc, hq_sc, row_sc = L["gv_sc"], L["hq_sc"], L["row_sc"]
    c_zero, rc16 = L["c_zero"], L["rc16"]

    em.seq("v", lambda v: v.tensor_scalar(out=gh_run[:], in0=gh_run[:], scalar1=c_geps[:],
                                           scalar2=None, op0=ALU.max))
    em.seq("v", lambda v: v.tensor_scalar(out=rc16[:], in0=gh_run[:], scalar1=c_inv127[:],
                                          scalar2=None, op0=ALU.mult))
    em.seq("v", lambda v: v.reciprocal(out=sc_h127[:], in_=rc16[:]))
    tv1 = O("v", lambda v: v.tensor_copy(out=rc16[:], in_=sc_h127[:]), inc="v")
    W("t", "v", tv1)
    tt1 = O("t", lambda t_: t_.transpose(out=ps[0:NOT, 0:128], in_=gh_run[:],
                                         identity=idn_f[:]), inc="t")
    W("v", "t", tt1)
    tv2 = O("v", lambda v: v.tensor_copy(out=ghrow_t[:], in_=ps[0:NOT, 0:128]), inc="v")
    W("y", "v", tv2)
    td1 = DMA("y", row_sc[:, :].rearrange("one (t ss) -> (one t) ss", ss=128), ghrow_t[:], "lrow")
    W("y", "lrow", td1)
    td2 = DMA("y", shrow[:], row_sc[:, :], "lrow")
    W("s", "lrow", td2)
    ts1 = O("s", lambda s_: s_.activation(out=shrow[:], in_=shrow[:], func=AF.Identity,
                                          bias=c_zero[0:1, :],
                                          scale=g127_w["ow"][0:1, :]), inc="s")
    L.update(f3_shrow_s=ts1)

    g_gv = 0
    g_pb = [0, 0]
    g_stg_ld = 0
    for sbk in range(4):
        for st4 in range(4):
            sti = sbk * 4 + st4
            W("y", "v", g_gv)
            tgv = DMA("y", f3_gv[0][:], gv_sc[sti * 128:(sti + 1) * 128, :], "lgv0")
            W("v", "lgv0", tgv)
            if st4 == 0:
                W("v", "shq", g_stg_ld)
            em.seq("v", lambda v, st_=sti: v.tensor_scalar(
                out=f3_i32[:], in0=f3_gv[0][:], scalar1=sc_h127[:, st_:st_ + 1],
                scalar2=None, op0=ALU.mult))
            tv_hb = O("v", lambda v: v.tensor_copy(out=f3_hb[:], in_=f3_i32[:]), inc="v")
            g_gv = tv_hb
            W("t", "v", tv_hb)
            for cb in range(8):
                bnk = cb % 2
                W("t", "v", g_pb[bnk])
                tt = None
                for k in range(8):
                    c = cb * 8 + k
                    tt = O("t", lambda t_, c_=c, k_=k, b=bnk: t_.transpose(
                        out=bankpair(2 * b)[:, k_ * 128:(k_ + 1) * 128],
                        in_=f3_hb[:, c_ * 128:(c_ + 1) * 128], identity=idn_f[:]),
                        inc="t" if k == 7 else None)
                W("v", "t", tt)
                g_pb[bnk] = O("v", lambda v, cb_=cb, s4=st4, b=bnk: v.tensor_copy(
                    out=f3_stg[:].rearrange("dd (c ss) -> dd c ss", ss=512)[
                        :, cb_ * 8:(cb_ + 1) * 8, s4 * 128:(s4 + 1) * 128],
                    in_=bankpair(2 * b)[:]), inc="v")
        W("y", "v", em.cnt["v"])
        last = 0
        for dmac in range(8):
            last = DMA("y", hq_sc[sbk, dmac * 8:(dmac + 1) * 8, :, :].rearrange(
                "c hh ss -> hh c ss"),
                f3_stg[:, dmac * 8 * 512:(dmac + 1) * 8 * 512].rearrange(
                    "dd (c ss) -> dd c ss", ss=512), "shq")
        g_stg_ld = last
    em.barrier()


def emit_f3b(nc, em, L):
    O, W, DMA = em.op, em.wait, em.dma
    bank = L["bank"]
    f3_hq, f3_ow, f3_x, f3_ao, f3_t, f3_o, f3_sbc = (
        L["f3_hq"], L["f3_ow"], L["f3_x"], L["f3_ao"], L["f3_t"], L["f3_o"], L["f3_sbc"])
    c_one, c_mone = L["c_one"], L["c_mone"]
    onesrow_f, shrow = L["onesrow_f"], L["shrow"]
    hq_sc, ow_t8, xT_own, ao_sc, out_fm = (
        L["hq_sc"], L["ow_t8"], L["xT_own"], L["ao_sc"], L["out_fm"])

    g_hq = 0
    g_ow = [0, 0]
    g_x = [0, 0]
    g_ao = [0, 0]
    g_o = [0, 0]
    g_banks_v = 0
    g_sbc_v = 0
    for sbk in range(4):
        W("g", "t", g_hq)
        thq = DMA("g", f3_hq[:].rearrange("hh (c ss) -> hh c ss", ss=512),
              hq_sc[sbk, :, :, :].rearrange("c hh ss -> hh c ss"), "lhq")
        W("t", "v", g_sbc_v)
        tt_sbc = O("t", lambda t_, sb_=sbk: t_.matmul(
            out=bank(0), lhsT=onesrow_f[:], rhs=shrow[0:1, sb_ * 512:(sb_ + 1) * 512],
            start=True, stop=True, skip_group_check=True), inc="t")
        W("s", "t", tt_sbc)
        ts_sbc = O("s", lambda s_: s_.copy(out=f3_sbc[:], in_=bank(0)), inc="s")
        for dh in range(2):
            W("t", "s", ts_sbc)
            W("t", "v", g_banks_v)
            tt_ffn = None
            for grp in range(8):
                bi = grp % 2
                W("g", "t", g_ow[bi])
                tow = DMA("g", f3_ow[bi][:].rearrange("hh (c d) -> hh c d", d=1024),
                    ow_t8[grp * 1024:(grp + 1) * 1024,
                          dh * 1024:(dh + 1) * 1024].rearrange(
                    "(c hh) d -> hh c d", hh=128), f"low{bi}")
                W("v", f"low{bi}", tow)
                tv_owc = O("v", lambda v, b=bi: v.tensor_scalar(
                    out=f3_ow[b][:], in0=f3_ow[b][:], scalar1=c_one[:], scalar2=c_mone[:],
                    op0=ALU.min, op1=ALU.max), inc="v")
                W("t", "v", tv_owc)
                if grp == 0:
                    W("t", "lhq", thq)
                for dt in range(8):
                    for k in range(8):
                        c = grp * 8 + k
                        tt_ffn = O("t", lambda t_, b=bi, k_=k, dt_=dt, c_=c,
                                   s0=(grp == 0 and k == 0), s1=(grp == 7 and k == 7): t_.matmul(
                            out=bank(dt_),
                            lhsT=f3_ow[b][:, k_ * 1024 + dt_ * 128: k_ * 1024 + (dt_ + 1) * 128],
                            rhs=f3_hq[:, c_ * 512:(c_ + 1) * 512],
                            start=s0, stop=s1, skip_group_check=True),
                            inc="t" if (dt == 7 and k == 7) else None)
                g_ow[bi] = tt_ffn
            for dt in range(8):
                dchunk = dh * 8 + dt
                bi2 = dt % 2
                W("y", "v", g_x[bi2])
                tx = DMA("y", f3_x[bi2][:], xT_own[dchunk * 128:(dchunk + 1) * 128,
                                                   sbk * 512:(sbk + 1) * 512], f"lx3{bi2}")
                W("y", "v", g_ao[bi2])
                tao = DMA("y", f3_ao[bi2][:], ao_sc[dchunk * 128:(dchunk + 1) * 128,
                                                    sbk * 512:(sbk + 1) * 512], f"lao{bi2}")
                W("v", "t", tt_ffn)
                W("v", f"lao{bi2}", tao)
                W("v", f"lx3{bi2}", tx)
                W("v", f"so{bi2}", g_o[bi2])
                em.seq("v", lambda v, dt_=dt, b=bi2: v.tensor_tensor(
                    out=f3_t[b][:], in0=bank(dt_), in1=f3_sbc[:], op=ALU.mult))
                em.seq("v", lambda v, b=bi2: v.tensor_tensor(
                    out=f3_t[b][:], in0=f3_t[b][:], in1=f3_ao[b][:], op=ALU.add))
                tv_o = O("v", lambda v, b=bi2: v.tensor_tensor(
                    out=f3_o[b][:], in0=f3_t[b][:], in1=f3_x[b][:], op=ALU.add), inc="v")
                g_x[bi2] = tv_o
                g_ao[bi2] = tv_o
                W("y", "v", tv_o)
                g_o[bi2] = DMA("y", out_fm[dchunk * 128:(dchunk + 1) * 128,
                                           sbk * 512:(sbk + 1) * 512], f3_o[bi2][:],
                               f"so{bi2}")
            g_banks_v = em.cnt["v"]
            g_sbc_v = g_banks_v
        g_hq = em.cnt["t"]


def finish_all(nc, em, es):
    finals = {k: v for k, v in em.cnt.items() if k not in ("t", "v", "s", "g") and v > 0}
    for name in em.cnt:
        if name not in em.sems:
            em.sems[name] = es.enter_context(nc.semaphore(name))
    blk = es.enter_context(nc.Block())

    @blk.tensor
    def _(t):
        for f in em.ops["t"]:
            f(t)

    @blk.vector
    def _(v):
        for f in em.ops["v"]:
            f(v)

    @blk.scalar
    def _(s):
        for f in em.ops["s"]:
            f(s)

    @blk.gpsimd
    def _(g):
        for f in em.ops["g"]:
            f(g)

    @blk.sync
    def _(y):
        for f in em.ops["y"]:
            f(y)
        for name, v in finals.items():
            y.wait_ge(em.sems[name], v)

    es.close()


_CACHE = {}


def _get_nc():
    if "nc" not in _CACHE:
        _CACHE["nc"] = build_program()
    return _CACHE["nc"]


def kernel(L0_attn_x, L0_attn_W_Q, L0_attn_W_K, L0_attn_W_V, L0_attn_W_O,
           L0_ffn_gate_w, L0_ffn_gate_b, L0_ffn_val_w, L0_ffn_val_b,
           L0_ffn_out_w, L0_ffn_out_b):
    x = np.asarray(L0_attn_x, dtype=np.float32)
    wq = np.ascontiguousarray(np.asarray(L0_attn_W_Q, np.float32).transpose(0, 2, 1))
    wk = np.ascontiguousarray(np.asarray(L0_attn_W_K, np.float32).transpose(0, 2, 1))
    wv = np.ascontiguousarray(np.asarray(L0_attn_W_V, np.float32).transpose(0, 2, 1))
    wo = np.ascontiguousarray(np.asarray(L0_attn_W_O, np.float32).transpose(0, 2, 1))
    gw = np.ascontiguousarray(np.asarray(L0_ffn_gate_w, np.float32).T)
    vw = np.ascontiguousarray(np.asarray(L0_ffn_val_w, np.float32).T)
    ow = np.ascontiguousarray(np.asarray(L0_ffn_out_w, np.float32).T)
    gb = np.ascontiguousarray(np.asarray(L0_ffn_gate_b, np.float32).reshape(1, HID))
    vb = np.ascontiguousarray(np.asarray(L0_ffn_val_b, np.float32).reshape(1, HID))
    obc = np.ascontiguousarray(np.asarray(L0_ffn_out_b, np.float32).reshape(NCH, 128).T)
    iden_bf = np.eye(128, dtype=ml_dtypes.bfloat16)
    iden_f = np.eye(128, dtype=np.float32)

    in_maps = []
    for c in range(NCORES):
        b, r = c // 2, c % 2
        perm = np.arange(NT)
        if r == 1:
            perm = perm ^ 1
        xb = np.ascontiguousarray(
            x[b].reshape(NT, 128, D)[perm].reshape(S, D))
        own_tiles = [2 * t + r for t in range(NOT)]
        own_cols = np.concatenate([np.arange(g_ * 128, (g_ + 1) * 128) for g_ in own_tiles])
        xT = np.ascontiguousarray(x[b].T[:, own_cols])
        masks = np.zeros((8, 128, 512), dtype=ml_dtypes.bfloat16)
        kk = np.arange(128)[:, None]
        qq = np.arange(128)[None, :]
        for m in range(8):
            gk = (m + r) if m % 2 == 0 else (m - r)
            for g_ in range(4):
                masks[m][:, g_ * 128:(g_ + 1) * 128] = (
                    ((2 * g_ + r - gk) * 128 + qq) >= kk).astype(ml_dtypes.bfloat16)
        in_maps.append({
            "x_tm": xb, "xT_own": xT, "wqT": wq, "wkT": wk, "wvT": wv, "woT": wo,
            "gwT": gw, "vwT": vw, "owT": ow, "gb_in": gb, "vb_in": vb, "obc_in": obc,
            "iden_bf_in": iden_bf, "iden_f_in": iden_f, "masks_in": masks,
        })

    nc = _get_nc()
    res = run_bass_kernel_spmd(nc, in_maps, core_ids=list(range(NCORES)))

    out = np.empty((B, S, D), dtype=np.float32)
    for c in range(NCORES):
        b, r = c // 2, c % 2
        o = np.asarray(res.results[c]["out_fm"])
        oT = np.ascontiguousarray(o.T)
        for t in range(NOT):
            g_ = 2 * t + r
            out[b, g_ * 128:(g_ + 1) * 128, :] = oT[t * 128:(t + 1) * 128, :]
    return out


if __name__ == "__main__":
    nc = build_program()
    print("build ok")
